# version 01
# baseline (speedup 1.0000x reference)
import numpy as np
import ml_dtypes
import concourse.bass as bass
import concourse.mybir as mybir
from concourse.bass_utils import run_bass_kernel_spmd

F32 = mybir.dt.float32
BF16 = mybir.dt.bfloat16
AF = mybir.ActivationFunctionType
ALU = mybir.AluOpType
AX = mybir.AxisListType

D = 1024
T = 2048
KC = 8
NB = 4
EPS = 1e-5
BF = ml_dtypes.bfloat16

BASE = 16512
OFF_X = BASE
OFF_A = BASE + 65536
OFF_B = BASE + 98304
OFF_C = BASE + 131072
OFF_D = BASE + 163840
OFF_CONST = BASE + 196608 + 512
SCRW = BASE + 196608 + 512 + 3072
SBUF_END = 229344



def _reorder_qkv_ed(W):
    Wt = np.ascontiguousarray(np.asarray(W, dtype=np.float32).T).reshape(1024, 16, 3, 64)
    q = Wt[:, :, 0, :].reshape(1024, 1024)
    k = Wt[:, :, 1, :].reshape(1024, 1024)
    v = Wt[:, :, 2, :].reshape(1024, 1024)
    return np.ascontiguousarray(np.concatenate([q, k, v], axis=1))


def _prep(inp):
    f = lambda a: np.ascontiguousarray(np.asarray(a, dtype=np.float32))
    row = lambda a: np.ascontiguousarray(np.asarray(a, dtype=np.float32).reshape(1, -1))
    w = {}
    w["l0a_wq"] = f(inp["L0_attn_W_Q"])
    w["l0a_wk"] = f(inp["L0_attn_W_K"])
    w["l0a_wv"] = f(inp["L0_attn_W_V"])
    w["l0a_wfc"] = f(inp["L0_attn_W_fc"])
    w["l0a_g_row"] = row(inp["L0_attn_gamma"])
    w["l0a_b_row"] = row(inp["L0_attn_beta"])
    w["l0f_wr"] = f(np.asarray(inp["L0_ffn_W_r"]).T)
    w["l0f_wk"] = f(np.asarray(inp["L0_ffn_W_k"]).T)
    w["l0f_wv"] = f(np.asarray(inp["L0_ffn_W_v"]).T)
    w["l0f_mur_row"] = row(inp["L0_ffn_mu_r"])
    w["l0f_muk_row"] = row(inp["L0_ffn_mu_k"])
    w["l1a_g_row"] = row(inp["L1_attn_ln_gamma"])
    w["l1a_b_row"] = row(inp["L1_attn_ln_beta"])
    w["l1a_wqkv"] = _reorder_qkv_ed(inp["L1_attn_W_qkv"])
    w["l1a_wo"] = f(np.asarray(inp["L1_attn_W_o"]).T)
    w["l1f_w"] = f(inp["L1_ffn_weight"])
    w["l1f_bias_row"] = row(inp["L1_ffn_bias"])
    w["l2a_wqkv"] = _reorder_qkv_ed(inp["L2_attn_w_qkv"])
    w["l2a_wout"] = f(np.asarray(inp["L2_attn_w_out"]).T)
    w["l2a_g_row"] = row(inp["L2_attn_ln_w"])
    w["l2a_b_row"] = row(inp["L2_attn_ln_b"])
    w["l2f_g1_row"] = row(inp["L2_ffn_ln1_w"])
    w["l2f_b1_row"] = row(inp["L2_ffn_ln1_b"])
    w["l2f_wqkv"] = f(inp["L2_ffn_w_qkv"])
    w["l2f_wproj"] = f(inp["L2_ffn_w_proj"])
    w["l2f_g2_row"] = row(inp["L2_ffn_ln2_w"])
    w["l2f_b2_row"] = row(inp["L2_ffn_ln2_b"])
    w["l2f_gatew"] = f(inp["L2_ffn_gate_w"])
    w["l2f_gateb_row"] = row(inp["L2_ffn_gate_b"])
    for e in range(8):
        w[f"l2f_w1_{e}"] = f(np.asarray(inp["L2_ffn_w1"])[e])
        w[f"l2f_w2_{e}"] = f(np.asarray(inp["L2_ffn_w2"])[e])
    w["l3a_phase"] = f(inp["L3_attn_phase"])
    w["l3a_wq"] = f(inp["L3_attn_Wq"])
    w["l3a_wk"] = f(inp["L3_attn_Wk"])
    w["l3a_wv"] = f(inp["L3_attn_Wv"])
    w["l3a_bq_row"] = row(inp["L3_attn_bq"])
    w["l3a_bk_row"] = row(inp["L3_attn_bk"])
    w["l3a_bv_row"] = row(inp["L3_attn_bv"])
    w["l3a_wo"] = f(inp["L3_attn_Wo"])
    w["l3a_bo_row"] = row(inp["L3_attn_bo"])
    w["l3f_wr"] = f(np.asarray(inp["L3_ffn_W_r"]).T)
    w["l3f_wk"] = f(np.asarray(inp["L3_ffn_W_k"]).T)
    w["l3f_wv"] = f(np.asarray(inp["L3_ffn_W_v"]).T)
    w["l3f_mur_row"] = row(inp["L3_ffn_mu_r"])
    w["l3f_muk_row"] = row(inp["L3_ffn_mu_k"])
    BIAS_VALS = [1e-5, 1e-8, float(np.pi / 2), 1.0, 0.5, -30.0, -20.0]
    w["c_biases"] = np.tile(np.asarray(BIAS_VALS, dtype=np.float32)[None, :], (128, 1))
    w["c_ones_row"] = np.ones((1, 128), dtype=BF)
    w["c_ones_col"] = np.ones((128, 1), dtype=BF)
    w["c_ident"] = np.eye(128, dtype=BF)
    w["c_identf"] = np.eye(128, dtype=np.float32)
    w["c_tri"] = (np.arange(128)[None, :] >= np.arange(128)[:, None]).astype(BF)
    w["c_iota8"] = np.tile(np.arange(8, dtype=np.float32)[None, :], (128, 1))
    bo16 = np.zeros((128, 128), dtype=BF)
    for c_ in range(8):
        for h_ in range(16):
            if h_ // 2 == c_:
                rows = slice((h_ % 2) * 64, (h_ % 2) * 64 + 64)
                bo16[rows, c_ * 16 + h_] = 1
    w["c_bo16all"] = bo16
    e2_ = np.zeros((2, 128), dtype=BF)
    e2_[0, :64] = 1
    e2_[1, 64:] = 1
    w["c_e2"] = e2_
    return w



class Ctr:
    def __init__(self, h):
        self.h = h
        self.n = 0

    def inc(self, inst, k=1):
        inst.then_inc(self.h, k)
        self.n += k
        return self.n


class Bld:
    def __init__(self, arrays):
        nc = bass.Bass("TRN2", target_bir_lowering=False)
        self.nc = nc
        self.arrays = arrays
        self.hbm = {}
        self.used = []
        self.q = {
            "h": Ctr(nc.alloc_semaphore("sem_h")),
            "s": Ctr(nc.alloc_semaphore("sem_s")),
            "g": Ctr(nc.alloc_semaphore("sem_g")),
        }
        self.ws = [Ctr(nc.alloc_semaphore("sem_w0")),
                   Ctr(nc.alloc_semaphore("sem_w1"))]
        self.st = [Ctr(nc.alloc_semaphore("sem_t0")),
                   Ctr(nc.alloc_semaphore("sem_t1"))]
        self.hp = [Ctr(nc.alloc_semaphore("sem_p0")),
                   Ctr(nc.alloc_semaphore("sem_p1"))]
        self.c_mm = Ctr(nc.alloc_semaphore("sem_mm"))
        self.c_ev = Ctr(nc.alloc_semaphore("sem_ev"))
        self.c_dv = Ctr(nc.alloc_semaphore("sem_dv"))
        self.c_pl = Ctr(nc.alloc_semaphore("sem_pl"))
        self.PA = nc.alloc_psum_tensor("PSA", [128, 2048], F32)
        self.PB = nc.alloc_psum_tensor("PSB", [128, 2048], F32)
        self._sbn = 0

    def inhbm(self, name):
        if name not in self.hbm:
            a = self.arrays[name]
            dt = BF16 if a.dtype == BF else F32
            self.hbm[name] = self.nc.dram_tensor(name, list(a.shape), dt,
                                                 kind="ExternalInput").ap()
            self.used.append(name)
        return self.hbm[name]

    def sb(self, shape, dtype, off, name=None):
        self._sbn += 1
        nm = name or f"sb{self._sbn}"
        esz = 2 if dtype == BF16 else 4
        nbytes = int(np.prod(shape[1:])) * esz
        assert off + nbytes <= SBUF_END, f"{nm}: {off}+{nbytes} over SBUF end"
        return self.nc.alloc_sbuf_tensor_at(nm, list(shape), dtype, offset=off)

    def ld(self, eng, qn, dst, src, **kw):
        eng.dma_start(dst, src, **kw).then_inc(self.q[qn].h, 16)
        self.q[qn].n += 16
        return self.q[qn].n

    def wq(self, eng, qn, tgt=None):
        eng.wait_ge(self.q[qn].h, self.q[qn].n if tgt is None else tgt)



def emit_proj_fm(b, w_hbm, n_out, rhs_tiles, evac, wbuf, evac_eng="scalar",
                 col_off=0):
    nc = b.nc
    n_units = n_out // 256
    mm0, ev0 = b.c_mm.n, b.c_ev.n
    wt = []

    with nc.Block() as blk:
        @blk.gpsimd
        def _(g):
            for u in range(n_units):
                if u >= 2:
                    g.wait_ge(b.c_mm.h, mm0 + 2 * (u - 1))
                src = w_hbm[:, col_off + u * 256: col_off + (u + 1) * 256]
                src = src.rearrange("(kc kp) m -> kp kc m", kp=128)
                dst = wbuf[u % 2][:].rearrange("p (kc m) -> p kc m", m=256)
                ws = b.ws[u % 2]
                g.dma_start(dst, src).then_inc(ws.h, 16)
                ws.n += 16
                wt.append(ws.n)

        @blk.tensor
        def _(t):
            for u in range(n_units):
                t.wait_ge(b.ws[u % 2].h, wt[u])
                for mo in range(2):
                    j = u * 2 + mo
                    tgt = b.PA if j % 2 == 0 else b.PB
                    if j >= 2:
                        t.wait_ge(b.c_ev.h, ev0 + j - 1)
                    last = None
                    for tb in range(NB):
                        for kc in range(KC):
                            last = t.matmul(
                                tgt[:, tb * 512:(tb + 1) * 512],
                                wbuf[u % 2][:, kc * 256 + mo * 128: kc * 256 + (mo + 1) * 128],
                                rhs_tiles[kc][:, tb * 512:(tb + 1) * 512],
                                start=(kc == 0), stop=(kc == KC - 1))
                    b.c_mm.inc(last)

        def evs(e):
            for j in range(2 * n_units):
                tgt = b.PA if j % 2 == 0 else b.PB
                e.wait_ge(b.c_mm.h, mm0 + j + 1)
                last = evac(e, j, tgt)
                b.c_ev.inc(last)

        if evac_eng == "scalar":
            @blk.scalar
            def _(e):
                evs(e)
        else:
            @blk.vector
            def _(e):
                evs(e)


def emit_proj_tm(b, w_hbm, n_out, rhs_tiles, evac, wbuf, evac_eng="scalar",
                 col_off=0):
    nc = b.nc
    n_units = n_out // 256
    mm0, ev0 = b.c_mm.n, b.c_ev.n
    wt = []

    def pslice(i):
        off = (i % 4) * 512
        return b.PA[:, off:off + 256]

    with nc.Block() as blk:
        @blk.gpsimd
        def _(g):
            for u in range(n_units):
                if u >= 2:
                    g.wait_ge(b.c_mm.h, mm0 + 16 * (u - 1))
                src = w_hbm[:, col_off + u * 256: col_off + (u + 1) * 256]
                src = src.rearrange("(kc kp) m -> kp kc m", kp=128)
                dst = wbuf[u % 2][:].rearrange("p (kc m) -> p kc m", m=256)
                ws = b.ws[u % 2]
                g.dma_start(dst, src).then_inc(ws.h, 16)
                ws.n += 16
                wt.append(ws.n)

        @blk.tensor
        def _(t):
            i = 0
            for u in range(n_units):
                t.wait_ge(b.ws[u % 2].h, wt[u])
                for tt in range(16):
                    if i >= 4:
                        t.wait_ge(b.c_ev.h, ev0 + i - 3)
                    last = None
                    for kc in range(KC):
                        last = t.matmul(
                            pslice(i),
                            rhs_tiles[kc][:, tt * 128:(tt + 1) * 128],
                            wbuf[u % 2][:, kc * 256:(kc + 1) * 256],
                            start=(kc == 0), stop=(kc == KC - 1))
                    b.c_mm.inc(last)
                    i += 1

        def evs(e):
            i = 0
            for u in range(n_units):
                for tt in range(16):
                    e.wait_ge(b.c_mm.h, mm0 + i + 1)
                    last = evac(e, tt, u, pslice(i))
                    b.c_ev.inc(last)
                    i += 1

        if evac_eng == "scalar":
            @blk.scalar
            def _(e):
                evs(e)
        else:
            @blk.vector
            def _(e):
                evs(e)



def emit_stats(b, tiles, lhsT_fn, m, dstA=False):
    nc = b.nc
    tgt = b.PA if dstA else b.PB
    with nc.Block() as blk:
        @blk.tensor
        def _(t):
            for nb in range(NB):
                for kc in range(KC):
                    t.matmul(tgt[0:m, nb * 512:(nb + 1) * 512],
                             lhsT_fn(kc), tiles[kc][:, nb * 512:(nb + 1) * 512],
                             start=(kc == 0), stop=(kc == KC - 1))


def emit_rowmath_plain(b, s0_ap, s1_ap, scratch_row, a_row, b_row, eps=EPS):
    nc = b.nc
    with nc.Block() as blk:
        @blk.scalar
        def _(e):
            e.activation(scratch_row[:], s0_ap, AF.Square, scale=1.0 / D)
    with nc.Block() as blk:
        @blk.vector
        def _(v):
            v.scalar_tensor_tensor(scratch_row[:], s1_ap, 1.0 / D, scratch_row[:],
                                   ALU.mult, ALU.subtract)
    with nc.Block() as blk:
        @blk.scalar
        def _(e):
            e.activation(scratch_row[:], scratch_row[:], AF.Sqrt, bias=eps)
    with nc.Block() as blk:
        @blk.vector
        def _(v):
            with b.nc.allow_low_precision("bf16 LN broadcast rows"):
                v.reciprocal(a_row[:], scratch_row[:])
            v.drain()
            v.scalar_tensor_tensor(b_row[:], s0_ap, -1.0 / D, a_row[:],
                                   ALU.mult, ALU.mult)


def emit_bcast2(b, a_row, b_row, env):
    nc = b.nc
    orow = env["ones_row"]
    with nc.Block() as blk:
        @blk.tensor
        def _(t):
            for nb in range(NB):
                t.matmul(b.PA[:, nb * 512:(nb + 1) * 512], orow[:],
                         a_row[:, nb * 512:(nb + 1) * 512], start=True, stop=True)
                t.matmul(b.PB[:, nb * 512:(nb + 1) * 512], orow[:],
                         b_row[:, nb * 512:(nb + 1) * 512], start=True, stop=True)


def emit_affine_apply(b, src_tiles, dst_pool, tmp_a, tmp_b, gb=None, add_into_x=None):
    nc = b.nc
    dv0, pl0 = b.c_dv.n, b.c_pl.n
    pool_mode = (add_into_x is not None) or (gb is not None)
    with nc.Block() as blk:
        @blk.vector
        def _(v):
            for c in range(KC):
                tmp = tmp_a if c % 2 == 0 else tmp_b
                if pool_mode and c >= 2:
                    v.wait_ge(b.c_pl.h, pl0 + (c - 1))
                v.tensor_tensor(tmp[:], src_tiles[c][:], b.PA[:], ALU.mult)
                v.drain()
                if pool_mode:
                    i = v.tensor_tensor(tmp[:], tmp[:], b.PB[:], ALU.add)
                else:
                    i = v.tensor_tensor(dst_pool[c][:], tmp[:], b.PB[:], ALU.add)
                b.c_dv.inc(i)
        if add_into_x is not None:
            @blk.gpsimd
            def _(g):
                for c in range(KC):
                    tmp = tmp_a if c % 2 == 0 else tmp_b
                    g.wait_ge(b.c_dv.h, dv0 + c + 1)
                    i = g.tensor_tensor(add_into_x[c][:], add_into_x[c][:],
                                        tmp[:], ALU.add)
                    b.c_pl.inc(i)
        elif gb is not None:
            gcols, bcols = gb
            @blk.gpsimd
            def _(g):
                for c in range(KC):
                    tmp = tmp_a if c % 2 == 0 else tmp_b
                    g.wait_ge(b.c_dv.h, dv0 + c + 1)
                    i = g.tensor_scalar(dst_pool[c][:], tmp[:],
                                        gcols[c], bcols[c], ALU.mult, ALU.add)
                    b.c_pl.inc(i)


def scratch_rows(b, tag, n, base=SCRW, dtype=BF16):
    ts = []
    sz = 2048 * (2 if dtype == BF16 else 4)
    for i in range(n):
        ts.append(b.sb([1, 2048], dtype, base + i * sz, f"row_{tag}_{i}_{b._sbn}"))
    return ts


def emit_ln_stats_rows(b, env, tiles, sq_pool, sq_off, tag):
    nc = b.nc
    ones_col = env["ones_col"]
    a_row, b_row = scratch_rows(b, tag, 2)
    with nc.Block() as blk:
        @blk.scalar
        def _(e):
            for c in range(KC):
                e.activation(sq_pool[c][:], tiles[c][:], AF.Square)
    emit_stats(b, tiles, lambda kc: ones_col[:], 1)
    emit_stats(b, sq_pool, lambda kc: ones_col[:], 1, dstA=True)
    sc_row = b.sb([1, 2048], F32, sq_off, f"scr_{tag}_{b._sbn}")
    emit_rowmath_plain(b, b.PB[0:1, 0:2048], b.PA[0:1, 0:2048], sc_row,
                       a_row, b_row)
    emit_bcast2(b, a_row, b_row, env)


def emit_boundary(b, env, z_pool, sq_pool, sq_off, y_specs, tag):
    nc = b.nc
    X = env["X"]
    tmp_a = b.sb([128, 2048], F32, sq_off + 8192, f"tba_{tag}")
    tmp_b = b.sb([128, 2048], F32, sq_off + 16384, f"tbb_{tag}")

    if z_pool is not None:
        emit_ln_stats_rows(b, env, z_pool, sq_pool, sq_off, tag + "z")
        emit_affine_apply(b, z_pool, None, tmp_a, tmp_b, add_into_x=X)

    if y_specs:
        has_center = any(sp[0] == "center" for sp in y_specs)
        bf_specs = [sp for sp in y_specs if sp[0] != "center"]
        with nc.Block() as blk:
            @blk.scalar
            def _(e):
                for c in range(KC):
                    e.activation(sq_pool[c][:], X[c][:], AF.Square)
        if bf_specs:
            xcast = bf_specs[0][1]
        elif z_pool is not None and z_pool is not sq_pool:
            xcast = z_pool
        else:
            xcast = env["pC"] if sq_pool is not env["pC"] else env["pD"]
        assert xcast is not sq_pool
        with nc.Block() as blk:
            @blk.vector
            def _(v):
                for c in range(KC):
                    v.tensor_copy(xcast[c][:], X[c][:])
        ones_col = env["ones_col"]
        a_row, b_row = scratch_rows(b, tag + "x", 2)
        emit_stats(b, xcast, lambda kc: ones_col[:], 1)
        emit_stats(b, sq_pool, lambda kc: ones_col[:], 1, dstA=True)
        sc_row = b.sb([1, 2048], F32, sq_off, f"scx_{tag}_{b._sbn}")
        emit_rowmath_plain(b, b.PB[0:1, 0:2048], b.PA[0:1, 0:2048], sc_row,
                           a_row, b_row)
        if has_center:
            with nc.Block() as blk:
                @blk.vector
                def _(v):
                    with nc.allow_low_precision("negmu row"):
                        v.tensor_scalar(b_row[:], b.PB[0:1, 0:2048], -1.0 / D,
                                        None, ALU.mult)
                @blk.sync
                def _(sy):
                    b.ld(sy, "h", env["l1f_inv_hbm"], a_row[:])
                    b.wq(sy, "h")
            with nc.Block() as blk:
                @blk.tensor
                def _(t):
                    for nb in range(NB):
                        t.matmul(b.PA[:, nb * 512:(nb + 1) * 512],
                                 env["ones_row"][:],
                                 b_row[:, nb * 512:(nb + 1) * 512],
                                 start=True, stop=True)
            with nc.Block() as blk:
                @blk.vector
                def _(v):
                    for c in range(KC):
                        v.tensor_tensor(env["yc"][c][:], X[c][:], b.PA[:], ALU.add)
        if bf_specs:
            emit_bcast2(b, a_row, b_row, env)
            for spec in bf_specs:
                if spec[0] == "plain":
                    emit_affine_apply(b, X, spec[1], tmp_a, tmp_b)
                else:
                    emit_affine_apply(b, X, spec[1], tmp_a, tmp_b,
                                      gb=(spec[2], spec[3]))



def tm_tile(pool, tt):
    return pool[tt // 2][:, (tt % 2) * 1024:(tt % 2) * 1024 + 1024]


def emit_l0a(b, env):
    nc = b.nc
    X, A, Bp, C, Dp = env["X"], env["pA"], env["pB"], env["pC"], env["pD"]
    ones_col, ones_row = env["ones_col"], env["ones_row"]
    wbuf = env["wbuf"]
    wq, wk, wv, wfc = (b.inhbm(n) for n in ("l0a_wq", "l0a_wk", "l0a_wv", "l0a_wfc"))

    emit_proj_fm(b, wq, 1024, A,
                 lambda e, j, ps: e.activation(Bp[j][:], ps[:], AF.Sigmoid), wbuf)
    emit_proj_tm(b, wk, 1024, A,
                 lambda e, tt, u, ps: e.activation(
                     tm_tile(C, tt)[:, u * 256:(u + 1) * 256], ps[:], AF.Sigmoid), wbuf)
    emit_proj_tm(b, wv, 1024, A,
                 lambda e, tt, u, ps: e.activation(
                     tm_tile(Dp, tt)[:, u * 256:(u + 1) * 256], ps[:], AF.Copy), wbuf)

    kve = b.sb([128, 8 * 65], BF16, SCRW + 8192, "l0a_kve")
    for pair in range(8):
        with nc.Block() as blk:
            @blk.tensor
            def _(t, pair=pair):
                for tt in range(16):
                    t.matmul(b.PA[0:128, 0:128],
                             tm_tile(C, tt)[:, pair * 128:(pair + 1) * 128],
                             tm_tile(Dp, tt)[:, pair * 128:(pair + 1) * 128],
                             start=(tt == 0), stop=(tt == 15))
                for tt in range(16):
                    t.matmul(b.PB[0:128, 0:1],
                             tm_tile(C, tt)[:, pair * 128:(pair + 1) * 128],
                             ones_col[:],
                             start=(tt == 0), stop=(tt == 15))
        with nc.Block() as blk:
            @blk.vector
            def _(v, pair=pair):
                v.tensor_copy(kve[0:64, pair * 65:pair * 65 + 64], b.PA[0:64, 0:64])
                v.drain()
                v.tensor_copy(kve[64:128, pair * 65:pair * 65 + 64],
                              b.PA[64:128, 64:128])
                v.drain()
                v.tensor_copy(kve[:, pair * 65 + 64:pair * 65 + 65], b.PB[0:128, 0:1])

    den_row = b.sb([1, 2048], F32, OFF_D, "l0a_den")
    rden_row = b.sb([1, 2048], BF16, OFF_D + 12288, "l0a_rden")
    rdenb = b.sb([64, 2048], BF16, OFF_D + 8192, "l0a_rdenb")
    for h in range(16):
        pair, half = h // 2, (h % 2) * 64
        with nc.Block() as blk:
            @blk.tensor
            def _(t, pair=pair, half=half):
                for tb in range(NB):
                    t.matmul(b.PA[0:65, tb * 512:(tb + 1) * 512],
                             kve[half:half + 64, pair * 65:(pair + 1) * 65],
                             Bp[pair][half:half + 64, tb * 512:(tb + 1) * 512],
                             start=True, stop=True)
        with nc.Block() as blk:
            @blk.scalar
            def _(e):
                e.activation(den_row[:], b.PA[64:65, 0:2048], AF.Copy, bias=1e-8)
        with nc.Block() as blk:
            @blk.vector
            def _(v):
                with nc.allow_low_precision("bf16 rden row"):
                    v.reciprocal(rden_row[:], den_row[:])
        with nc.Block() as blk:
            @blk.tensor
            def _(t):
                for nb in range(NB):
                    t.matmul(b.PB[0:64, nb * 512:(nb + 1) * 512], ones_row[:, 0:64],
                             rden_row[:, nb * 512:(nb + 1) * 512],
                             start=True, stop=True)
        with nc.Block() as blk:
            @blk.scalar
            def _(e):
                e.activation(rdenb[:], b.PB[0:64, 0:2048], AF.Copy)
        with nc.Block() as blk:
            @blk.vector
            def _(v, pair=pair, half=half):
                v.tensor_tensor(C[pair][half:half + 64, :], b.PA[0:64, 0:2048],
                                rdenb[:], ALU.mult)

    emit_proj_fm(b, wfc, 1024, C,
                 lambda e, j, ps: e.tensor_tensor(Dp[j][:], ps[:], A[j][:], ALU.add),
                 wbuf, evac_eng="vector")

    emit_ln_stats_rows(b, env, Dp, Bp, OFF_B, "l0a_in")
    tmp_a = b.sb([128, 2048], F32, OFF_B + 8192, "l0a_tma")
    tmp_b = b.sb([128, 2048], F32, OFF_B + 16384, "l0a_tmb")
    emit_affine_apply(b, Dp, A, tmp_a, tmp_b,
                      gb=(env["gb"]["l0a_g"], env["gb"]["l0a_b"]))
    return A, Bp, OFF_B





def emit_ffn(b, env, pfx):
    nc = b.nc
    A, Bp, C, Dp = env["pA"], env["pB"], env["pC"], env["pD"]
    wbuf = env["wbuf"]
    wr = b.inhbm(f"{pfx}_wr")
    wk = b.inhbm(f"{pfx}_wk")
    wv = b.inhbm(f"{pfx}_wv")
    mur = env["gb"][f"{pfx}_mur"]
    muk = env["gb"][f"{pfx}_muk"]

    with nc.Block() as blk:
        @blk.vector
        def _(v):
            for c in range(KC):
                v.memset(Bp[c][:, 0:1], 0.0)
                v.drain()
                v.tensor_copy(Bp[c][:, 1:2048], A[c][:, 0:2047])
    with nc.Block() as blk:
        @blk.vector
        def _(v):
            for c in range(KC):
                v.tensor_tensor(C[c][:], A[c][:], Bp[c][:], ALU.subtract)
                v.drain()
                v.scalar_tensor_tensor(Dp[c][:], C[c][:], muk[c], Bp[c][:],
                                       ALU.mult, ALU.add)
                v.drain()
                v.scalar_tensor_tensor(C[c][:], C[c][:], mur[c], Bp[c][:],
                                       ALU.mult, ALU.add)
    emit_proj_fm(b, wr, 1024, C,
                 lambda e, j, ps: e.activation(Bp[j][:], ps[:], AF.Sigmoid), wbuf)

    def ev_relu_sq(e, j, ps):
        e.activation(C[j][:], ps[:], AF.Relu)
        e.drain()
        return e.activation(C[j][:], C[j][:], AF.Square)
    emit_proj_fm(b, wk, 1024, Dp, ev_relu_sq, wbuf)

    emit_proj_fm(b, wv, 1024, C,
                 lambda e, j, ps: e.tensor_tensor(Dp[j][:], ps[:], Bp[j][:], ALU.mult),
                 wbuf, evac_eng="vector")
    return Dp, Bp, OFF_B


def emit_l0f(b, env):
    return emit_ffn(b, env, "l0f")


def emit_l3f(b, env):
    return emit_ffn(b, env, "l3f")




def _emit_fp6_quant_chain(b, blk_ops, W, G, src_view, rscl_bc, scl_bc,
                          t0, t1, t2, t3, dst):
    v3 = lambda ap: ap.rearrange("p (g k) -> p g k", k=32)

    def ops(v):
        with b.nc.allow_low_precision("fp6 quant chain"):
            v.tensor_tensor(v3(t0), v3(src_view), rscl_bc, ALU.mult)
            v.drain()
            v.tensor_tensor(t1, t0, t0, ALU.mult)
            v.drain()
            v.tensor_single_scalar(t2, t1, 16.0, ALU.is_ge)
            v.drain()
            v.tensor_single_scalar(t1, t1, 4.0, ALU.is_ge)
            v.drain()
            v.tensor_scalar(t3, t2, -2.0, 8.0, ALU.mult, ALU.add)
            v.drain()
            v.scalar_tensor_tensor(t3, t1, -4.0, t3, ALU.mult, ALU.add)
            v.drain()
            v.tensor_scalar(t2, t2, 0.25, 0.125, ALU.mult, ALU.add)
            v.drain()
            v.scalar_tensor_tensor(t2, t1, 0.125, t2, ALU.mult, ALU.add)
            v.drain()
            v.tensor_tensor(t0, t0, t3, ALU.mult)
            v.drain()
            v.tensor_scalar(t0, t0, -1.0, 0.5, ALU.mult, ALU.add)
            v.drain()
            v.tensor_single_scalar(t3, t0, 12582912.0, ALU.add)
            v.drain()
            v.tensor_single_scalar(t3, t3, 12582912.0, ALU.subtract)
            v.drain()
            v.tensor_tensor(t1, t3, t0, ALU.is_gt)
            v.drain()
            v.tensor_tensor(t3, t3, t1, ALU.subtract)
            v.drain()
            v.scalar_tensor_tensor(t0, t3, -1.0, t2, ALU.mult, ALU.mult)
            v.drain()
            v.tensor_tensor(v3(t0), v3(t0), scl_bc, ALU.mult)
            if dst is not None:
                v.drain()
                v.tensor_copy(dst, t0)
    blk_ops.append(ops)


def emit_proj_fm_sb(b, lhsT_tiles, n_out, rhs_tiles, evac, evac_eng="scalar"):
    nc = b.nc
    mm0, ev0 = b.c_mm.n, b.c_ev.n
    with nc.Block() as blk:
        @blk.tensor
        def _(t):
            for j in range(n_out // 128):
                tgt = b.PA if j % 2 == 0 else b.PB
                if j >= 2:
                    t.wait_ge(b.c_ev.h, ev0 + j - 1)
                last = None
                for tb in range(NB):
                    for kc in range(KC):
                        last = t.matmul(
                            tgt[:, tb * 512:(tb + 1) * 512],
                            lhsT_tiles(kc, j),
                            rhs_tiles[kc][:, tb * 512:(tb + 1) * 512],
                            start=(kc == 0), stop=(kc == KC - 1))
                b.c_mm.inc(last)

        def evs(e):
            for j in range(n_out // 128):
                tgt = b.PA if j % 2 == 0 else b.PB
                e.wait_ge(b.c_mm.h, mm0 + j + 1)
                last = evac(e, j, tgt)
                b.c_ev.inc(last)
        if evac_eng == "scalar":
            @blk.scalar
            def _(e):
                evs(e)
        else:
            @blk.vector
            def _(e):
                evs(e)


def emit_l1f(b, env):
    nc = b.nc
    A, Bp, C, Dp = env["pA"], env["pB"], env["pC"], env["pD"]
    ident = env["ident"]
    yc = env["yc"]
    w_hbm = b.inhbm("l1f_w")
    bias = env["gb"]["l1f_bias"]

    tmpt = b.sb([128, 2048], F32, SCRW, "l1f_tmpt")
    inv_row = b.sb([1, 2048], BF16, SCRW, "l1f_invloc")
    amx = b.sb([128, 64], F32, SCRW + 8192, "l1f_amx")
    scl = b.sb([128, 64], F32, SCRW + 8448, "l1f_scl")
    rscl = b.sb([128, 64], F32, SCRW + 8704, "l1f_rscl")
    t0 = b.sb([128, 2048], F32, OFF_D, "l1f_t0")
    t1 = b.sb([128, 2048], F32, OFF_D + 8192, "l1f_t1")
    t2 = b.sb([128, 2048], F32, OFF_D + 16384, "l1f_t2")
    t3 = b.sb([128, 2048], F32, OFF_D + 24576, "l1f_t3")

    for c in range(KC):
        with nc.Block() as blk:
            @blk.vector
            def _(v, c=c):
                v.transpose(tmpt[:], yc[c][:])
        with nc.Block() as blk:
            @blk.vector
            def _(v):
                with nc.allow_low_precision("fp6 absmax"):
                    v.tensor_reduce(amx[:], tmpt[:].rearrange("p (g k) -> p g k", k=32),
                                    AX.X, ALU.max, apply_absolute_value=True)
                v.drain()
                v.tensor_scalar(scl[:], amx[:], 1.0 / 7.5, 1e-30, ALU.mult, ALU.max)
                v.drain()
                v.reciprocal(rscl[:], scl[:])
        blk_ops = []
        sbc = scl[:].unsqueeze(-1).broadcast_to((128, 64, 32))
        rbc = rscl[:].unsqueeze(-1).broadcast_to((128, 64, 32))
        _emit_fp6_quant_chain(b, blk_ops, 2048, 64, tmpt[:], rbc, sbc,
                              t0[:], t1[:], t2[:], t3[:], None)
        with nc.Block() as blk:
            @blk.vector
            def _(v, op=blk_ops[0]):
                op(v)
        with nc.Block() as blk:
            @blk.vector
            def _(v, c=c):
                v.transpose(tmpt[:], t0[:])
                v.drain()
                v.tensor_copy(C[c][:], tmpt[:])

    wf = [b.sb([128, 1024], F32, OFF_A + c * 4096, f"l1f_wf{c}") for c in range(KC)]
    with nc.Block() as blk:
        @blk.sync
        def _(sy):
            for c in range(KC):
                b.ld(sy, "h", wf[c][:], w_hbm[c * 128:(c + 1) * 128, :])
            b.wq(sy, "h")
    for c in range(KC):
        with nc.Block() as blk:
            @blk.vector
            def _(v, c=c):
                with nc.allow_low_precision("fp6 absmax"):
                    v.tensor_reduce(amx[:, 0:32],
                                    wf[c][:].rearrange("p (g k) -> p g k", k=32),
                                    AX.X, ALU.max, apply_absolute_value=True)
                v.drain()
                v.tensor_scalar(scl[:, 0:32], amx[:, 0:32], 1.0 / 7.5, 1e-30,
                                ALU.mult, ALU.max)
                v.drain()
                v.reciprocal(rscl[:, 0:32], scl[:, 0:32])
        blk_ops = []
        sbc = scl[:, 0:32].unsqueeze(-1).broadcast_to((128, 32, 32))
        rbc = rscl[:, 0:32].unsqueeze(-1).broadcast_to((128, 32, 32))
        _emit_fp6_quant_chain(b, blk_ops, 1024, 32, wf[c][:], rbc, sbc,
                              t0[:, 0:1024], t1[:, 0:1024], t2[:, 0:1024],
                              t3[:, 0:1024], None)
        with nc.Block() as blk:
            @blk.vector
            def _(v, op=blk_ops[0]):
                op(v)
        with nc.Block() as blk:
            @blk.vector
            def _(v, c=c):
                v.tensor_copy(Bp[c][:, 0:1024], t0[:, 0:1024])

    pabf = b.PA[:].bitcast(BF16)
    mm0, ev0 = b.c_mm.n, b.c_ev.n
    with nc.Block() as blk:
        @blk.tensor
        def _(t):
            i = 0
            for oc in range(KC):
                for dc in range(KC):
                    if i >= 4:
                        t.wait_ge(b.c_ev.h, ev0 + i - 3)
                    last = t.transpose(pabf[:, (i % 4) * 1024:(i % 4) * 1024 + 128],
                                       Bp[oc][:, dc * 128:(dc + 1) * 128], ident[:])
                    b.c_mm.inc(last)
                    i += 1
        @blk.vector
        def _(v):
            i = 0
            for oc in range(KC):
                for dc in range(KC):
                    v.wait_ge(b.c_mm.h, mm0 + i + 1)
                    last = v.tensor_copy(Bp[dc][:, 1024 + oc * 128:1024 + (oc + 1) * 128],
                                         pabf[:, (i % 4) * 1024:(i % 4) * 1024 + 128])
                    b.c_ev.inc(last)
                    i += 1

    invb = b.sb([128, 2048], BF16, OFF_D, "l1f_invb")
    with nc.Block() as blk:
        @blk.sync
        def _(sy):
            b.ld(sy, "h", inv_row[:], env["l1f_inv_hbm"])
            b.wq(sy, "h")
    with nc.Block() as blk:
        @blk.tensor
        def _(t):
            for nb in range(NB):
                t.matmul(b.PB[:, nb * 512:(nb + 1) * 512], env["ones_row"][:],
                         inv_row[:, nb * 512:(nb + 1) * 512], start=True, stop=True)
    with nc.Block() as blk:
        @blk.scalar
        def _(e):
            e.activation(invb[:], b.PB[:], AF.Copy)

    def ev(e, j, ps):
        e.tensor_tensor(A[j][:], ps[:], invb[:], ALU.mult)
        e.drain()
        return e.tensor_scalar(A[j][:], A[j][:], bias[j], None, ALU.add)
    emit_proj_fm_sb(b, lambda kc, j: Bp[kc][:, 1024 + j * 128:1024 + (j + 1) * 128],
                    1024, C, ev, evac_eng="vector")
    return A, C, OFF_C





def _attn_schedule(causal):
    evs = []
    pairs_of_bq = []
    gp = 0
    for bq in range(NB):
        ntk = 4 * bq + 4 if causal else 16
        np_ = ntk // 2
        sec = []
        for p in range(np_):
            g = gp + p
            sec.append(("sp", g, bq, p))
            if p >= 1:
                sec.append(("av", g - 1, bq, p - 1))
        sec.append(("av", gp + np_ - 1, bq, np_ - 1))
        evs.append(sec)
        pairs_of_bq.append((gp, np_))
        gp += np_
    return evs, pairs_of_bq, gp


def emit_attention(b, env, y_pool, q_pool, k_pool, ctx_pool, wqkv_hbm, scale,
                   causal, tag):
    nc = b.nc
    wbuf = env["wbuf"]
    vt = b.sb([128, 16640], BF16, OFF_D, f"vt_{tag}")
    tri = env["tri"]
    orow = env["ones_row"]

    emit_proj_fm(b, wqkv_hbm, 1024, y_pool,
                 lambda e, j, ps: e.activation(q_pool[j][:], ps[:], AF.Copy,
                                               scale=scale), wbuf, col_off=0)
    emit_proj_fm(b, wqkv_hbm, 1024, y_pool,
                 lambda e, j, ps: e.activation(k_pool[j][:], ps[:], AF.Copy),
                 wbuf, col_off=1024)
    with nc.Block() as blk:
        @blk.vector
        def _(v):
            ones_view = vt[:].rearrange("p (t h j) -> p t h j", h=16, j=65)
            v.memset(ones_view[:, :, :, 64:65], 1.0)

    def ev_v(e, tt, u, ps):
        dst = vt[:, tt * 1040 + 4 * u * 65: tt * 1040 + 4 * u * 65 + 260]
        dst = dst.rearrange("p (h j) -> p h j", j=65)[:, :, 0:64]
        return e.activation(dst, ps[:].rearrange("p (h j) -> p h j", j=64), AF.Copy)
    emit_proj_tm(b, wqkv_hbm, 1024, y_pool, ev_v, wbuf, col_off=2048)

    a2 = [b.sb([128, 1024], BF16, SCRW, f"a2a_{tag}"),
          b.sb([128, 1024], BF16, SCRW + 2048, f"a2b_{tag}")]
    den_sb = b.sb([1, 2048], BF16, SCRW + 8192, f"den_{tag}")
    rden = b.sb([1, 2048], BF16, SCRW, f"rden_{tag}")
    rb = b.sb([64, 2048], BF16, SCRW + 4096, f"rb_{tag}")

    evs, pairs_of_bq, npairs = _attn_schedule(causal)
    diag_pairs = []
    if causal:
        for bq in range(NB):
            gp, np_ = pairs_of_bq[bq]
            diag_pairs += [gp + np_ - 2, gp + np_ - 1]

    for h in range(16):
        pair, half = h // 2, (h % 2) * 64
        mm0, ev0, dv0 = b.c_mm.n, b.c_ev.n, b.c_dv.n
        sp_ms, av_ms = {}, {}
        mmn = mm0
        for sec in evs:
            for e_ in sec:
                mmn += 1
                if e_[0] == "sp":
                    sp_ms[e_[1]] = mmn
                else:
                    av_ms[e_[1]] = mmn
        ex_ms = {g: ev0 + g + 1 for g in range(npairs)}
        mk_ms = {g: dv0 + i + 1 for i, g in enumerate(diag_pairs)}

        with nc.Block() as blk:
            @blk.tensor
            def _(t, h=h, pair=pair, half=half):
                for sec in evs:
                    for e_ in sec:
                        kind, g, bq, p = e_
                        if kind == "sp":
                            if g >= 2:
                                t.wait_ge(b.c_ev.h, ex_ms[g - 2])
                            i = None
                            for tki in (2 * p, 2 * p + 1):
                                i = t.matmul(
                                    b.PB[:, (tki % 4) * 512:(tki % 4) * 512 + 512],
                                    k_pool[pair][half:half + 64,
                                                 tki * 128:(tki + 1) * 128],
                                    q_pool[pair][half:half + 64,
                                                 bq * 512:(bq + 1) * 512],
                                    start=True, stop=True)
                            b.c_mm.inc(i)
                            assert b.c_mm.n == sp_ms[g]
                        else:
                            t.wait_ge(b.c_ev.h, ex_ms[g])
                            if g in mk_ms:
                                t.wait_ge(b.c_dv.h, mk_ms[g])
                            gp0, np_ = pairs_of_bq[bq]
                            last_of_bq = (p == np_ - 1)
                            i = None
                            for n_, tki in enumerate((2 * p, 2 * p + 1)):
                                lo = 128 * (tki - 4 * bq) if (causal and tki >= 4 * bq) else 0
                                at = a2[g % 2][:, (tki % 2) * 512 + lo:
                                               (tki % 2) * 512 + 512]
                                i = t.matmul(
                                    b.PA[0:65, bq * 512 + lo: bq * 512 + 512],
                                    vt[:, tki * 1040 + h * 65:
                                       tki * 1040 + h * 65 + 65],
                                    at,
                                    start=(tki == 0),
                                    stop=(last_of_bq and n_ == 1))
                            b.c_mm.inc(i)
                            assert b.c_mm.n == av_ms[g]

            @blk.scalar
            def _(e, h=h):
                for g in range(npairs):
                    e.wait_ge(b.c_mm.h, sp_ms[g])
                    if g >= 2:
                        e.wait_ge(b.c_mm.h, av_ms[g - 2])
                    slot = (2 * g) % 4
                    i = e.activation(a2[g % 2][:],
                                     b.PB[:, slot * 512: slot * 512 + 1024],
                                     AF.Exp, bias=-20.0)
                    b.c_ev.inc(i)
                    assert b.c_ev.n == ex_ms[g]

            if causal:
                @blk.vector
                def _(v, h=h):
                    for g in diag_pairs:
                        v.wait_ge(b.c_ev.h, ex_ms[g])
                        bq = next(bq_ for bq_ in range(NB)
                                  if pairs_of_bq[bq_][0] <= g < pairs_of_bq[bq_][0] + pairs_of_bq[bq_][1])
                        i = None
                        for tki in (2 * (g - pairs_of_bq[bq][0]) + 4 * 0,):
                            pass
                        p = g - pairs_of_bq[bq][0]
                        for tki in (2 * p, 2 * p + 1):
                            off = 128 * (tki - 4 * bq)
                            if 0 <= off < 512:
                                ap_ = a2[g % 2][:, (tki % 2) * 512 + off:
                                                (tki % 2) * 512 + off + 128]
                                i = v.tensor_tensor(ap_, ap_, tri[:], ALU.mult)
                                v.drain()
                        b.c_dv.inc(i)
                        assert b.c_dv.n == mk_ms[g]

        with nc.Block() as blk:
            @blk.scalar
            def _(e):
                e.activation(den_sb[:], b.PA[64:65, 0:2048], AF.Copy).then_inc(
                    b.c_ev.h, 1)
            b.c_ev.n += 1
            @blk.vector
            def _(v):
                v.wait_ge(b.c_ev.h, b.c_ev.n)
                with nc.allow_low_precision("attn rden"):
                    i = v.reciprocal(rden[:], den_sb[:])
                b.c_dv.inc(i)
            @blk.tensor
            def _(t):
                t.wait_ge(b.c_dv.h, b.c_dv.n)
                i = None
                for nb in range(NB):
                    i = t.matmul(b.PB[0:64, nb * 512:(nb + 1) * 512],
                                 orow[:, 0:64], rden[:, nb * 512:(nb + 1) * 512],
                                 start=True, stop=True)
                b.c_mm.inc(i)
            @blk.scalar
            def _(e):
                e.wait_ge(b.c_mm.h, b.c_mm.n)
                i = e.activation(rb[:], b.PB[0:64, 0:2048], AF.Copy)
                b.c_ev.inc(i)
            @blk.vector
            def _(v, pair=pair, half=half):
                v.wait_ge(b.c_ev.h, b.c_ev.n)
                v.tensor_tensor(ctx_pool[pair][half:half + 64, :],
                                b.PA[0:64, 0:2048], rb[:], ALU.mult)
    return vt


def emit_l1a(b, env):
    nc = b.nc
    A, Bp, C, Dp = env["pA"], env["pB"], env["pC"], env["pD"]
    wqkv = b.inhbm("l1a_wqkv")
    wo = b.inhbm("l1a_wo")
    emit_attention(b, env, A, Bp, C, A, wqkv, 0.125, True, "l1a")
    emit_proj_fm(b, wo, 1024, A,
                 lambda e, j, ps: e.activation(Bp[j][:], ps[:], AF.Copy),
                 env["wbuf"])
    return Bp, C, OFF_C




def emit_proj_fm_store(b, w_hbm, n_out, rhs_tiles, act, stage, dst_hbm, wbuf,
                       col_off=0, bias_cols=None):
    nc = b.nc
    n_units = n_out // 256
    mm0, ev0 = b.c_mm.n, b.c_ev.n
    wt = []
    st_entry = [b.st[0].n, b.st[1].n]
    stt = [st_entry[j % 2] + 16 * (j // 2 + 1) for j in range(2 * n_units)]
    with nc.Block() as blk:
        @blk.gpsimd
        def _(g):
            for u in range(n_units):
                if u >= 2:
                    g.wait_ge(b.c_mm.h, mm0 + 2 * (u - 1))
                src = w_hbm[:, col_off + u * 256: col_off + (u + 1) * 256]
                src = src.rearrange("(kc kp) m -> kp kc m", kp=128)
                dst = wbuf[u % 2][:].rearrange("p (kc m) -> p kc m", m=256)
                ws = b.ws[u % 2]
                g.dma_start(dst, src).then_inc(ws.h, 16)
                ws.n += 16
                wt.append(ws.n)
        @blk.tensor
        def _(t):
            for u in range(n_units):
                t.wait_ge(b.ws[u % 2].h, wt[u])
                for mo in range(2):
                    j = u * 2 + mo
                    tgt = b.PA if j % 2 == 0 else b.PB
                    if j >= 2:
                        t.wait_ge(b.c_ev.h, ev0 + j - 1)
                    last = None
                    for tb in range(NB):
                        for kc in range(KC):
                            last = t.matmul(
                                tgt[:, tb * 512:(tb + 1) * 512],
                                wbuf[u % 2][:, kc * 256 + mo * 128: kc * 256 + (mo + 1) * 128],
                                rhs_tiles[kc][:, tb * 512:(tb + 1) * 512],
                                start=(kc == 0), stop=(kc == KC - 1))
                    b.c_mm.inc(last)
        @blk.scalar
        def _(e):
            for j in range(2 * n_units):
                tgt = b.PA if j % 2 == 0 else b.PB
                e.wait_ge(b.c_mm.h, mm0 + j + 1)
                e.wait_ge(b.st[j % 2].h,
                          stt[j - 2] if j >= 2 else st_entry[j % 2])
                if bias_cols is not None:
                    last = e.activation(stage[j % 2][:], tgt[:], act,
                                        bias=bias_cols[j])
                else:
                    last = e.activation(stage[j % 2][:], tgt[:], act)
                b.c_ev.inc(last)
        @blk.sync
        def _(sy):
            for j in range(2 * n_units):
                sy.wait_ge(b.c_ev.h, ev0 + j + 1)
                st = b.st[j % 2]
                sy.dma_start(dst_hbm[j * 128:(j + 1) * 128, :],
                             stage[j % 2][:]).then_inc(st.h, 16)
                st.n += 16
                assert st.n == stt[j]
            sy.wait_ge(b.st[0].h, b.st[0].n)
            sy.wait_ge(b.st[1].h, b.st[1].n)


def emit_l2f(b, env):
    nc = b.nc
    A, Bp, C, Dp = env["pA"], env["pB"], env["pC"], env["pD"]
    wbuf = env["wbuf"]
    X = env["X"]
    ones_col, ones_row, ident = env["ones_col"], env["ones_row"], env["ident"]
    wqkv = b.inhbm("l2f_wqkv")
    wproj = b.inhbm("l2f_wproj")

    emit_attention(b, env, A, Bp, C, A, wqkv, 0.125, False, "l2f")
    emit_proj_fm(b, wproj, 1024, A,
                 lambda e, j, ps: e.activation(Bp[j][:], ps[:], AF.Copy), wbuf)

    with nc.Block() as blk:
        @blk.scalar
        def _(e):
            for c in range(KC):
                e.activation(C[c][:], X[c][:], AF.Square)
    with nc.Block() as blk:
        @blk.vector
        def _(v):
            for c in range(KC):
                v.tensor_copy(A[c][:], X[c][:])
    a_row, b_row = scratch_rows(b, "l2fx", 2)
    emit_stats(b, A, lambda kc: ones_col[:], 1)
    emit_stats(b, C, lambda kc: ones_col[:], 1, dstA=True)
    sc_row = b.sb([1, 2048], F32, OFF_C, "l2f_scrow")
    emit_rowmath_plain(b, b.PB[0:1, 0:2048], b.PA[0:1, 0:2048], sc_row, a_row, b_row)
    emit_bcast2(b, a_row, b_row, env)
    tmp_a = b.sb([128, 2048], F32, OFF_C, "l2f_tma")
    tmp_b = b.sb([128, 2048], F32, OFF_C + 8192, "l2f_tmb")
    with nc.Block() as blk:
        @blk.vector
        def _(v):
            for c in range(KC):
                tmp = tmp_a if c % 2 == 0 else tmp_b
                v.tensor_tensor(tmp[:], X[c][:], b.PA[:], ALU.mult)
                v.drain()
                v.tensor_tensor(tmp[:], tmp[:], b.PB[:], ALU.add)
                v.drain()
                v.tensor_tensor(Bp[c][:], Bp[c][:], tmp[:], ALU.add)
                v.drain()

    emit_ln_stats_rows(b, env, Bp, C, OFF_C, "l2f_ln2")
    tmp_a2 = b.sb([128, 2048], F32, OFF_A, "l2f_tma2")
    tmp_b2 = b.sb([128, 2048], F32, OFF_A + 8192, "l2f_tmb2")
    emit_affine_apply(b, Bp, C, tmp_a2, tmp_b2,
                      gb=(env["gb"]["l2f_g2"], env["gb"]["l2f_b2"]))

    gw = b.sb([128, 64], BF16, OFF_D, "l2f_gw")
    gb_col = b.sb([8, 1], F32, OFF_D + 128, "l2f_gbcol")
    lg = b.sb([8, 2048], F32, OFF_D + 1024, "l2f_lg")
    ltm = b.sb([128, 128], F32, OFF_D + 9216, "l2f_ltm")
    mxt = b.sb([128, 16], F32, OFF_D + 9728, "l2f_mx")
    eqt = b.sb([128, 128], F32, OFF_D + 9856, "l2f_eq")
    idxc = b.sb([128, 16], F32, OFF_D + 10368, "l2f_idxc")
    idxrow = b.sb([1, 2048], BF16, SCRW + 8192, "l2f_idxrow")
    mbs = b.sb([128, 2048], mybir.dt.uint8, OFF_A + 4096, "l2f_mbs")
    iota8 = env["iota8"]
    with nc.Block() as blk:
        @blk.gpsimd
        def _(g):
            b.ld(g, "g", gw[:].rearrange("p (c e) -> p c e", e=8),
                 b.inhbm("l2f_gatew").rearrange("(c p) e -> p c e", p=128),
                 allow_slow_non_contiguous=True)
            b.ld(g, "g", gb_col[:], b.inhbm("l2f_gateb_row").rearrange(
                "one e -> e one"), allow_slow_non_contiguous=True)
            b.wq(g, "g")
    with nc.Block() as blk:
        @blk.tensor
        def _(t):
            for nb in range(NB):
                for kc in range(KC):
                    t.matmul(b.PB[0:8, nb * 512:(nb + 1) * 512],
                             gw[:, kc * 8:(kc + 1) * 8],
                             C[kc][:, nb * 512:(nb + 1) * 512],
                             start=(kc == 0), stop=(kc == KC - 1))
    with nc.Block() as blk:
        @blk.scalar
        def _(e):
            e.activation(lg[:], b.PB[0:8, 0:2048], AF.Identity, bias=gb_col[:])
    mm0, ev0 = b.c_mm.n, b.c_ev.n
    with nc.Block() as blk:
        @blk.tensor
        def _(t):
            for tt in range(16):
                if tt >= 4:
                    t.wait_ge(b.c_ev.h, ev0 + tt - 3)
                i = t.transpose(b.PA[:, (tt % 4) * 512:(tt % 4) * 512 + 8],
                                lg[:, tt * 128:(tt + 1) * 128],
                                env["identf"][0:8, 0:8])
                b.c_mm.inc(i)
        @blk.vector
        def _(v):
            for tt in range(16):
                v.wait_ge(b.c_mm.h, mm0 + tt + 1)
                i = v.tensor_copy(ltm[:, tt * 8:(tt + 1) * 8],
                                  b.PA[:, (tt % 4) * 512:(tt % 4) * 512 + 8])
                b.c_ev.inc(i)
    with nc.Block() as blk:
        @blk.vector
        def _(v):
            v.tensor_reduce(mxt[:], ltm[:].rearrange("p (t e) -> p t e", e=8),
                            AX.X, ALU.max)
            v.drain()
            v.tensor_tensor(eqt[:].rearrange("p (t e) -> p t e", e=8),
                            ltm[:].rearrange("p (t e) -> p t e", e=8),
                            mxt[:].unsqueeze(-1).broadcast_to((128, 16, 8)),
                            ALU.is_ge)
            v.drain()
            v.scalar_tensor_tensor(eqt[:].rearrange("p (t e) -> p t e", e=8),
                                   eqt[:].rearrange("p (t e) -> p t e", e=8),
                                   -99.0,
                                   iota8[:].unsqueeze(1).broadcast_to((128, 16, 8)),
                                   ALU.mult, ALU.add)
            v.drain()
            v.tensor_single_scalar(eqt[:], eqt[:], 99.0, ALU.add)
            v.drain()
            v.tensor_reduce(idxc[:], eqt[:].rearrange("p (t e) -> p t e", e=8),
                            AX.X, ALU.min)
    idx_hbm = nc.dram_tensor("l2f_idxs", [128, 16], F32).ap()
    with nc.Block() as blk:
        @blk.sync
        def _(sy):
            b.ld(sy, "h", idx_hbm, idxc[:])
            b.wq(sy, "h")
    with nc.Block() as blk:
        @blk.gpsimd
        def _(g):
            b.ld(g, "g", idxrow[:].rearrange("one (tt p) -> one tt p", p=128),
                 idx_hbm.rearrange("p tt -> tt p").unsqueeze(0),
                 allow_slow_non_contiguous=True)
            b.wq(g, "g")

    stage = [b.sb([128, 2048], BF16, OFF_A, "l2f_st0"),
             b.sb([128, 2048], BF16, OFF_A + 12288, "l2f_st1")]
    h_hbm = []
    for e in range(8):
        h_hbm.append(nc.dram_tensor(f"l2f_h{e}", [4096, 2048], BF16).ap())
    for e in range(8):
        emit_proj_fm_store(b, b.inhbm(f"l2f_w1_{e}"), 4096, C,
                           AF.Gelu_apprx_tanh, stage, h_hbm[e], wbuf)

    w2s = b.sb([128, 16384], BF16, OFF_D, "l2f_w2s")
    htile = [b.sb([128, 1024], BF16, OFF_A, "l2f_h0t"),
             b.sb([128, 1024], BF16, OFF_A + 2048, "l2f_h1t")]
    for e in range(8):
        with nc.Block() as blk:
            @blk.tensor
            def _(t):
                for nb in range(NB):
                    t.matmul(b.PB[:, nb * 512:(nb + 1) * 512], ones_row[:],
                             idxrow[:, nb * 512:(nb + 1) * 512],
                             start=True, stop=True)
        with nc.Block() as blk:
            @blk.vector
            def _(v, e=e):
                v.tensor_single_scalar(mbs[:], b.PB[:], float(e), ALU.is_equal)
        for mh in range(2):
            with nc.Block() as blk:
                @blk.gpsimd
                def _(g, e=e, mh=mh):
                    src = b.inhbm(f"l2f_w2_{e}")[:, mh * 512:(mh + 1) * 512]
                    src = src.rearrange("(kc kp) m -> kp kc m", kp=128)
                    b.ld(g, "g", w2s[:].rearrange("p (kc m) -> p kc m", m=512), src)
                    b.wq(g, "g")
            for tb in range(2):
                mm0b, ev0b = b.c_mm.n, b.c_ev.n
                wtl = []
                with nc.Block() as blk:
                    @blk.sync
                    def _(sy, e=e, tb=tb):
                        for kc in range(32):
                            if kc >= 2:
                                sy.wait_ge(b.c_mm.h, mm0b + kc - 1)
                            hp = b.hp[kc % 2]
                            sy.dma_start(
                                htile[kc % 2][:],
                                h_hbm[e][kc * 128:(kc + 1) * 128,
                                         tb * 1024:(tb + 1) * 1024]
                            ).then_inc(hp.h, 16)
                            hp.n += 16
                            wtl.append(hp.n)
                    @blk.tensor
                    def _(t, mh=mh):
                        for kc in range(32):
                            t.wait_ge(b.hp[kc % 2].h, wtl[kc])
                            last = None
                            for m in range(4):
                                tgt = b.PA if m < 2 else b.PB
                                off = (m % 2) * 1024
                                for nn in range(2):
                                    last = t.matmul(
                                        tgt[:, off + nn * 512: off + (nn + 1) * 512],
                                        w2s[:, kc * 512 + m * 128: kc * 512 + (m + 1) * 128],
                                        htile[kc % 2][:, nn * 512:(nn + 1) * 512],
                                        start=(kc == 0), stop=(kc == 31))
                            b.c_mm.inc(last)
                with nc.Block() as blk:
                    @blk.vector
                    def _(v, e=e, mh=mh, tb=tb):
                        for m in range(4):
                            tgt = b.PA if m < 2 else b.PB
                            off = (m % 2) * 1024
                            oc = mh * 4 + m
                            v.copy_predicated(
                                C[oc][:, tb * 1024:(tb + 1) * 1024],
                                mbs[:, tb * 1024:(tb + 1) * 1024],
                                tgt[:, off:off + 1024])
                            v.drain()

    with nc.Block() as blk:
        @blk.vector
        def _(v):
            for c in range(KC):
                v.tensor_tensor(C[c][:], C[c][:], Bp[c][:], ALU.add)
    return C, Bp, OFF_B






def emit_l3a(b, env):
    nc = b.nc
    A, Bp, C, Dp = env["pA"], env["pB"], env["pC"], env["pD"]
    wbuf = env["wbuf"]
    orow = env["ones_row"]

    q_hbm = nc.dram_tensor("l3a_qs", [4096, 2048], BF16).ap()
    k_hbm = nc.dram_tensor("l3a_ks", [4096, 2048], BF16).ap()
    v_hbm = nc.dram_tensor("l3a_vs", [4096, 2048], BF16).ap()
    o_hbm = nc.dram_tensor("l3a_os", [4096, 2048], BF16).ap()
    wo_bf = nc.dram_tensor("l3a_wobf", [4096, 1024], BF16).ap()

    ph = b.sb([128, 32], F32, SCRW + 8192, "l3a_ph")
    dif = b.sb([128, 128], F32, SCRW + 8320, "l3a_dif")
    wcol = b.sb([128, 128], BF16, SCRW + 8832, "l3a_wcol")
    with nc.Block() as blk:
        @blk.sync
        def _(sy):
            b.ld(sy, "h", ph[:].rearrange("p (h c) -> p h c", c=8),
                 b.inhbm("l3a_phase").rearrange("h (c p) -> p h c", p=128),
                 allow_slow_non_contiguous=True)
            b.wq(sy, "h")
    with nc.Block() as blk:
        @blk.vector
        def _(v):
            for hh in range(4):
                for gg in range(4):
                    p = hh * 4 + gg
                    v.tensor_tensor(dif[:, p * 8:(p + 1) * 8],
                                    ph[:, hh * 8:(hh + 1) * 8],
                                    ph[:, gg * 8:(gg + 1) * 8], ALU.subtract)
    with nc.Block() as blk:
        @blk.scalar
        def _(e):
            e.activation(dif[:], dif[:], AF.Sin, bias=float(np.pi / 2))
    with nc.Block() as blk:
        @blk.vector
        def _(v):
            v.tensor_scalar_mul(wcol[:], dif[:], 1.0 / 32.0)

    stage = [b.sb([128, 2048], BF16, OFF_B, "l3a_st0"),
             b.sb([128, 2048], BF16, OFF_B + 4096, "l3a_st1")]
    for w_name, bias_name, dst in (("l3a_wq", "l3a_bq", q_hbm),
                                   ("l3a_wk", "l3a_bk", k_hbm),
                                   ("l3a_wv", "l3a_bv", v_hbm)):
        emit_proj_fm_store(b, b.inhbm(w_name), 4096, A, AF.Identity, stage,
                           dst, wbuf, bias_cols=env["gb"][bias_name])
    with nc.Block() as blk:
        @blk.gpsimd
        def _(g):
            b.ld(g, "g", wo_bf, b.inhbm("l3a_wo")[:])
            b.wq(g, "g")

    u8 = [b.sb([128, 2048], BF16, OFF_D, "l3a_u8a"),
          b.sb([128, 2048], BF16, OFF_D + 4096, "l3a_u8b")]
    expr = [b.sb([1, 256], BF16, OFF_D + 8192 + i * 512, f"l3a_ex{i}")
            for i in range(16)]
    denr = [b.sb([1, 256], F32, OFF_D + 16384 + i * 1024, f"l3a_den{i}")
            for i in range(4)]
    rdenr = [b.sb([1, 256], BF16, OFF_D + 20480 + i * 512, f"l3a_rd{i}")
             for i in range(4)]
    abs_t = [b.sb([128, 256], BF16, OFF_D + 22528, "l3a_ab0"),
             b.sb([128, 256], BF16, OFF_D + 23040, "l3a_ab1")]
    tmpc = b.sb([128, 256], BF16, OFF_D + 23552, "l3a_tmpc")

    def sub(base_pool, base, fc, tw=256):
        per = 2048 // tw
        return base_pool[base + fc // per][:, (fc % per) * tw:(fc % per + 1) * tw]

    for tc in range(8):
        t0c = tc * 256
        with nc.Block() as blk:
            @blk.sync
            def _(sy, t0c=t0c):
                for i in range(4):
                    b.ld(sy, "h",
                         Bp[i][:].rearrange("p (s t) -> p s t", t=256),
                         q_hbm[i * 1024:(i + 1) * 1024, t0c:t0c + 256]
                         .rearrange("(s p) t -> p s t", p=128))
                    b.ld(sy, "h",
                         Bp[4 + i][:].rearrange("p (s t) -> p s t", t=256),
                         k_hbm[i * 1024:(i + 1) * 1024, t0c:t0c + 256]
                         .rearrange("(s p) t -> p s t", p=128))
                    b.ld(sy, "h",
                         C[i][:].rearrange("p (s t) -> p s t", t=256),
                         v_hbm[i * 1024:(i + 1) * 1024, t0c:t0c + 256]
                         .rearrange("(s p) t -> p s t", p=128))
                b.wq(sy, "h")
        mm0, ev0, dv0 = b.c_mm.n, b.c_ev.n, b.c_dv.n
        with nc.Block() as blk:
            @blk.vector
            def _(v):
                for p in range(16):
                    if p >= 2:
                        v.wait_ge(b.c_mm.h, mm0 + p - 1)
                    hh, gg = p // 4, p % 4
                    last = None
                    for c in range(KC):
                        last = v.tensor_tensor(
                            u8[p % 2][:, c * 256:(c + 1) * 256],
                            sub(Bp, 0, hh * 8 + c),
                            sub(Bp, 4, gg * 8 + c), ALU.mult)
                    b.c_dv.inc(last)
            @blk.tensor
            def _(t):
                for p in range(16):
                    t.wait_ge(b.c_dv.h, dv0 + p + 1)
                    if p >= 2:
                        t.wait_ge(b.c_ev.h, ev0 + p - 1)
                    last = None
                    for c in range(KC):
                        last = t.matmul(
                            b.PB[0:1, (p % 2) * 512:(p % 2) * 512 + 256],
                            wcol[:, p * 8 + c: p * 8 + c + 1],
                            u8[p % 2][:, c * 256:(c + 1) * 256],
                            start=(c == 0), stop=(c == KC - 1))
                    b.c_mm.inc(last)
            @blk.scalar
            def _(e):
                for p in range(16):
                    e.wait_ge(b.c_mm.h, mm0 + p + 1)
                    last = e.activation(expr[p][:],
                                        b.PB[0:1, (p % 2) * 512:(p % 2) * 512 + 256],
                                        AF.Exp, bias=-20.0)
                    b.c_ev.inc(last)
        with nc.Block() as blk:
            @blk.vector
            def _(v):
                for hh in range(4):
                    v.tensor_tensor(denr[hh][:], expr[4 * hh][:],
                                    expr[4 * hh + 1][:], ALU.add)
                    v.drain()
                    v.tensor_tensor(denr[hh][:], denr[hh][:],
                                    expr[4 * hh + 2][:], ALU.add)
                    v.drain()
                    v.tensor_tensor(denr[hh][:], denr[hh][:],
                                    expr[4 * hh + 3][:], ALU.add)
                    v.drain()
                    with nc.allow_low_precision("l3a rden"):
                        v.reciprocal(rdenr[hh][:], denr[hh][:])
                    v.drain()
                    for gg in range(4):
                        v.tensor_tensor(expr[4 * hh + gg][:],
                                        expr[4 * hh + gg][:],
                                        rdenr[hh][:], ALU.mult)
                        v.drain()
        mm0c, ev0c, dv0c = b.c_mm.n, b.c_ev.n, b.c_dv.n
        with nc.Block() as blk:
            @blk.tensor
            def _(t):
                for p in range(16):
                    if p >= 2:
                        t.wait_ge(b.c_ev.h, ev0c + p - 1)
                    i = t.matmul(b.PA[0:128, (p % 2) * 512:(p % 2) * 512 + 256],
                                 orow[:], expr[p][:], start=True, stop=True)
                    b.c_mm.inc(i)
            @blk.scalar
            def _(e):
                for p in range(16):
                    e.wait_ge(b.c_mm.h, mm0c + p + 1)
                    if p >= 2:
                        e.wait_ge(b.c_dv.h, dv0c + p - 1)
                    i = e.activation(abs_t[p % 2][:],
                                     b.PA[0:128, (p % 2) * 512:(p % 2) * 512 + 256],
                                     AF.Copy)
                    b.c_ev.inc(i)
            @blk.vector
            def _(v):
                for p in range(16):
                    hh, gg = p // 4, p % 4
                    v.wait_ge(b.c_ev.h, ev0c + p + 1)
                    last = None
                    for c in range(KC):
                        if gg == 0:
                            last = v.tensor_tensor(sub(C, 4, hh * 8 + c),
                                                   sub(C, 0, gg * 8 + c),
                                                   abs_t[p % 2][:], ALU.mult)
                        else:
                            v.tensor_tensor(tmpc[:], sub(C, 0, gg * 8 + c),
                                            abs_t[p % 2][:], ALU.mult)
                            v.drain()
                            last = v.tensor_tensor(sub(C, 4, hh * 8 + c),
                                                   sub(C, 4, hh * 8 + c),
                                                   tmpc[:], ALU.add)
                            v.drain()
                    b.c_dv.inc(last)
        with nc.Block() as blk:
            @blk.sync
            def _(sy, t0c=t0c):
                for i in range(4):
                    b.ld(sy, "h",
                         o_hbm[i * 1024:(i + 1) * 1024, t0c:t0c + 256]
                         .rearrange("(s p) t -> p s t", p=128),
                         C[4 + i][:].rearrange("p (s t) -> p s t", t=256))
                b.wq(sy, "h")

    bo_cols = env["gb"]["l3a_bo"]
    ot = [b.sb([128, 2048], BF16, OFF_B + i * 4096, f"l3a_ot{i}") for i in range(2)]
    for hh in range(4):
        mm0w, ev0w = b.c_mm.n, b.c_ev.n
        hp0 = [b.hp[0].n, b.hp[1].n]
        wt0 = [b.ws[0].n, b.ws[1].n]
        with nc.Block() as blk:
            @blk.sync
            def _(sy, hh=hh):
                for kc in range(KC):
                    if kc >= 2:
                        sy.wait_ge(b.c_mm.h, mm0w + 4 * (kc - 1))
                    hp = b.hp[kc % 2]
                    sy.dma_start(ot[kc % 2][:],
                                 o_hbm[hh * 1024 + kc * 128: hh * 1024 + (kc + 1) * 128, :]
                                 ).then_inc(hp.h, 16)
                    hp.n += 16
            @blk.gpsimd
            def _(g, hh=hh):
                i = 0
                for kc in range(KC):
                    for r in range(4):
                        if i >= 2:
                            g.wait_ge(b.c_mm.h, mm0w + i - 1)
                        ws = b.ws[i % 2]
                        g.dma_start(
                            wbuf[i % 2][:, 0:1024],
                            wo_bf[r * 1024 + kc * 128: r * 1024 + (kc + 1) * 128, :]
                        ).then_inc(ws.h, 16)
                        ws.n += 16
                        i += 1
            @blk.tensor
            def _(t, hh=hh):
                i = 0
                for kc in range(KC):
                    t.wait_ge(b.hp[kc % 2].h, hp0[kc % 2] + 16 * (kc // 2 + 1))
                    for r in range(4):
                        t.wait_ge(b.ws[i % 2].h, wt0[i % 2] + 16 * (i // 2 + 1))
                        rhs = ot[kc % 2][:].rearrange("p (a r) -> p r a", r=4)[:, r, :]
                        last = None
                        for m in range(8):
                            tgt = b.PA if m < 4 else b.PB
                            last = t.matmul(
                                tgt[:, (m % 4) * 512:(m % 4) * 512 + 512],
                                wbuf[i % 2][:, m * 128:(m + 1) * 128],
                                rhs,
                                start=(kc == 0 and r == 0),
                                stop=(kc == KC - 1 and r == 3))
                        b.c_mm.inc(last)
                        i += 1
        with nc.Block() as blk:
            @blk.scalar
            def _(e, hh=hh):
                for m in range(8):
                    tgt = b.PA if m < 4 else b.PB
                    e.activation(A[m][:, hh * 512:(hh + 1) * 512],
                                 tgt[:, (m % 4) * 512:(m % 4) * 512 + 512],
                                 AF.Identity, bias=bo_cols[m])
    return A, Bp, OFF_B




L2A_UPTO = 4


def emit_l2a(b, env):
    nc = b.nc
    A, Bp, C, Dp = env["pA"], env["pB"], env["pC"], env["pD"]
    wbuf = env["wbuf"]
    e2, bo16 = env["e2"], env["bo16all"]
    wqkv = b.inhbm("l2a_wqkv")
    wout = b.inhbm("l2a_wout")

    cc_in = nc.dram_tensor("l2a_ccin", [2048, 2048], BF16).ap()
    cc_out = nc.dram_tensor("l2a_ccout", [16384, 2048], BF16,
                            addr_space="Shared").ap()

    emit_proj_fm(b, wqkv, 1024, Bp,
                 lambda e, j, ps: e.activation(C[j][:], ps[:], AF.Copy,
                                               scale=1.0 / 32.0), wbuf, col_off=0)
    stage = [b.sb([128, 2048], BF16, OFF_D, "l2a_st0"),
             b.sb([128, 2048], BF16, OFF_D + 4096, "l2a_st1")]
    emit_proj_fm_store(b, wqkv, 1024, Bp, AF.Copy, stage, cc_in[0:1024, :],
                       wbuf, col_off=1024)
    emit_proj_fm_store(b, wqkv, 1024, Bp, AF.Copy, stage, cc_in[1024:2048, :],
                       wbuf, col_off=2048)
    with nc.Block() as blk:
        @blk.gpsimd
        def _(g):
            g.wait_ge(b.st[0].h, b.st[0].n)
            g.wait_ge(b.st[1].h, b.st[1].n)
            g.collective_compute(
                "AllGather", ALU.bypass, replica_groups=[list(range(8))],
                ins=[cc_in.opt()], outs=[cc_out.opt()]).then_inc(b.c_pl.h, 1)
            b.c_pl.n += 1
            g.wait_ge(b.c_pl.h, b.c_pl.n)

    kj = [b.sb([128, 2048], BF16, OFF_D, "l2a_kj0"),
          b.sb([128, 2048], BF16, OFF_D + 4096, "l2a_kj1")]
    u2 = [b.sb([128, 2048], BF16, OFF_D + 8192, "l2a_u0"),
          b.sb([128, 2048], BF16, OFF_D + 12288, "l2a_u1")]
    tmpd = b.sb([128, 2048], BF16, OFF_D + 16384, "l2a_tmp")
    rden16 = b.sb([16, 2048], BF16, OFF_D + 20480, "l2a_rden")
    den16 = b.sb([16, 2048], F32, OFF_D + 24576, "l2a_den")
    rdp = [b.sb([2, 2048], BF16, SCRW + 8192, "l2a_rdp0"),
           b.sb([2, 2048], BF16, OFF_D + 8192, "l2a_rdp1")]

    for j in range(8 if L2A_UPTO >= 2 else 0):
        hp0 = [b.hp[0].n, b.hp[1].n]
        mm0, ev0, dv0 = b.c_mm.n, b.c_ev.n, b.c_dv.n
        with nc.Block() as blk:
            @blk.sync
            def _(sy, j=j):
                for c in range(KC):
                    if c >= 2:
                        sy.wait_ge(b.c_dv.h, dv0 + c - 1)
                    hp = b.hp[c % 2]
                    sy.dma_start(kj[c % 2][:],
                                 cc_out[j * 2048 + c * 128: j * 2048 + (c + 1) * 128, :]
                                 ).then_inc(hp.h, 16)
                    hp.n += 16
            @blk.vector
            def _(v, j=j):
                for c in range(KC):
                    v.wait_ge(b.hp[c % 2].h, hp0[c % 2] + 16 * (c // 2 + 1))
                    if c >= 2:
                        v.wait_ge(b.c_mm.h, mm0 + c - 1)
                    i = v.tensor_tensor(u2[c % 2][:], C[c][:], kj[c % 2][:],
                                        ALU.mult)
                    b.c_dv.inc(i)
            @blk.tensor
            def _(t, j=j):
                for c in range(KC):
                    t.wait_ge(b.c_dv.h, dv0 + c + 1)
                    last = None
                    for nb in range(NB):
                        last = t.matmul(
                            b.PB[0:16, nb * 512:(nb + 1) * 512],
                            bo16[:, c * 16:(c + 1) * 16],
                            u2[c % 2][:, nb * 512:(nb + 1) * 512],
                            start=(c == 0), stop=(c == KC - 1))
                    b.c_mm.inc(last)
            @blk.scalar
            def _(e, j=j):
                e.wait_ge(b.c_mm.h, mm0 + KC)
                e.activation(Bp[j][0:16, :], b.PB[0:16, 0:2048], AF.Exp,
                             bias=-20.0)

    if L2A_UPTO < 2:
        return A, Bp, OFF_B
    with nc.Block() as blk:
        @blk.vector
        def _(v):
            v.tensor_tensor(den16[:], Bp[0][0:16, :], Bp[1][0:16, :], ALU.add)
            for j in range(2, 8):
                v.drain()
                v.tensor_tensor(den16[:], den16[:], Bp[j][0:16, :], ALU.add)
            v.drain()
            with nc.allow_low_precision("l2a rden"):
                v.reciprocal(rden16[:], den16[:])

    for c in range(KC):
        if L2A_UPTO < 3:
            break
        hp0 = [b.hp[0].n, b.hp[1].n]
        st0l = [b.st[0].n, b.st[1].n]
        mm0, ev0, dv0, pl0 = b.c_mm.n, b.c_ev.n, b.c_dv.n, b.c_pl.n
        with nc.Block() as blk:
            @blk.sync
            def _(sy, c=c):
                for j in range(8):
                    if j >= 2:
                        sy.wait_ge(b.c_dv.h, dv0 + j - 1)
                    hp = b.hp[j % 2]
                    sy.dma_start(kj[j % 2][:],
                                 cc_out[j * 2048 + 1024 + c * 128:
                                        j * 2048 + 1024 + (c + 1) * 128, :]
                                 ).then_inc(hp.h, 16)
                    hp.n += 16
                    st = b.st[j % 2]
                    sy.dma_start(rdp[j % 2][:],
                                 Bp[j][2 * c:2 * c + 2, :]).then_inc(st.h, 16)
                    st.n += 16
            @blk.tensor
            def _(t, c=c):
                for j in range(8):
                    t.wait_ge(b.st[j % 2].h, st0l[j % 2] + 16 * (j // 2 + 1))
                    if j >= 2:
                        t.wait_ge(b.c_dv.h, dv0 + j - 1)
                    tgt = b.PA if j % 2 == 0 else b.PB
                    last = None
                    for nb in range(NB):
                        last = t.matmul(tgt[:, nb * 512:(nb + 1) * 512],
                                        e2[:], rdp[j % 2][:, nb * 512:(nb + 1) * 512],
                                        start=True, stop=True)
                    b.c_mm.inc(last)
            @blk.vector
            def _(v, c=c):
                for j in range(8):
                    v.wait_ge(b.c_mm.h, mm0 + j + 1)
                    v.wait_ge(b.hp[j % 2].h, hp0[j % 2] + 16 * (j // 2 + 1))
                    tgt = b.PA if j % 2 == 0 else b.PB
                    if j == 0:
                        i = v.tensor_tensor(C[c][:], kj[0][:], tgt[:], ALU.mult)
                    else:
                        v.tensor_tensor(tmpd[:], kj[j % 2][:], tgt[:], ALU.mult)
                        v.drain()
                        i = v.tensor_tensor(C[c][:], C[c][:], tmpd[:], ALU.add)
                    v.drain()
                    b.c_dv.inc(i)
        with nc.Block() as blk:
            @blk.sync
            def _(sy, c=c):
                b.ld(sy, "h", rdp[0][:], rden16[2 * c:2 * c + 2, :])
                b.wq(sy, "h")
        with nc.Block() as blk:
            @blk.tensor
            def _(t):
                for nb in range(NB):
                    t.matmul(b.PA[:, nb * 512:(nb + 1) * 512], e2[:],
                             rdp[0][:, nb * 512:(nb + 1) * 512],
                             start=True, stop=True)
        with nc.Block() as blk:
            @blk.vector
            def _(v, c=c):
                v.tensor_tensor(C[c][:], C[c][:], b.PA[:], ALU.mult)

    if L2A_UPTO < 4:
        return A, Bp, OFF_B
    emit_proj_fm(b, wout, 1024, C,
                 lambda e, j, ps: e.tensor_tensor(A[j][:], ps[:], A[j][:], ALU.add),
                 wbuf, evac_eng="vector")
    return A, Bp, OFF_B


NEED_Y = {
    "l0a": [("plain", "pA")],
    "l0f": [("plain", "pA")],
    "l1a": [("affine", "pA", "l1a_g", "l1a_b")],
    "l1f": [("center",)],
    "l2a": [("plain", "pA"), ("affine", "pB", "l2a_g", "l2a_b")],
    "l2f": [("affine", "pA", "l2f_g1", "l2f_b1")],
    "l3a": [("plain", "pA")],
    "l3f": [("plain", "pA")],
}

GB_ROWS = {"l0a_g": "l0a_g_row", "l0a_b": "l0a_b_row",
           "l1f_bias": "l1f_bias_row",
           "l3a_bq": "l3a_bq_row", "l3a_bk": "l3a_bk_row",
           "l3a_bv": "l3a_bv_row", "l3a_bo": "l3a_bo_row",
           "l0f_mur": "l0f_mur_row", "l0f_muk": "l0f_muk_row",
           "l3f_mur": "l3f_mur_row", "l3f_muk": "l3f_muk_row",
           "l1a_g": "l1a_g_row", "l1a_b": "l1a_b_row",
           "l2a_g": "l2a_g_row", "l2a_b": "l2a_b_row",
           "l2f_g1": "l2f_g1_row", "l2f_b1": "l2f_b1_row",
           "l2f_g2": "l2f_g2_row", "l2f_b2": "l2f_b2_row"}

EMITTERS = {}
EMITTERS["l0a"] = emit_l0a
EMITTERS["l0f"] = emit_l0f
EMITTERS["l3f"] = emit_l3f
EMITTERS["l1f"] = emit_l1f
EMITTERS["l1a"] = emit_l1a
EMITTERS["l2f"] = emit_l2f
EMITTERS["l3a"] = emit_l3a
EMITTERS["l2a"] = emit_l2a


def yspecs(env, st):
    out = []
    for sp in NEED_Y[st]:
        if sp[0] == "plain":
            out.append(("plain", env[sp[1]]))
        elif sp[0] == "center":
            out.append(("center",))
        else:
            out.append(("affine", env[sp[1]], env["gb"][sp[2]], env["gb"][sp[3]]))
    return out


def build_nc(arrays, steps):
    b = Bld(arrays)
    nc = b.nc
    x_in = nc.dram_tensor("x_in", [1024, 2048], F32, kind="ExternalInput").ap()
    b.used.append("x_in")
    x_out = nc.dram_tensor("x_out", [1024, 2048], F32, kind="ExternalOutput").ap()

    env = {}
    env["X"] = [b.sb([128, 2048], F32, OFF_X + c * 8192, f"X{c}") for c in range(KC)]
    for nm, off in (("pA", OFF_A), ("pB", OFF_B), ("pC", OFF_C), ("pD", OFF_D)):
        env[nm] = [b.sb([128, 2048], BF16, off + c * 4096, f"{nm}{c}") for c in range(KC)]
    env["wbuf"] = [b.sb([128, 2048], BF16, SCRW, "wb0"),
                   b.sb([128, 2048], BF16, SCRW + 4096, "wb1")]
    env["ones_col"] = b.sb([128, 1], BF16, OFF_CONST, "ones_col")
    env["biases"] = b.sb([128, 7], F32, OFF_CONST + 1856, "biases")
    for bi, bval in enumerate([1e-5, 1e-8, float(np.pi / 2), 1.0, 0.5, -30.0, -20.0]):
        nc.const_aps.register(F32, bval, env["biases"][:, bi:bi + 1])
    env["ones_row"] = b.sb([1, 128], BF16, OFF_CONST + 32, "ones_row")
    env["e2"] = b.sb([2, 128], BF16, OFF_CONST + 1056, "e2")
    env["ident"] = b.sb([128, 128], BF16, OFF_CONST + 288, "ident")
    if "l1f" in steps:
        env["yc"] = [b.sb([128, 2048], F32, OFF_A + c * 8192, f"yc{c}")
                     for c in range(KC)]
        env["l1f_inv_hbm"] = nc.dram_tensor("l1f_invh", [1, 2048], BF16).ap()
    env["tri"] = b.sb([128, 128], BF16, OFF_CONST + 544, "tri")
    env["iota8"] = b.sb([128, 8], F32, OFF_CONST + 1312, "iota8")
    env["identf"] = b.sb([128, 128], F32, OFF_CONST + 1344, "identf")
    env["bo16all"] = b.sb([128, 128], BF16, OFF_CONST + 800, "bo16all")

    env["gb"] = {}
    gb_needed = []
    for st in steps:
        for sp in NEED_Y.get(st, []):
            if sp[0] == "affine":
                gb_needed += [sp[2], sp[3]]
    if "l0a" in steps:
        gb_needed += ["l0a_g", "l0a_b"]
    if "l0f" in steps:
        gb_needed += ["l0f_mur", "l0f_muk"]
    if "l3f" in steps:
        gb_needed += ["l3f_mur", "l3f_muk"]
    if "l1f" in steps:
        gb_needed += ["l1f_bias"]
    if "l3a" in steps:
        gb_needed += ["l3a_bq", "l3a_bk", "l3a_bv", "l3a_bo"]
    if "l2f" in steps:
        gb_needed += ["l2f_g2", "l2f_b2"]
    gb_off = OFF_CONST + 1888
    gb_tiles = {}
    for gname in dict.fromkeys(gb_needed):
        wcols = b_arr_len(arrays := None, GB_ROWS[gname]) if False else None
        import numpy as _np
        wid = int(b.arrays[GB_ROWS[gname]].shape[1]) // 128
        tile = b.sb([128, wid], F32, gb_off, f"gb_{gname}")
        gb_tiles[gname] = tile
        env["gb"][gname] = [tile[:, c:c + 1] for c in range(wid)]
        gb_off += 4 * wid
    assert gb_off <= OFF_CONST + 3072, gb_off

    with nc.Block() as blk:
        @blk.sync
        def _(s):
            for c in range(KC):
                b.ld(s, "h", env["X"][c][:], x_in[c * 128:(c + 1) * 128, :])
            b.ld(s, "h", env["ones_col"][:], b.inhbm("c_ones_col")[:])
            b.ld(s, "h", env["biases"][:], b.inhbm("c_biases")[:])
            b.ld(s, "h", env["ones_row"][:], b.inhbm("c_ones_row")[:])
            b.ld(s, "h", env["ident"][:], b.inhbm("c_ident")[:])
            b.ld(s, "h", env["tri"][:], b.inhbm("c_tri")[:])
            b.ld(s, "h", env["iota8"][:], b.inhbm("c_iota8")[:])
            b.ld(s, "h", env["identf"][:], b.inhbm("c_identf")[:])
            b.ld(s, "h", env["bo16all"][:], b.inhbm("c_bo16all")[:])
            b.ld(s, "h", env["e2"][:], b.inhbm("c_e2")[:])
            for gname, tile in gb_tiles.items():
                src = b.inhbm(GB_ROWS[gname]).rearrange("one (c p) -> (one p) c", p=128)
                b.ld(s, "h", tile[:], src, allow_slow_non_contiguous=True)
            b.wq(s, "h")

    first_specs = yspecs(env, steps[0]) if steps else []
    emit_boundary(b, env, None, env["pD"], OFF_D, first_specs, "binit")

    for i, st in enumerate(steps):
        z_pool, sq_pool, sq_off = EMITTERS[st](b, env)
        nxt = yspecs(env, steps[i + 1]) if i + 1 < len(steps) else []
        emit_boundary(b, env, z_pool, sq_pool, sq_off, nxt, f"b_{st}")

    with nc.Block() as blk:
        @blk.sync
        def _(s):
            for c in range(KC):
                b.ld(s, "h", x_out[c * 128:(c + 1) * 128, :], env["X"][c][:])
            b.wq(s, "h")
    return b, env


def run_steps(arrays, x_fm_percore, steps, trace=False):
    b, env = build_nc(arrays, steps)
    in_maps = []
    for c in range(len(x_fm_percore)):
        m = {"x_in": x_fm_percore[c]}
        for nm in b.used:
            if nm != "x_in":
                m[nm] = arrays[nm]
        in_maps.append(m)
    res = run_bass_kernel_spmd(b.nc, in_maps, core_ids=list(range(len(in_maps))),
                               trace=trace)
    return [r["x_out"] for r in res.results], res


STEPS_ALL = ["l0a", "l0f", "l1a", "l1f", "l2a", "l2f", "l3a", "l3f"]


def kernel(**inputs):
    arrays = _prep(inputs)
    x = np.asarray(inputs["L0_attn_inputs"], dtype=np.float32)
    x_fm = [np.ascontiguousarray(x[c].T) for c in range(8)]
    outs, _ = run_steps(arrays, x_fm, STEPS_ALL)
    return np.stack([o.T for o in outs], axis=0)




# version 02
# speedup vs baseline: 1.0866x; 1.0866x over previous
import numpy as np
import ml_dtypes
import concourse.bass as bass
import concourse.mybir as mybir
from concourse.bass_utils import run_bass_kernel_spmd

F32 = mybir.dt.float32
BF16 = mybir.dt.bfloat16
AF = mybir.ActivationFunctionType
ALU = mybir.AluOpType
AX = mybir.AxisListType

D = 1024
T = 2048
KC = 8
NB = 4
EPS = 1e-5
BF = ml_dtypes.bfloat16

BASE = 16512
OFF_X = BASE
OFF_A = BASE + 65536
OFF_B = BASE + 98304
OFF_C = BASE + 131072
OFF_D = BASE + 163840
OFF_CONST = BASE + 196608 + 512
SCRW = BASE + 196608 + 512 + 3072
SBUF_END = 229344



def _reorder_qkv_ed(W):
    Wt = np.ascontiguousarray(np.asarray(W, dtype=np.float32).T).reshape(1024, 16, 3, 64)
    q = Wt[:, :, 0, :].reshape(1024, 1024)
    k = Wt[:, :, 1, :].reshape(1024, 1024)
    v = Wt[:, :, 2, :].reshape(1024, 1024)
    return np.ascontiguousarray(np.concatenate([q, k, v], axis=1))


def _prep(inp):
    f = lambda a: np.ascontiguousarray(np.asarray(a, dtype=np.float32))
    row = lambda a: np.ascontiguousarray(np.asarray(a, dtype=np.float32).reshape(1, -1))
    w = {}
    w["l0a_wq"] = f(inp["L0_attn_W_Q"])
    w["l0a_wk"] = f(inp["L0_attn_W_K"])
    w["l0a_wv"] = f(inp["L0_attn_W_V"])
    w["l0a_wfc"] = f(inp["L0_attn_W_fc"])
    w["l0a_g_row"] = row(inp["L0_attn_gamma"])
    w["l0a_b_row"] = row(inp["L0_attn_beta"])
    w["l0f_wr"] = f(np.asarray(inp["L0_ffn_W_r"]).T)
    w["l0f_wk"] = f(np.asarray(inp["L0_ffn_W_k"]).T)
    w["l0f_wv"] = f(np.asarray(inp["L0_ffn_W_v"]).T)
    w["l0f_mur_row"] = row(inp["L0_ffn_mu_r"])
    w["l0f_muk_row"] = row(inp["L0_ffn_mu_k"])
    w["l1a_g_row"] = row(inp["L1_attn_ln_gamma"])
    w["l1a_b_row"] = row(inp["L1_attn_ln_beta"])
    w["l1a_wqkv"] = _reorder_qkv_ed(inp["L1_attn_W_qkv"])
    w["l1a_wo"] = f(np.asarray(inp["L1_attn_W_o"]).T)
    w["l1f_w"] = f(inp["L1_ffn_weight"])
    w["l1f_bias_row"] = row(inp["L1_ffn_bias"])
    w["l2a_wqkv"] = _reorder_qkv_ed(inp["L2_attn_w_qkv"])
    w["l2a_wout"] = f(np.asarray(inp["L2_attn_w_out"]).T)
    w["l2a_g_row"] = row(inp["L2_attn_ln_w"])
    w["l2a_b_row"] = row(inp["L2_attn_ln_b"])
    w["l2f_g1_row"] = row(inp["L2_ffn_ln1_w"])
    w["l2f_b1_row"] = row(inp["L2_ffn_ln1_b"])
    w["l2f_wqkv"] = f(inp["L2_ffn_w_qkv"])
    w["l2f_wproj"] = f(inp["L2_ffn_w_proj"])
    w["l2f_g2_row"] = row(inp["L2_ffn_ln2_w"])
    w["l2f_b2_row"] = row(inp["L2_ffn_ln2_b"])
    w["l2f_gatew"] = f(inp["L2_ffn_gate_w"])
    w["l2f_gateb_row"] = row(inp["L2_ffn_gate_b"])
    for e in range(8):
        w[f"l2f_w1_{e}"] = f(np.asarray(inp["L2_ffn_w1"])[e])
        w[f"l2f_w2_{e}"] = f(np.asarray(inp["L2_ffn_w2"])[e])
    w["l3a_phase"] = f(inp["L3_attn_phase"])
    w["l3a_wq"] = f(inp["L3_attn_Wq"])
    w["l3a_wk"] = f(inp["L3_attn_Wk"])
    w["l3a_wv"] = f(inp["L3_attn_Wv"])
    w["l3a_bq_row"] = row(inp["L3_attn_bq"])
    w["l3a_bk_row"] = row(inp["L3_attn_bk"])
    w["l3a_bv_row"] = row(inp["L3_attn_bv"])
    w["l3a_wo"] = f(inp["L3_attn_Wo"])
    w["l3a_bo_row"] = row(inp["L3_attn_bo"])
    w["l3f_wr"] = f(np.asarray(inp["L3_ffn_W_r"]).T)
    w["l3f_wk"] = f(np.asarray(inp["L3_ffn_W_k"]).T)
    w["l3f_wv"] = f(np.asarray(inp["L3_ffn_W_v"]).T)
    w["l3f_mur_row"] = row(inp["L3_ffn_mu_r"])
    w["l3f_muk_row"] = row(inp["L3_ffn_mu_k"])
    BIAS_VALS = [1e-5, 1e-8, float(np.pi / 2), 1.0, 0.5, -30.0, -20.0]
    w["c_biases"] = np.tile(np.asarray(BIAS_VALS, dtype=np.float32)[None, :], (128, 1))
    w["c_ones_row"] = np.ones((1, 128), dtype=BF)
    w["c_ones_col"] = np.ones((128, 1), dtype=BF)
    w["c_ident"] = np.eye(128, dtype=BF)
    w["c_identf"] = np.eye(128, dtype=np.float32)
    w["c_tri"] = (np.arange(128)[None, :] >= np.arange(128)[:, None]).astype(BF)
    w["c_iota8"] = np.tile(np.arange(8, dtype=np.float32)[None, :], (128, 1))
    bo16 = np.zeros((128, 128), dtype=BF)
    for c_ in range(8):
        for h_ in range(16):
            if h_ // 2 == c_:
                rows = slice((h_ % 2) * 64, (h_ % 2) * 64 + 64)
                bo16[rows, c_ * 16 + h_] = 1
    w["c_bo16all"] = bo16
    e2_ = np.zeros((2, 128), dtype=BF)
    e2_[0, :64] = 1
    e2_[1, 64:] = 1
    w["c_e2"] = e2_
    o64 = np.zeros((65, 128), dtype=BF)
    o64[64, :] = 1
    w["c_ones64"] = o64
    return w



class Ctr:
    def __init__(self, h):
        self.h = h
        self.n = 0

    def inc(self, inst, k=1):
        inst.then_inc(self.h, k)
        self.n += k
        return self.n


class Bld:
    def __init__(self, arrays):
        nc = bass.Bass("TRN2", target_bir_lowering=False)
        self.nc = nc
        self.arrays = arrays
        self.hbm = {}
        self.used = []
        self.q = {
            "h": Ctr(nc.alloc_semaphore("sem_h")),
            "s": Ctr(nc.alloc_semaphore("sem_s")),
            "g": Ctr(nc.alloc_semaphore("sem_g")),
        }
        self.ws = [Ctr(nc.alloc_semaphore("sem_w0")),
                   Ctr(nc.alloc_semaphore("sem_w1"))]
        self.st = [Ctr(nc.alloc_semaphore("sem_t0")),
                   Ctr(nc.alloc_semaphore("sem_t1"))]
        self.hp = [Ctr(nc.alloc_semaphore("sem_p0")),
                   Ctr(nc.alloc_semaphore("sem_p1"))]
        self.c_mm = Ctr(nc.alloc_semaphore("sem_mm"))
        self.c_ev = Ctr(nc.alloc_semaphore("sem_ev"))
        self.c_dv = Ctr(nc.alloc_semaphore("sem_dv"))
        self.c_pl = Ctr(nc.alloc_semaphore("sem_pl"))
        self.PA = nc.alloc_psum_tensor("PSA", [128, 2048], F32)
        self.PB = nc.alloc_psum_tensor("PSB", [128, 2048], F32)
        self._sbn = 0

    def inhbm(self, name):
        if name not in self.hbm:
            a = self.arrays[name]
            dt = BF16 if a.dtype == BF else F32
            self.hbm[name] = self.nc.dram_tensor(name, list(a.shape), dt,
                                                 kind="ExternalInput").ap()
            self.used.append(name)
        return self.hbm[name]

    def sb(self, shape, dtype, off, name=None):
        self._sbn += 1
        nm = name or f"sb{self._sbn}"
        esz = 2 if dtype == BF16 else 4
        nbytes = int(np.prod(shape[1:])) * esz
        assert off + nbytes <= SBUF_END, f"{nm}: {off}+{nbytes} over SBUF end"
        return self.nc.alloc_sbuf_tensor_at(nm, list(shape), dtype, offset=off)

    def ld(self, eng, qn, dst, src, **kw):
        eng.dma_start(dst, src, **kw).then_inc(self.q[qn].h, 16)
        self.q[qn].n += 16
        return self.q[qn].n

    def wq(self, eng, qn, tgt=None):
        eng.wait_ge(self.q[qn].h, self.q[qn].n if tgt is None else tgt)



def emit_proj_fm(b, w_hbm, n_out, rhs_tiles, evac, wbuf, evac_eng="scalar",
                 col_off=0):
    nc = b.nc
    n_units = n_out // 256
    mm0, ev0 = b.c_mm.n, b.c_ev.n
    wt = []

    with nc.Block() as blk:
        @blk.gpsimd
        def _(g):
            for u in range(n_units):
                if u >= 2:
                    g.wait_ge(b.c_mm.h, mm0 + 2 * (u - 1))
                src = w_hbm[:, col_off + u * 256: col_off + (u + 1) * 256]
                src = src.rearrange("(kc kp) m -> kp kc m", kp=128)
                dst = wbuf[u % 2][:].rearrange("p (kc m) -> p kc m", m=256)
                ws = b.ws[u % 2]
                g.dma_start(dst, src).then_inc(ws.h, 16)
                ws.n += 16
                wt.append(ws.n)

        @blk.tensor
        def _(t):
            for u in range(n_units):
                t.wait_ge(b.ws[u % 2].h, wt[u])
                for mo in range(2):
                    j = u * 2 + mo
                    tgt = b.PA if j % 2 == 0 else b.PB
                    if j >= 2:
                        t.wait_ge(b.c_ev.h, ev0 + j - 1)
                    last = None
                    for tb in range(NB):
                        for kc in range(KC):
                            last = t.matmul(
                                tgt[:, tb * 512:(tb + 1) * 512],
                                wbuf[u % 2][:, kc * 256 + mo * 128: kc * 256 + (mo + 1) * 128],
                                rhs_tiles[kc][:, tb * 512:(tb + 1) * 512],
                                start=(kc == 0), stop=(kc == KC - 1))
                    b.c_mm.inc(last)

        def evs(e):
            for j in range(2 * n_units):
                tgt = b.PA if j % 2 == 0 else b.PB
                e.wait_ge(b.c_mm.h, mm0 + j + 1)
                last = evac(e, j, tgt)
                b.c_ev.inc(last)

        if evac_eng == "scalar":
            @blk.scalar
            def _(e):
                evs(e)
        else:
            @blk.vector
            def _(e):
                evs(e)


def emit_proj_tm(b, w_hbm, n_out, rhs_tiles, evac, wbuf, evac_eng="scalar",
                 col_off=0):
    nc = b.nc
    n_units = n_out // 256
    mm0, ev0 = b.c_mm.n, b.c_ev.n
    wt = []

    def pslice(i):
        off = (i % 4) * 512
        return b.PA[:, off:off + 256]

    with nc.Block() as blk:
        @blk.gpsimd
        def _(g):
            for u in range(n_units):
                if u >= 2:
                    g.wait_ge(b.c_mm.h, mm0 + 16 * (u - 1))
                src = w_hbm[:, col_off + u * 256: col_off + (u + 1) * 256]
                src = src.rearrange("(kc kp) m -> kp kc m", kp=128)
                dst = wbuf[u % 2][:].rearrange("p (kc m) -> p kc m", m=256)
                ws = b.ws[u % 2]
                g.dma_start(dst, src).then_inc(ws.h, 16)
                ws.n += 16
                wt.append(ws.n)

        @blk.tensor
        def _(t):
            i = 0
            for u in range(n_units):
                t.wait_ge(b.ws[u % 2].h, wt[u])
                for tt in range(16):
                    if i >= 4:
                        t.wait_ge(b.c_ev.h, ev0 + i - 3)
                    last = None
                    for kc in range(KC):
                        last = t.matmul(
                            pslice(i),
                            rhs_tiles[kc][:, tt * 128:(tt + 1) * 128],
                            wbuf[u % 2][:, kc * 256:(kc + 1) * 256],
                            start=(kc == 0), stop=(kc == KC - 1))
                    b.c_mm.inc(last)
                    i += 1

        def evs(e):
            i = 0
            for u in range(n_units):
                for tt in range(16):
                    e.wait_ge(b.c_mm.h, mm0 + i + 1)
                    last = evac(e, tt, u, pslice(i))
                    b.c_ev.inc(last)
                    i += 1

        if evac_eng == "scalar":
            @blk.scalar
            def _(e):
                evs(e)
        else:
            @blk.vector
            def _(e):
                evs(e)



def emit_stats(b, tiles, lhsT_fn, m, dstA=False):
    nc = b.nc
    tgt = b.PA if dstA else b.PB
    with nc.Block() as blk:
        @blk.tensor
        def _(t):
            for nb in range(NB):
                for kc in range(KC):
                    t.matmul(tgt[0:m, nb * 512:(nb + 1) * 512],
                             lhsT_fn(kc), tiles[kc][:, nb * 512:(nb + 1) * 512],
                             start=(kc == 0), stop=(kc == KC - 1))


def emit_rowmath_plain(b, s0_ap, s1_ap, scratch_row, a_row, b_row, eps=EPS):
    nc = b.nc
    with nc.Block() as blk:
        @blk.scalar
        def _(e):
            e.activation(scratch_row[:], s0_ap, AF.Square, scale=1.0 / D)
    with nc.Block() as blk:
        @blk.vector
        def _(v):
            v.scalar_tensor_tensor(scratch_row[:], s1_ap, 1.0 / D, scratch_row[:],
                                   ALU.mult, ALU.subtract)
    with nc.Block() as blk:
        @blk.scalar
        def _(e):
            e.activation(scratch_row[:], scratch_row[:], AF.Sqrt, bias=eps)
    with nc.Block() as blk:
        @blk.vector
        def _(v):
            with b.nc.allow_low_precision("bf16 LN broadcast rows"):
                v.reciprocal(a_row[:], scratch_row[:])
            v.drain()
            v.scalar_tensor_tensor(b_row[:], s0_ap, -1.0 / D, a_row[:],
                                   ALU.mult, ALU.mult)


def emit_bcast2(b, a_row, b_row, env):
    nc = b.nc
    orow = env["ones_row"]
    with nc.Block() as blk:
        @blk.tensor
        def _(t):
            for nb in range(NB):
                t.matmul(b.PA[:, nb * 512:(nb + 1) * 512], orow[:],
                         a_row[:, nb * 512:(nb + 1) * 512], start=True, stop=True)
                t.matmul(b.PB[:, nb * 512:(nb + 1) * 512], orow[:],
                         b_row[:, nb * 512:(nb + 1) * 512], start=True, stop=True)


def emit_affine_apply(b, src_tiles, dst_pool, tmp_a, tmp_b, gb=None, add_into_x=None):
    nc = b.nc
    dv0, pl0 = b.c_dv.n, b.c_pl.n
    pool_mode = (add_into_x is not None) or (gb is not None)
    with nc.Block() as blk:
        @blk.vector
        def _(v):
            for c in range(KC):
                tmp = tmp_a if c % 2 == 0 else tmp_b
                if pool_mode and c >= 2:
                    v.wait_ge(b.c_pl.h, pl0 + (c - 1))
                v.tensor_tensor(tmp[:], src_tiles[c][:], b.PA[:], ALU.mult)
                v.drain()
                if pool_mode:
                    i = v.tensor_tensor(tmp[:], tmp[:], b.PB[:], ALU.add)
                else:
                    i = v.tensor_tensor(dst_pool[c][:], tmp[:], b.PB[:], ALU.add)
                b.c_dv.inc(i)
        if add_into_x is not None:
            @blk.gpsimd
            def _(g):
                for c in range(KC):
                    tmp = tmp_a if c % 2 == 0 else tmp_b
                    g.wait_ge(b.c_dv.h, dv0 + c + 1)
                    i = g.tensor_tensor(add_into_x[c][:], add_into_x[c][:],
                                        tmp[:], ALU.add)
                    b.c_pl.inc(i)
        elif gb is not None:
            gcols, bcols = gb
            @blk.gpsimd
            def _(g):
                for c in range(KC):
                    tmp = tmp_a if c % 2 == 0 else tmp_b
                    g.wait_ge(b.c_dv.h, dv0 + c + 1)
                    i = g.tensor_scalar(dst_pool[c][:], tmp[:],
                                        gcols[c], bcols[c], ALU.mult, ALU.add)
                    b.c_pl.inc(i)


def scratch_rows(b, tag, n, base=SCRW, dtype=BF16):
    ts = []
    sz = 2048 * (2 if dtype == BF16 else 4)
    for i in range(n):
        ts.append(b.sb([1, 2048], dtype, base + i * sz, f"row_{tag}_{i}_{b._sbn}"))
    return ts


def emit_ln_stats_rows(b, env, tiles, sq_pool, sq_off, tag):
    nc = b.nc
    ones_col = env["ones_col"]
    a_row, b_row = scratch_rows(b, tag, 2)
    with nc.Block() as blk:
        @blk.scalar
        def _(e):
            for c in range(KC):
                e.activation(sq_pool[c][:], tiles[c][:], AF.Square)
    emit_stats(b, tiles, lambda kc: ones_col[:], 1)
    emit_stats(b, sq_pool, lambda kc: ones_col[:], 1, dstA=True)
    sc_row = b.sb([1, 2048], F32, sq_off, f"scr_{tag}_{b._sbn}")
    emit_rowmath_plain(b, b.PB[0:1, 0:2048], b.PA[0:1, 0:2048], sc_row,
                       a_row, b_row)
    emit_bcast2(b, a_row, b_row, env)


def emit_boundary(b, env, z_pool, sq_pool, sq_off, y_specs, tag):
    nc = b.nc
    X = env["X"]
    tmp_a = b.sb([128, 2048], F32, sq_off + 8192, f"tba_{tag}")
    tmp_b = b.sb([128, 2048], F32, sq_off + 16384, f"tbb_{tag}")

    if z_pool is not None:
        emit_ln_stats_rows(b, env, z_pool, sq_pool, sq_off, tag + "z")
        emit_affine_apply(b, z_pool, None, tmp_a, tmp_b, add_into_x=X)

    if y_specs:
        has_center = any(sp[0] == "center" for sp in y_specs)
        bf_specs = [sp for sp in y_specs if sp[0] != "center"]
        with nc.Block() as blk:
            @blk.scalar
            def _(e):
                for c in range(KC):
                    e.activation(sq_pool[c][:], X[c][:], AF.Square)
        if bf_specs:
            xcast = bf_specs[0][1]
        elif z_pool is not None and z_pool is not sq_pool:
            xcast = z_pool
        else:
            xcast = env["pC"] if sq_pool is not env["pC"] else env["pD"]
        assert xcast is not sq_pool
        with nc.Block() as blk:
            @blk.vector
            def _(v):
                for c in range(KC):
                    v.tensor_copy(xcast[c][:], X[c][:])
        ones_col = env["ones_col"]
        a_row, b_row = scratch_rows(b, tag + "x", 2)
        emit_stats(b, xcast, lambda kc: ones_col[:], 1)
        emit_stats(b, sq_pool, lambda kc: ones_col[:], 1, dstA=True)
        sc_row = b.sb([1, 2048], F32, sq_off, f"scx_{tag}_{b._sbn}")
        emit_rowmath_plain(b, b.PB[0:1, 0:2048], b.PA[0:1, 0:2048], sc_row,
                           a_row, b_row)
        if has_center:
            with nc.Block() as blk:
                @blk.vector
                def _(v):
                    with nc.allow_low_precision("negmu row"):
                        v.tensor_scalar(b_row[:], b.PB[0:1, 0:2048], -1.0 / D,
                                        None, ALU.mult)
                @blk.sync
                def _(sy):
                    b.ld(sy, "h", env["l1f_inv_hbm"], a_row[:])
                    b.wq(sy, "h")
            with nc.Block() as blk:
                @blk.tensor
                def _(t):
                    for nb in range(NB):
                        t.matmul(b.PA[:, nb * 512:(nb + 1) * 512],
                                 env["ones_row"][:],
                                 b_row[:, nb * 512:(nb + 1) * 512],
                                 start=True, stop=True)
            with nc.Block() as blk:
                @blk.vector
                def _(v):
                    for c in range(KC):
                        v.tensor_tensor(env["yc"][c][:], X[c][:], b.PA[:], ALU.add)
        if bf_specs:
            emit_bcast2(b, a_row, b_row, env)
            for spec in bf_specs:
                if spec[0] == "plain":
                    emit_affine_apply(b, X, spec[1], tmp_a, tmp_b)
                else:
                    emit_affine_apply(b, X, spec[1], tmp_a, tmp_b,
                                      gb=(spec[2], spec[3]))



def tm_tile(pool, tt):
    return pool[tt // 2][:, (tt % 2) * 1024:(tt % 2) * 1024 + 1024]


def emit_l0a(b, env):
    nc = b.nc
    X, A, Bp, C, Dp = env["X"], env["pA"], env["pB"], env["pC"], env["pD"]
    ones_col, ones_row = env["ones_col"], env["ones_row"]
    wbuf = env["wbuf"]
    wq, wk, wv, wfc = (b.inhbm(n) for n in ("l0a_wq", "l0a_wk", "l0a_wv", "l0a_wfc"))

    emit_proj_fm(b, wq, 1024, A,
                 lambda e, j, ps: e.activation(Bp[j][:], ps[:], AF.Sigmoid), wbuf)
    emit_proj_tm(b, wk, 1024, A,
                 lambda e, tt, u, ps: e.activation(
                     tm_tile(C, tt)[:, u * 256:(u + 1) * 256], ps[:], AF.Sigmoid), wbuf)
    emit_proj_tm(b, wv, 1024, A,
                 lambda e, tt, u, ps: e.activation(
                     tm_tile(Dp, tt)[:, u * 256:(u + 1) * 256], ps[:], AF.Copy), wbuf)

    kve = b.sb([128, 8 * 65], BF16, SCRW + 8192, "l0a_kve")
    for pair in range(8):
        with nc.Block() as blk:
            @blk.tensor
            def _(t, pair=pair):
                for tt in range(16):
                    t.matmul(b.PA[0:128, 0:128],
                             tm_tile(C, tt)[:, pair * 128:(pair + 1) * 128],
                             tm_tile(Dp, tt)[:, pair * 128:(pair + 1) * 128],
                             start=(tt == 0), stop=(tt == 15))
                for tt in range(16):
                    t.matmul(b.PB[0:128, 0:1],
                             tm_tile(C, tt)[:, pair * 128:(pair + 1) * 128],
                             ones_col[:],
                             start=(tt == 0), stop=(tt == 15))
        with nc.Block() as blk:
            @blk.vector
            def _(v, pair=pair):
                v.tensor_copy(kve[0:64, pair * 65:pair * 65 + 64], b.PA[0:64, 0:64])
                v.drain()
                v.tensor_copy(kve[64:128, pair * 65:pair * 65 + 64],
                              b.PA[64:128, 64:128])
                v.drain()
                v.tensor_copy(kve[:, pair * 65 + 64:pair * 65 + 65], b.PB[0:128, 0:1])

    den_row = b.sb([1, 2048], F32, OFF_D, "l0a_den")
    rden_row = b.sb([1, 2048], BF16, OFF_D + 12288, "l0a_rden")
    rdenb = b.sb([64, 2048], BF16, OFF_D + 8192, "l0a_rdenb")
    for h in range(16):
        pair, half = h // 2, (h % 2) * 64
        with nc.Block() as blk:
            @blk.tensor
            def _(t, pair=pair, half=half):
                for tb in range(NB):
                    t.matmul(b.PA[0:65, tb * 512:(tb + 1) * 512],
                             kve[half:half + 64, pair * 65:(pair + 1) * 65],
                             Bp[pair][half:half + 64, tb * 512:(tb + 1) * 512],
                             start=True, stop=True)
        with nc.Block() as blk:
            @blk.scalar
            def _(e):
                e.activation(den_row[:], b.PA[64:65, 0:2048], AF.Copy, bias=1e-8)
        with nc.Block() as blk:
            @blk.vector
            def _(v):
                with nc.allow_low_precision("bf16 rden row"):
                    v.reciprocal(rden_row[:], den_row[:])
        with nc.Block() as blk:
            @blk.tensor
            def _(t):
                for nb in range(NB):
                    t.matmul(b.PB[0:64, nb * 512:(nb + 1) * 512], ones_row[:, 0:64],
                             rden_row[:, nb * 512:(nb + 1) * 512],
                             start=True, stop=True)
        with nc.Block() as blk:
            @blk.scalar
            def _(e):
                e.activation(rdenb[:], b.PB[0:64, 0:2048], AF.Copy)
        with nc.Block() as blk:
            @blk.vector
            def _(v, pair=pair, half=half):
                v.tensor_tensor(C[pair][half:half + 64, :], b.PA[0:64, 0:2048],
                                rdenb[:], ALU.mult)

    emit_proj_fm(b, wfc, 1024, C,
                 lambda e, j, ps: e.tensor_tensor(Dp[j][:], ps[:], A[j][:], ALU.add),
                 wbuf, evac_eng="vector")

    emit_ln_stats_rows(b, env, Dp, Bp, OFF_B, "l0a_in")
    tmp_a = b.sb([128, 2048], F32, OFF_B + 8192, "l0a_tma")
    tmp_b = b.sb([128, 2048], F32, OFF_B + 16384, "l0a_tmb")
    emit_affine_apply(b, Dp, A, tmp_a, tmp_b,
                      gb=(env["gb"]["l0a_g"], env["gb"]["l0a_b"]))
    return A, Bp, OFF_B





def emit_ffn(b, env, pfx):
    nc = b.nc
    A, Bp, C, Dp = env["pA"], env["pB"], env["pC"], env["pD"]
    wbuf = env["wbuf"]
    wr = b.inhbm(f"{pfx}_wr")
    wk = b.inhbm(f"{pfx}_wk")
    wv = b.inhbm(f"{pfx}_wv")
    mur = env["gb"][f"{pfx}_mur"]
    muk = env["gb"][f"{pfx}_muk"]

    with nc.Block() as blk:
        @blk.vector
        def _(v):
            for c in range(KC):
                v.memset(Bp[c][:, 0:1], 0.0)
                v.drain()
                v.tensor_copy(Bp[c][:, 1:2048], A[c][:, 0:2047])
    with nc.Block() as blk:
        @blk.vector
        def _(v):
            for c in range(KC):
                v.tensor_tensor(C[c][:], A[c][:], Bp[c][:], ALU.subtract)
                v.drain()
                v.scalar_tensor_tensor(Dp[c][:], C[c][:], muk[c], Bp[c][:],
                                       ALU.mult, ALU.add)
                v.drain()
                v.scalar_tensor_tensor(C[c][:], C[c][:], mur[c], Bp[c][:],
                                       ALU.mult, ALU.add)
    emit_proj_fm(b, wr, 1024, C,
                 lambda e, j, ps: e.activation(Bp[j][:], ps[:], AF.Sigmoid), wbuf)

    def ev_relu_sq(e, j, ps):
        e.activation(C[j][:], ps[:], AF.Relu)
        e.drain()
        return e.activation(C[j][:], C[j][:], AF.Square)
    emit_proj_fm(b, wk, 1024, Dp, ev_relu_sq, wbuf)

    emit_proj_fm(b, wv, 1024, C,
                 lambda e, j, ps: e.tensor_tensor(Dp[j][:], ps[:], Bp[j][:], ALU.mult),
                 wbuf, evac_eng="vector")
    return Dp, Bp, OFF_B


def emit_l0f(b, env):
    return emit_ffn(b, env, "l0f")


def emit_l3f(b, env):
    return emit_ffn(b, env, "l3f")




def _emit_fp6_quant_chain(b, blk_ops, W, G, src_view, rscl_bc, scl_bc,
                          t0, t1, t2, t3, dst):
    v3 = lambda ap: ap.rearrange("p (g k) -> p g k", k=32)

    def ops(v):
        with b.nc.allow_low_precision("fp6 quant chain"):
            v.tensor_tensor(v3(t0), v3(src_view), rscl_bc, ALU.mult)
            v.drain()
            v.tensor_tensor(t1, t0, t0, ALU.mult)
            v.drain()
            v.tensor_single_scalar(t2, t1, 16.0, ALU.is_ge)
            v.drain()
            v.tensor_single_scalar(t1, t1, 4.0, ALU.is_ge)
            v.drain()
            v.tensor_scalar(t3, t2, -2.0, 8.0, ALU.mult, ALU.add)
            v.drain()
            v.scalar_tensor_tensor(t3, t1, -4.0, t3, ALU.mult, ALU.add)
            v.drain()
            v.tensor_scalar(t2, t2, 0.25, 0.125, ALU.mult, ALU.add)
            v.drain()
            v.scalar_tensor_tensor(t2, t1, 0.125, t2, ALU.mult, ALU.add)
            v.drain()
            v.tensor_tensor(t0, t0, t3, ALU.mult)
            v.drain()
            v.tensor_scalar(t0, t0, -1.0, 0.5, ALU.mult, ALU.add)
            v.drain()
            v.tensor_single_scalar(t3, t0, 12582912.0, ALU.add)
            v.drain()
            v.tensor_single_scalar(t3, t3, 12582912.0, ALU.subtract)
            v.drain()
            v.tensor_tensor(t1, t3, t0, ALU.is_gt)
            v.drain()
            v.tensor_tensor(t3, t3, t1, ALU.subtract)
            v.drain()
            v.scalar_tensor_tensor(t0, t3, -1.0, t2, ALU.mult, ALU.mult)
            v.drain()
            v.tensor_tensor(v3(t0), v3(t0), scl_bc, ALU.mult)
            if dst is not None:
                v.drain()
                v.tensor_copy(dst, t0)
    blk_ops.append(ops)


def emit_proj_fm_sb(b, lhsT_tiles, n_out, rhs_tiles, evac, evac_eng="scalar"):
    nc = b.nc
    mm0, ev0 = b.c_mm.n, b.c_ev.n
    with nc.Block() as blk:
        @blk.tensor
        def _(t):
            for j in range(n_out // 128):
                tgt = b.PA if j % 2 == 0 else b.PB
                if j >= 2:
                    t.wait_ge(b.c_ev.h, ev0 + j - 1)
                last = None
                for tb in range(NB):
                    for kc in range(KC):
                        last = t.matmul(
                            tgt[:, tb * 512:(tb + 1) * 512],
                            lhsT_tiles(kc, j),
                            rhs_tiles[kc][:, tb * 512:(tb + 1) * 512],
                            start=(kc == 0), stop=(kc == KC - 1))
                b.c_mm.inc(last)

        def evs(e):
            for j in range(n_out // 128):
                tgt = b.PA if j % 2 == 0 else b.PB
                e.wait_ge(b.c_mm.h, mm0 + j + 1)
                last = evac(e, j, tgt)
                b.c_ev.inc(last)
        if evac_eng == "scalar":
            @blk.scalar
            def _(e):
                evs(e)
        else:
            @blk.vector
            def _(e):
                evs(e)


def emit_l1f(b, env):
    nc = b.nc
    A, Bp, C, Dp = env["pA"], env["pB"], env["pC"], env["pD"]
    ident = env["ident"]
    yc = env["yc"]
    w_hbm = b.inhbm("l1f_w")
    bias = env["gb"]["l1f_bias"]

    tmpt = b.sb([128, 2048], F32, SCRW, "l1f_tmpt")
    inv_row = b.sb([1, 2048], BF16, SCRW, "l1f_invloc")
    amx = b.sb([128, 64], F32, SCRW + 8192, "l1f_amx")
    scl = b.sb([128, 64], F32, SCRW + 8448, "l1f_scl")
    rscl = b.sb([128, 64], F32, SCRW + 8704, "l1f_rscl")
    t0 = b.sb([128, 2048], F32, OFF_D, "l1f_t0")
    t1 = b.sb([128, 2048], F32, OFF_D + 8192, "l1f_t1")
    t2 = b.sb([128, 2048], F32, OFF_D + 16384, "l1f_t2")
    t3 = b.sb([128, 2048], F32, OFF_D + 24576, "l1f_t3")

    for c in range(KC):
        with nc.Block() as blk:
            @blk.vector
            def _(v, c=c):
                v.transpose(tmpt[:], yc[c][:])
        with nc.Block() as blk:
            @blk.vector
            def _(v):
                with nc.allow_low_precision("fp6 absmax"):
                    v.tensor_reduce(amx[:], tmpt[:].rearrange("p (g k) -> p g k", k=32),
                                    AX.X, ALU.max, apply_absolute_value=True)
                v.drain()
                v.tensor_scalar(scl[:], amx[:], 1.0 / 7.5, 1e-30, ALU.mult, ALU.max)
                v.drain()
                v.reciprocal(rscl[:], scl[:])
        blk_ops = []
        sbc = scl[:].unsqueeze(-1).broadcast_to((128, 64, 32))
        rbc = rscl[:].unsqueeze(-1).broadcast_to((128, 64, 32))
        _emit_fp6_quant_chain(b, blk_ops, 2048, 64, tmpt[:], rbc, sbc,
                              t0[:], t1[:], t2[:], t3[:], None)
        with nc.Block() as blk:
            @blk.vector
            def _(v, op=blk_ops[0]):
                op(v)
        with nc.Block() as blk:
            @blk.vector
            def _(v, c=c):
                v.transpose(tmpt[:], t0[:])
                v.drain()
                v.tensor_copy(C[c][:], tmpt[:])

    wf = [b.sb([128, 1024], F32, OFF_A + c * 4096, f"l1f_wf{c}") for c in range(KC)]
    with nc.Block() as blk:
        @blk.sync
        def _(sy):
            for c in range(KC):
                b.ld(sy, "h", wf[c][:], w_hbm[c * 128:(c + 1) * 128, :])
            b.wq(sy, "h")
    for c in range(KC):
        with nc.Block() as blk:
            @blk.vector
            def _(v, c=c):
                with nc.allow_low_precision("fp6 absmax"):
                    v.tensor_reduce(amx[:, 0:32],
                                    wf[c][:].rearrange("p (g k) -> p g k", k=32),
                                    AX.X, ALU.max, apply_absolute_value=True)
                v.drain()
                v.tensor_scalar(scl[:, 0:32], amx[:, 0:32], 1.0 / 7.5, 1e-30,
                                ALU.mult, ALU.max)
                v.drain()
                v.reciprocal(rscl[:, 0:32], scl[:, 0:32])
        blk_ops = []
        sbc = scl[:, 0:32].unsqueeze(-1).broadcast_to((128, 32, 32))
        rbc = rscl[:, 0:32].unsqueeze(-1).broadcast_to((128, 32, 32))
        _emit_fp6_quant_chain(b, blk_ops, 1024, 32, wf[c][:], rbc, sbc,
                              t0[:, 0:1024], t1[:, 0:1024], t2[:, 0:1024],
                              t3[:, 0:1024], None)
        with nc.Block() as blk:
            @blk.vector
            def _(v, op=blk_ops[0]):
                op(v)
        with nc.Block() as blk:
            @blk.vector
            def _(v, c=c):
                v.tensor_copy(Bp[c][:, 0:1024], t0[:, 0:1024])

    pabf = b.PA[:].bitcast(BF16)
    mm0, ev0 = b.c_mm.n, b.c_ev.n
    with nc.Block() as blk:
        @blk.tensor
        def _(t):
            i = 0
            for oc in range(KC):
                for dc in range(KC):
                    if i >= 4:
                        t.wait_ge(b.c_ev.h, ev0 + i - 3)
                    last = t.transpose(pabf[:, (i % 4) * 1024:(i % 4) * 1024 + 128],
                                       Bp[oc][:, dc * 128:(dc + 1) * 128], ident[:])
                    b.c_mm.inc(last)
                    i += 1
        @blk.vector
        def _(v):
            i = 0
            for oc in range(KC):
                for dc in range(KC):
                    v.wait_ge(b.c_mm.h, mm0 + i + 1)
                    last = v.tensor_copy(Bp[dc][:, 1024 + oc * 128:1024 + (oc + 1) * 128],
                                         pabf[:, (i % 4) * 1024:(i % 4) * 1024 + 128])
                    b.c_ev.inc(last)
                    i += 1

    invb = b.sb([128, 2048], BF16, OFF_D, "l1f_invb")
    with nc.Block() as blk:
        @blk.sync
        def _(sy):
            b.ld(sy, "h", inv_row[:], env["l1f_inv_hbm"])
            b.wq(sy, "h")
    with nc.Block() as blk:
        @blk.tensor
        def _(t):
            for nb in range(NB):
                t.matmul(b.PB[:, nb * 512:(nb + 1) * 512], env["ones_row"][:],
                         inv_row[:, nb * 512:(nb + 1) * 512], start=True, stop=True)
    with nc.Block() as blk:
        @blk.scalar
        def _(e):
            e.activation(invb[:], b.PB[:], AF.Copy)

    def ev(e, j, ps):
        e.tensor_tensor(A[j][:], ps[:], invb[:], ALU.mult)
        e.drain()
        return e.tensor_scalar(A[j][:], A[j][:], bias[j], None, ALU.add)
    emit_proj_fm_sb(b, lambda kc, j: Bp[kc][:, 1024 + j * 128:1024 + (j + 1) * 128],
                    1024, C, ev, evac_eng="vector")
    return A, C, OFF_C





def _attn_schedule(causal):
    evs = []
    pairs_of_bq = []
    gp = 0
    for bq in range(NB):
        ntk = 4 * bq + 4 if causal else 16
        np_ = ntk // 2
        sec = []
        for p in range(np_):
            g = gp + p
            sec.append(("sp", g, bq, p))
            if p >= 1:
                sec.append(("av", g - 1, bq, p - 1))
        sec.append(("av", gp + np_ - 1, bq, np_ - 1))
        evs.append(sec)
        pairs_of_bq.append((gp, np_))
        gp += np_
    return evs, pairs_of_bq, gp


def emit_attention(b, env, y_pool, q_pool, k_pool, ctx_pool, wqkv_hbm, scale,
                   causal, tag):
    nc = b.nc
    wbuf = env["wbuf"]
    vt = b.sb([128, 16640], BF16, OFF_D, f"vt_{tag}")
    tri = env["tri"]
    orow = env["ones_row"]
    ones64 = env["ones64"]

    emit_proj_fm(b, wqkv_hbm, 1024, y_pool,
                 lambda e, j, ps: e.activation(q_pool[j][:], ps[:], AF.Copy,
                                               scale=scale), wbuf, col_off=0)
    emit_proj_fm(b, wqkv_hbm, 1024, y_pool,
                 lambda e, j, ps: e.activation(k_pool[j][:], ps[:], AF.Copy),
                 wbuf, col_off=1024)
    with nc.Block() as blk:
        @blk.vector
        def _(v):
            ones_view = vt[:].rearrange("p (t h j) -> p t h j", h=16, j=65)
            v.memset(ones_view[:, :, :, 64:65], 1.0)

    def ev_v(e, tt, u, ps):
        dst = vt[:, tt * 1040 + 4 * u * 65: tt * 1040 + 4 * u * 65 + 260]
        dst = dst.rearrange("p (h j) -> p h j", j=65)[:, :, 0:64]
        return e.activation(dst, ps[:].rearrange("p (h j) -> p h j", j=64), AF.Copy)
    emit_proj_tm(b, wqkv_hbm, 1024, y_pool, ev_v, wbuf, col_off=2048)

    a2 = [b.sb([128, 1024], BF16, SCRW, f"a2a_{tag}"),
          b.sb([128, 1024], BF16, SCRW + 2048, f"a2b_{tag}")]
    ctxs = [b.sb([65, 2048], BF16, SCRW + 4096, f"cx0_{tag}"),
            b.sb([65, 2048], BF16, SCRW + 8192, f"cx1_{tag}")]

    evs, pairs_of_bq, npairs = _attn_schedule(causal)
    diag_pairs = []
    if causal:
        for bq in range(NB):
            gp, np_ = pairs_of_bq[bq]
            diag_pairs += [gp + np_ - 2, gp + np_ - 1]

    prev_tt2 = None
    prev_ctt = [None, None]
    for h in range(16):
        pair, half = h // 2, (h % 2) * 64
        mm0, ev0, dv0 = b.c_mm.n, b.c_ev.n, b.c_dv.n
        sp_ms, av_ms = {}, {}
        mmn = mm0
        for sec in evs:
            for e_ in sec:
                mmn += 1
                (sp_ms if e_[0] == "sp" else av_ms)[e_[1]] = mmn
        bc_ms = [mmn + 1, mmn + 2]
        ex_ms = {g: ev0 + g + 1 for g in range(npairs)}
        evac_ms = ev0 + npairs + 1
        mk_ms = {g: dv0 + i + 1 for i, g in enumerate(diag_pairs)}
        recip_ms = dv0 + len(diag_pairs) + 1
        tt_ms = [recip_ms + 1, recip_ms + 2]
        last_av = max(av_ms.values())

        with nc.Block() as blk:
            @blk.tensor
            def _(t, h=h, pair=pair, half=half, prev_tt2=prev_tt2):
                first_av = True
                for sec in evs:
                    for e_ in sec:
                        kind, g, bq, p = e_
                        if kind == "sp":
                            if g >= 1:
                                t.wait_ge(b.c_ev.h, ex_ms[g - 1])
                            i = None
                            for tki in (2 * p, 2 * p + 1):
                                i = t.matmul(
                                    b.PB[:, (tki % 2) * 512:(tki % 2) * 512 + 512],
                                    k_pool[pair][half:half + 64,
                                                 tki * 128:(tki + 1) * 128],
                                    q_pool[pair][half:half + 64,
                                                 bq * 512:(bq + 1) * 512],
                                    start=True, stop=True)
                            b.c_mm.inc(i)
                            assert b.c_mm.n == sp_ms[g]
                        else:
                            t.wait_ge(b.c_ev.h, ex_ms[g])
                            if g in mk_ms:
                                t.wait_ge(b.c_dv.h, mk_ms[g])
                            if first_av and prev_tt2 is not None:
                                t.wait_ge(b.c_dv.h, prev_tt2)
                                first_av = False
                            elif first_av:
                                first_av = False
                            gp0, np_ = pairs_of_bq[bq]
                            last_of_bq = (p == np_ - 1)
                            i = None
                            for n_, tki in enumerate((2 * p, 2 * p + 1)):
                                lo = 128 * (tki - 4 * bq) if (causal and tki >= 4 * bq) else 0
                                at = a2[g % 2][:, (tki % 2) * 512 + lo:
                                               (tki % 2) * 512 + 512]
                                i = t.matmul(
                                    b.PA[0:65, bq * 512 + lo: bq * 512 + 512],
                                    vt[:, tki * 1040 + h * 65:
                                       tki * 1040 + h * 65 + 65],
                                    at,
                                    start=(tki == 0),
                                    stop=(last_of_bq and n_ == 1))
                            b.c_mm.inc(i)
                            assert b.c_mm.n == av_ms[g]
                for hf in range(2):
                    t.wait_ge(b.c_dv.h, recip_ms if hf == 0 else tt_ms[0])
                    i = None
                    for nb in range(2):
                        cc = hf * 1024 + nb * 512
                        i = t.matmul(b.PA[0:64, nb * 512:(nb + 1) * 512],
                                     ones64[64:65, 0:64],
                                     ctxs[h % 2][64:65, cc:cc + 512],
                                     start=True, stop=True)
                    b.c_mm.inc(i)
                    assert b.c_mm.n == bc_ms[hf]

            @blk.scalar
            def _(e, h=h):
                for g in range(npairs):
                    e.wait_ge(b.c_mm.h, sp_ms[g])
                    if g >= 2:
                        e.wait_ge(b.c_mm.h, av_ms[g - 2])
                    i = e.activation(a2[g % 2][:], b.PB[:, 0:1024],
                                     AF.Exp, bias=-20.0)
                    b.c_ev.inc(i)
                    assert b.c_ev.n == ex_ms[g]
                e.wait_ge(b.c_mm.h, last_av)
                if prev_ctt[h % 2] is not None:
                    e.wait_ge(b.c_dv.h, prev_ctt[h % 2])
                i = e.activation(ctxs[h % 2][:], b.PA[0:65, 0:2048], AF.Copy)
                b.c_ev.inc(i)
                assert b.c_ev.n == evac_ms

            @blk.vector
            def _(v, h=h, pair=pair, half=half):
                for g in diag_pairs:
                    v.wait_ge(b.c_ev.h, ex_ms[g])
                    bq = next(bq_ for bq_ in range(NB)
                              if pairs_of_bq[bq_][0] <= g < pairs_of_bq[bq_][0]
                              + pairs_of_bq[bq_][1])
                    p = g - pairs_of_bq[bq][0]
                    i = None
                    for tki in (2 * p, 2 * p + 1):
                        off = 128 * (tki - 4 * bq)
                        if 0 <= off < 512:
                            ap_ = a2[g % 2][:, (tki % 2) * 512 + off:
                                            (tki % 2) * 512 + off + 128]
                            i = v.tensor_tensor(ap_, ap_, tri[:], ALU.mult)
                            v.drain()
                    b.c_dv.inc(i)
                    assert b.c_dv.n == mk_ms[g]
                v.wait_ge(b.c_ev.h, evac_ms)
                with nc.allow_low_precision("attn rden"):
                    i = v.reciprocal(ctxs[h % 2][64:65, :], ctxs[h % 2][64:65, :])
                b.c_dv.inc(i)
                assert b.c_dv.n == recip_ms
                for hf in range(2):
                    v.wait_ge(b.c_mm.h, bc_ms[hf])
                    i = v.tensor_tensor(
                        ctx_pool[pair][half:half + 64,
                                       hf * 1024:(hf + 1) * 1024],
                        ctxs[h % 2][0:64, hf * 1024:(hf + 1) * 1024],
                        b.PA[0:64, 0:1024], ALU.mult)
                    v.drain()
                    b.c_dv.inc(i)
                    assert b.c_dv.n == tt_ms[hf]
        prev_tt2 = tt_ms[1]
        prev_ctt[h % 2] = tt_ms[1]
    return vt


def emit_l1a(b, env):
    nc = b.nc
    A, Bp, C, Dp = env["pA"], env["pB"], env["pC"], env["pD"]
    wqkv = b.inhbm("l1a_wqkv")
    wo = b.inhbm("l1a_wo")
    emit_attention(b, env, A, Bp, C, A, wqkv, 0.125, True, "l1a")
    emit_proj_fm(b, wo, 1024, A,
                 lambda e, j, ps: e.activation(Bp[j][:], ps[:], AF.Copy),
                 env["wbuf"])
    return Bp, C, OFF_C




def emit_proj_fm_store(b, w_hbm, n_out, rhs_tiles, act, stage, dst_hbm, wbuf,
                       col_off=0, bias_cols=None):
    nc = b.nc
    n_units = n_out // 256
    mm0, ev0 = b.c_mm.n, b.c_ev.n
    wt = []
    st_entry = [b.st[0].n, b.st[1].n]
    stt = [st_entry[j % 2] + 16 * (j // 2 + 1) for j in range(2 * n_units)]
    with nc.Block() as blk:
        @blk.gpsimd
        def _(g):
            for u in range(n_units):
                if u >= 2:
                    g.wait_ge(b.c_mm.h, mm0 + 2 * (u - 1))
                src = w_hbm[:, col_off + u * 256: col_off + (u + 1) * 256]
                src = src.rearrange("(kc kp) m -> kp kc m", kp=128)
                dst = wbuf[u % 2][:].rearrange("p (kc m) -> p kc m", m=256)
                ws = b.ws[u % 2]
                g.dma_start(dst, src).then_inc(ws.h, 16)
                ws.n += 16
                wt.append(ws.n)
        @blk.tensor
        def _(t):
            for u in range(n_units):
                t.wait_ge(b.ws[u % 2].h, wt[u])
                for mo in range(2):
                    j = u * 2 + mo
                    tgt = b.PA if j % 2 == 0 else b.PB
                    if j >= 2:
                        t.wait_ge(b.c_ev.h, ev0 + j - 1)
                    last = None
                    for tb in range(NB):
                        for kc in range(KC):
                            last = t.matmul(
                                tgt[:, tb * 512:(tb + 1) * 512],
                                wbuf[u % 2][:, kc * 256 + mo * 128: kc * 256 + (mo + 1) * 128],
                                rhs_tiles[kc][:, tb * 512:(tb + 1) * 512],
                                start=(kc == 0), stop=(kc == KC - 1))
                    b.c_mm.inc(last)
        @blk.scalar
        def _(e):
            for j in range(2 * n_units):
                tgt = b.PA if j % 2 == 0 else b.PB
                e.wait_ge(b.c_mm.h, mm0 + j + 1)
                e.wait_ge(b.st[j % 2].h,
                          stt[j - 2] if j >= 2 else st_entry[j % 2])
                if bias_cols is not None:
                    last = e.activation(stage[j % 2][:], tgt[:], act,
                                        bias=bias_cols[j])
                else:
                    last = e.activation(stage[j % 2][:], tgt[:], act)
                b.c_ev.inc(last)
        @blk.sync
        def _(sy):
            for j in range(2 * n_units):
                sy.wait_ge(b.c_ev.h, ev0 + j + 1)
                st = b.st[j % 2]
                sy.dma_start(dst_hbm[j * 128:(j + 1) * 128, :],
                             stage[j % 2][:]).then_inc(st.h, 16)
                st.n += 16
                assert st.n == stt[j]
            sy.wait_ge(b.st[0].h, b.st[0].n)
            sy.wait_ge(b.st[1].h, b.st[1].n)


def emit_l2f(b, env):
    nc = b.nc
    A, Bp, C, Dp = env["pA"], env["pB"], env["pC"], env["pD"]
    wbuf = env["wbuf"]
    X = env["X"]
    ones_col, ones_row, ident = env["ones_col"], env["ones_row"], env["ident"]
    wqkv = b.inhbm("l2f_wqkv")
    wproj = b.inhbm("l2f_wproj")

    emit_attention(b, env, A, Bp, C, A, wqkv, 0.125, False, "l2f")
    emit_proj_fm(b, wproj, 1024, A,
                 lambda e, j, ps: e.activation(Bp[j][:], ps[:], AF.Copy), wbuf)

    with nc.Block() as blk:
        @blk.scalar
        def _(e):
            for c in range(KC):
                e.activation(C[c][:], X[c][:], AF.Square)
    with nc.Block() as blk:
        @blk.vector
        def _(v):
            for c in range(KC):
                v.tensor_copy(A[c][:], X[c][:])
    a_row, b_row = scratch_rows(b, "l2fx", 2)
    emit_stats(b, A, lambda kc: ones_col[:], 1)
    emit_stats(b, C, lambda kc: ones_col[:], 1, dstA=True)
    sc_row = b.sb([1, 2048], F32, OFF_C, "l2f_scrow")
    emit_rowmath_plain(b, b.PB[0:1, 0:2048], b.PA[0:1, 0:2048], sc_row, a_row, b_row)
    emit_bcast2(b, a_row, b_row, env)
    tmp_a = b.sb([128, 2048], F32, OFF_C, "l2f_tma")
    tmp_b = b.sb([128, 2048], F32, OFF_C + 8192, "l2f_tmb")
    with nc.Block() as blk:
        @blk.vector
        def _(v):
            for c in range(KC):
                tmp = tmp_a if c % 2 == 0 else tmp_b
                v.tensor_tensor(tmp[:], X[c][:], b.PA[:], ALU.mult)
                v.drain()
                v.tensor_tensor(tmp[:], tmp[:], b.PB[:], ALU.add)
                v.drain()
                v.tensor_tensor(Bp[c][:], Bp[c][:], tmp[:], ALU.add)
                v.drain()

    emit_ln_stats_rows(b, env, Bp, C, OFF_C, "l2f_ln2")
    tmp_a2 = b.sb([128, 2048], F32, OFF_A, "l2f_tma2")
    tmp_b2 = b.sb([128, 2048], F32, OFF_A + 8192, "l2f_tmb2")
    emit_affine_apply(b, Bp, C, tmp_a2, tmp_b2,
                      gb=(env["gb"]["l2f_g2"], env["gb"]["l2f_b2"]))

    gw = b.sb([128, 64], BF16, OFF_D, "l2f_gw")
    gb_col = b.sb([8, 1], F32, OFF_D + 128, "l2f_gbcol")
    lg = b.sb([8, 2048], F32, OFF_D + 1024, "l2f_lg")
    ltm = b.sb([128, 128], F32, OFF_D + 9216, "l2f_ltm")
    mxt = b.sb([128, 16], F32, OFF_D + 9728, "l2f_mx")
    eqt = b.sb([128, 128], F32, OFF_D + 9856, "l2f_eq")
    idxc = b.sb([128, 16], F32, OFF_D + 10368, "l2f_idxc")
    idxrow = b.sb([1, 2048], BF16, SCRW + 8192, "l2f_idxrow")
    mbs = b.sb([128, 2048], mybir.dt.uint8, OFF_A + 4096, "l2f_mbs")
    iota8 = env["iota8"]
    with nc.Block() as blk:
        @blk.gpsimd
        def _(g):
            b.ld(g, "g", gw[:].rearrange("p (c e) -> p c e", e=8),
                 b.inhbm("l2f_gatew").rearrange("(c p) e -> p c e", p=128),
                 allow_slow_non_contiguous=True)
            b.ld(g, "g", gb_col[:], b.inhbm("l2f_gateb_row").rearrange(
                "one e -> e one"), allow_slow_non_contiguous=True)
            b.wq(g, "g")
    with nc.Block() as blk:
        @blk.tensor
        def _(t):
            for nb in range(NB):
                for kc in range(KC):
                    t.matmul(b.PB[0:8, nb * 512:(nb + 1) * 512],
                             gw[:, kc * 8:(kc + 1) * 8],
                             C[kc][:, nb * 512:(nb + 1) * 512],
                             start=(kc == 0), stop=(kc == KC - 1))
    with nc.Block() as blk:
        @blk.scalar
        def _(e):
            e.activation(lg[:], b.PB[0:8, 0:2048], AF.Identity, bias=gb_col[:])
    mm0, ev0 = b.c_mm.n, b.c_ev.n
    with nc.Block() as blk:
        @blk.tensor
        def _(t):
            for tt in range(16):
                if tt >= 4:
                    t.wait_ge(b.c_ev.h, ev0 + tt - 3)
                i = t.transpose(b.PA[:, (tt % 4) * 512:(tt % 4) * 512 + 8],
                                lg[:, tt * 128:(tt + 1) * 128],
                                env["identf"][0:8, 0:8])
                b.c_mm.inc(i)
        @blk.vector
        def _(v):
            for tt in range(16):
                v.wait_ge(b.c_mm.h, mm0 + tt + 1)
                i = v.tensor_copy(ltm[:, tt * 8:(tt + 1) * 8],
                                  b.PA[:, (tt % 4) * 512:(tt % 4) * 512 + 8])
                b.c_ev.inc(i)
    with nc.Block() as blk:
        @blk.vector
        def _(v):
            v.tensor_reduce(mxt[:], ltm[:].rearrange("p (t e) -> p t e", e=8),
                            AX.X, ALU.max)
            v.drain()
            v.tensor_tensor(eqt[:].rearrange("p (t e) -> p t e", e=8),
                            ltm[:].rearrange("p (t e) -> p t e", e=8),
                            mxt[:].unsqueeze(-1).broadcast_to((128, 16, 8)),
                            ALU.is_ge)
            v.drain()
            v.scalar_tensor_tensor(eqt[:].rearrange("p (t e) -> p t e", e=8),
                                   eqt[:].rearrange("p (t e) -> p t e", e=8),
                                   -99.0,
                                   iota8[:].unsqueeze(1).broadcast_to((128, 16, 8)),
                                   ALU.mult, ALU.add)
            v.drain()
            v.tensor_single_scalar(eqt[:], eqt[:], 99.0, ALU.add)
            v.drain()
            v.tensor_reduce(idxc[:], eqt[:].rearrange("p (t e) -> p t e", e=8),
                            AX.X, ALU.min)
    idx_hbm = nc.dram_tensor("l2f_idxs", [128, 16], F32).ap()
    with nc.Block() as blk:
        @blk.sync
        def _(sy):
            b.ld(sy, "h", idx_hbm, idxc[:])
            b.wq(sy, "h")
    with nc.Block() as blk:
        @blk.gpsimd
        def _(g):
            b.ld(g, "g", idxrow[:].rearrange("one (tt p) -> one tt p", p=128),
                 idx_hbm.rearrange("p tt -> tt p").unsqueeze(0),
                 allow_slow_non_contiguous=True)
            b.wq(g, "g")

    stage = [b.sb([128, 2048], BF16, OFF_A, "l2f_st0"),
             b.sb([128, 2048], BF16, OFF_A + 12288, "l2f_st1")]
    h_hbm = []
    for e in range(8):
        h_hbm.append(nc.dram_tensor(f"l2f_h{e}", [4096, 2048], BF16).ap())
    for e in range(8):
        emit_proj_fm_store(b, b.inhbm(f"l2f_w1_{e}"), 4096, C,
                           AF.Gelu_apprx_tanh, stage, h_hbm[e], wbuf)

    w2s = b.sb([128, 16384], BF16, OFF_D, "l2f_w2s")
    htile = [b.sb([128, 1024], BF16, OFF_A, "l2f_h0t"),
             b.sb([128, 1024], BF16, OFF_A + 2048, "l2f_h1t")]
    for e in range(8):
        with nc.Block() as blk:
            @blk.tensor
            def _(t):
                for nb in range(NB):
                    t.matmul(b.PB[:, nb * 512:(nb + 1) * 512], ones_row[:],
                             idxrow[:, nb * 512:(nb + 1) * 512],
                             start=True, stop=True)
        with nc.Block() as blk:
            @blk.vector
            def _(v, e=e):
                v.tensor_single_scalar(mbs[:], b.PB[:], float(e), ALU.is_equal)
        for mh in range(2):
            with nc.Block() as blk:
                @blk.gpsimd
                def _(g, e=e, mh=mh):
                    src = b.inhbm(f"l2f_w2_{e}")[:, mh * 512:(mh + 1) * 512]
                    src = src.rearrange("(kc kp) m -> kp kc m", kp=128)
                    b.ld(g, "g", w2s[:].rearrange("p (kc m) -> p kc m", m=512), src)
                    b.wq(g, "g")
            for tb in range(2):
                mm0b, ev0b = b.c_mm.n, b.c_ev.n
                wtl = []
                with nc.Block() as blk:
                    @blk.sync
                    def _(sy, e=e, tb=tb):
                        for kc in range(32):
                            if kc >= 2:
                                sy.wait_ge(b.c_mm.h, mm0b + kc - 1)
                            hp = b.hp[kc % 2]
                            sy.dma_start(
                                htile[kc % 2][:],
                                h_hbm[e][kc * 128:(kc + 1) * 128,
                                         tb * 1024:(tb + 1) * 1024]
                            ).then_inc(hp.h, 16)
                            hp.n += 16
                            wtl.append(hp.n)
                    @blk.tensor
                    def _(t, mh=mh):
                        for kc in range(32):
                            t.wait_ge(b.hp[kc % 2].h, wtl[kc])
                            last = None
                            for m in range(4):
                                tgt = b.PA if m < 2 else b.PB
                                off = (m % 2) * 1024
                                for nn in range(2):
                                    last = t.matmul(
                                        tgt[:, off + nn * 512: off + (nn + 1) * 512],
                                        w2s[:, kc * 512 + m * 128: kc * 512 + (m + 1) * 128],
                                        htile[kc % 2][:, nn * 512:(nn + 1) * 512],
                                        start=(kc == 0), stop=(kc == 31))
                            b.c_mm.inc(last)
                with nc.Block() as blk:
                    @blk.vector
                    def _(v, e=e, mh=mh, tb=tb):
                        for m in range(4):
                            tgt = b.PA if m < 2 else b.PB
                            off = (m % 2) * 1024
                            oc = mh * 4 + m
                            v.copy_predicated(
                                C[oc][:, tb * 1024:(tb + 1) * 1024],
                                mbs[:, tb * 1024:(tb + 1) * 1024],
                                tgt[:, off:off + 1024])
                            v.drain()

    with nc.Block() as blk:
        @blk.vector
        def _(v):
            for c in range(KC):
                v.tensor_tensor(C[c][:], C[c][:], Bp[c][:], ALU.add)
    return C, Bp, OFF_B






def emit_l3a(b, env):
    nc = b.nc
    A, Bp, C, Dp = env["pA"], env["pB"], env["pC"], env["pD"]
    wbuf = env["wbuf"]
    orow = env["ones_row"]

    q_hbm = nc.dram_tensor("l3a_qs", [4096, 2048], BF16).ap()
    k_hbm = nc.dram_tensor("l3a_ks", [4096, 2048], BF16).ap()
    v_hbm = nc.dram_tensor("l3a_vs", [4096, 2048], BF16).ap()
    o_hbm = nc.dram_tensor("l3a_os", [4096, 2048], BF16).ap()
    wo_bf = nc.dram_tensor("l3a_wobf", [4096, 1024], BF16).ap()

    ph = b.sb([128, 32], F32, SCRW + 8192, "l3a_ph")
    dif = b.sb([128, 128], F32, SCRW + 8320, "l3a_dif")
    wcol = b.sb([128, 128], BF16, SCRW + 8832, "l3a_wcol")
    with nc.Block() as blk:
        @blk.sync
        def _(sy):
            b.ld(sy, "h", ph[:].rearrange("p (h c) -> p h c", c=8),
                 b.inhbm("l3a_phase").rearrange("h (c p) -> p h c", p=128),
                 allow_slow_non_contiguous=True)
            b.wq(sy, "h")
    with nc.Block() as blk:
        @blk.vector
        def _(v):
            for hh in range(4):
                for gg in range(4):
                    p = hh * 4 + gg
                    v.tensor_tensor(dif[:, p * 8:(p + 1) * 8],
                                    ph[:, hh * 8:(hh + 1) * 8],
                                    ph[:, gg * 8:(gg + 1) * 8], ALU.subtract)
    with nc.Block() as blk:
        @blk.scalar
        def _(e):
            e.activation(dif[:], dif[:], AF.Sin, bias=float(np.pi / 2))
    with nc.Block() as blk:
        @blk.vector
        def _(v):
            v.tensor_scalar_mul(wcol[:], dif[:], 1.0 / 32.0)

    stage = [b.sb([128, 2048], BF16, OFF_B, "l3a_st0"),
             b.sb([128, 2048], BF16, OFF_B + 4096, "l3a_st1")]
    for w_name, bias_name, dst in (("l3a_wq", "l3a_bq", q_hbm),
                                   ("l3a_wk", "l3a_bk", k_hbm),
                                   ("l3a_wv", "l3a_bv", v_hbm)):
        emit_proj_fm_store(b, b.inhbm(w_name), 4096, A, AF.Identity, stage,
                           dst, wbuf, bias_cols=env["gb"][bias_name])
    with nc.Block() as blk:
        @blk.gpsimd
        def _(g):
            b.ld(g, "g", wo_bf, b.inhbm("l3a_wo")[:])
            b.wq(g, "g")

    u8 = [b.sb([128, 2048], BF16, OFF_D, "l3a_u8a"),
          b.sb([128, 2048], BF16, OFF_D + 4096, "l3a_u8b")]
    expr = [b.sb([1, 256], BF16, OFF_D + 8192 + i * 512, f"l3a_ex{i}")
            for i in range(16)]
    denr = [b.sb([1, 256], F32, OFF_D + 16384 + i * 1024, f"l3a_den{i}")
            for i in range(4)]
    rdenr = [b.sb([1, 256], BF16, OFF_D + 20480 + i * 512, f"l3a_rd{i}")
             for i in range(4)]
    abs_t = [b.sb([128, 256], BF16, OFF_D + 22528, "l3a_ab0"),
             b.sb([128, 256], BF16, OFF_D + 23040, "l3a_ab1")]
    tmpc = b.sb([128, 256], BF16, OFF_D + 23552, "l3a_tmpc")

    def sub(base_pool, base, fc, tw=256):
        per = 2048 // tw
        return base_pool[base + fc // per][:, (fc % per) * tw:(fc % per + 1) * tw]

    for tc in range(8):
        t0c = tc * 256
        with nc.Block() as blk:
            @blk.sync
            def _(sy, t0c=t0c):
                for i in range(4):
                    b.ld(sy, "h",
                         Bp[i][:].rearrange("p (s t) -> p s t", t=256),
                         q_hbm[i * 1024:(i + 1) * 1024, t0c:t0c + 256]
                         .rearrange("(s p) t -> p s t", p=128))
                    b.ld(sy, "h",
                         Bp[4 + i][:].rearrange("p (s t) -> p s t", t=256),
                         k_hbm[i * 1024:(i + 1) * 1024, t0c:t0c + 256]
                         .rearrange("(s p) t -> p s t", p=128))
                    b.ld(sy, "h",
                         C[i][:].rearrange("p (s t) -> p s t", t=256),
                         v_hbm[i * 1024:(i + 1) * 1024, t0c:t0c + 256]
                         .rearrange("(s p) t -> p s t", p=128))
                b.wq(sy, "h")
        mm0, ev0, dv0 = b.c_mm.n, b.c_ev.n, b.c_dv.n
        with nc.Block() as blk:
            @blk.vector
            def _(v):
                for p in range(16):
                    if p >= 2:
                        v.wait_ge(b.c_mm.h, mm0 + p - 1)
                    hh, gg = p // 4, p % 4
                    last = None
                    for c in range(KC):
                        last = v.tensor_tensor(
                            u8[p % 2][:, c * 256:(c + 1) * 256],
                            sub(Bp, 0, hh * 8 + c),
                            sub(Bp, 4, gg * 8 + c), ALU.mult)
                    b.c_dv.inc(last)
            @blk.tensor
            def _(t):
                for p in range(16):
                    t.wait_ge(b.c_dv.h, dv0 + p + 1)
                    if p >= 2:
                        t.wait_ge(b.c_ev.h, ev0 + p - 1)
                    last = None
                    for c in range(KC):
                        last = t.matmul(
                            b.PB[0:1, (p % 2) * 512:(p % 2) * 512 + 256],
                            wcol[:, p * 8 + c: p * 8 + c + 1],
                            u8[p % 2][:, c * 256:(c + 1) * 256],
                            start=(c == 0), stop=(c == KC - 1))
                    b.c_mm.inc(last)
            @blk.scalar
            def _(e):
                for p in range(16):
                    e.wait_ge(b.c_mm.h, mm0 + p + 1)
                    last = e.activation(expr[p][:],
                                        b.PB[0:1, (p % 2) * 512:(p % 2) * 512 + 256],
                                        AF.Exp, bias=-20.0)
                    b.c_ev.inc(last)
        with nc.Block() as blk:
            @blk.vector
            def _(v):
                for hh in range(4):
                    v.tensor_tensor(denr[hh][:], expr[4 * hh][:],
                                    expr[4 * hh + 1][:], ALU.add)
                    v.drain()
                    v.tensor_tensor(denr[hh][:], denr[hh][:],
                                    expr[4 * hh + 2][:], ALU.add)
                    v.drain()
                    v.tensor_tensor(denr[hh][:], denr[hh][:],
                                    expr[4 * hh + 3][:], ALU.add)
                    v.drain()
                    with nc.allow_low_precision("l3a rden"):
                        v.reciprocal(rdenr[hh][:], denr[hh][:])
                    v.drain()
                    for gg in range(4):
                        v.tensor_tensor(expr[4 * hh + gg][:],
                                        expr[4 * hh + gg][:],
                                        rdenr[hh][:], ALU.mult)
                        v.drain()
        mm0c, ev0c, dv0c = b.c_mm.n, b.c_ev.n, b.c_dv.n
        with nc.Block() as blk:
            @blk.tensor
            def _(t):
                for p in range(16):
                    if p >= 2:
                        t.wait_ge(b.c_ev.h, ev0c + p - 1)
                    i = t.matmul(b.PA[0:128, (p % 2) * 512:(p % 2) * 512 + 256],
                                 orow[:], expr[p][:], start=True, stop=True)
                    b.c_mm.inc(i)
            @blk.scalar
            def _(e):
                for p in range(16):
                    e.wait_ge(b.c_mm.h, mm0c + p + 1)
                    if p >= 2:
                        e.wait_ge(b.c_dv.h, dv0c + p - 1)
                    i = e.activation(abs_t[p % 2][:],
                                     b.PA[0:128, (p % 2) * 512:(p % 2) * 512 + 256],
                                     AF.Copy)
                    b.c_ev.inc(i)
            @blk.vector
            def _(v):
                for p in range(16):
                    hh, gg = p // 4, p % 4
                    v.wait_ge(b.c_ev.h, ev0c + p + 1)
                    last = None
                    for c in range(KC):
                        if gg == 0:
                            last = v.tensor_tensor(sub(C, 4, hh * 8 + c),
                                                   sub(C, 0, gg * 8 + c),
                                                   abs_t[p % 2][:], ALU.mult)
                        else:
                            v.tensor_tensor(tmpc[:], sub(C, 0, gg * 8 + c),
                                            abs_t[p % 2][:], ALU.mult)
                            v.drain()
                            last = v.tensor_tensor(sub(C, 4, hh * 8 + c),
                                                   sub(C, 4, hh * 8 + c),
                                                   tmpc[:], ALU.add)
                            v.drain()
                    b.c_dv.inc(last)
        with nc.Block() as blk:
            @blk.sync
            def _(sy, t0c=t0c):
                for i in range(4):
                    b.ld(sy, "h",
                         o_hbm[i * 1024:(i + 1) * 1024, t0c:t0c + 256]
                         .rearrange("(s p) t -> p s t", p=128),
                         C[4 + i][:].rearrange("p (s t) -> p s t", t=256))
                b.wq(sy, "h")

    bo_cols = env["gb"]["l3a_bo"]
    ot = [b.sb([128, 2048], BF16, OFF_B + i * 4096, f"l3a_ot{i}") for i in range(2)]
    for hh in range(4):
        mm0w, ev0w = b.c_mm.n, b.c_ev.n
        hp0 = [b.hp[0].n, b.hp[1].n]
        wt0 = [b.ws[0].n, b.ws[1].n]
        with nc.Block() as blk:
            @blk.sync
            def _(sy, hh=hh):
                for kc in range(KC):
                    if kc >= 2:
                        sy.wait_ge(b.c_mm.h, mm0w + 4 * (kc - 1))
                    hp = b.hp[kc % 2]
                    sy.dma_start(ot[kc % 2][:],
                                 o_hbm[hh * 1024 + kc * 128: hh * 1024 + (kc + 1) * 128, :]
                                 ).then_inc(hp.h, 16)
                    hp.n += 16
            @blk.gpsimd
            def _(g, hh=hh):
                i = 0
                for kc in range(KC):
                    for r in range(4):
                        if i >= 2:
                            g.wait_ge(b.c_mm.h, mm0w + i - 1)
                        ws = b.ws[i % 2]
                        g.dma_start(
                            wbuf[i % 2][:, 0:1024],
                            wo_bf[r * 1024 + kc * 128: r * 1024 + (kc + 1) * 128, :]
                        ).then_inc(ws.h, 16)
                        ws.n += 16
                        i += 1
            @blk.tensor
            def _(t, hh=hh):
                i = 0
                for kc in range(KC):
                    t.wait_ge(b.hp[kc % 2].h, hp0[kc % 2] + 16 * (kc // 2 + 1))
                    for r in range(4):
                        t.wait_ge(b.ws[i % 2].h, wt0[i % 2] + 16 * (i // 2 + 1))
                        rhs = ot[kc % 2][:].rearrange("p (a r) -> p r a", r=4)[:, r, :]
                        last = None
                        for m in range(8):
                            tgt = b.PA if m < 4 else b.PB
                            last = t.matmul(
                                tgt[:, (m % 4) * 512:(m % 4) * 512 + 512],
                                wbuf[i % 2][:, m * 128:(m + 1) * 128],
                                rhs,
                                start=(kc == 0 and r == 0),
                                stop=(kc == KC - 1 and r == 3))
                        b.c_mm.inc(last)
                        i += 1
        with nc.Block() as blk:
            @blk.scalar
            def _(e, hh=hh):
                for m in range(8):
                    tgt = b.PA if m < 4 else b.PB
                    e.activation(A[m][:, hh * 512:(hh + 1) * 512],
                                 tgt[:, (m % 4) * 512:(m % 4) * 512 + 512],
                                 AF.Identity, bias=bo_cols[m])
    return A, Bp, OFF_B




L2A_UPTO = 4


def emit_l2a(b, env):
    nc = b.nc
    A, Bp, C, Dp = env["pA"], env["pB"], env["pC"], env["pD"]
    wbuf = env["wbuf"]
    e2, bo16 = env["e2"], env["bo16all"]
    wqkv = b.inhbm("l2a_wqkv")
    wout = b.inhbm("l2a_wout")

    cc_in = nc.dram_tensor("l2a_ccin", [2048, 2048], BF16).ap()
    cc_out = nc.dram_tensor("l2a_ccout", [16384, 2048], BF16,
                            addr_space="Shared").ap()

    stage = [b.sb([128, 2048], BF16, OFF_D, "l2a_st0"),
             b.sb([128, 2048], BF16, OFF_D + 4096, "l2a_st1")]
    emit_proj_fm_store(b, wqkv, 1024, Bp, AF.Copy, stage, cc_in[0:1024, :],
                       wbuf, col_off=1024)
    emit_proj_fm_store(b, wqkv, 1024, Bp, AF.Copy, stage, cc_in[1024:2048, :],
                       wbuf, col_off=2048)
    with nc.Block() as blk:
        @blk.gpsimd
        def _(g):
            g.wait_ge(b.st[0].h, b.st[0].n)
            g.wait_ge(b.st[1].h, b.st[1].n)
            g.collective_compute(
                "AllGather", ALU.bypass, replica_groups=[list(range(8))],
                ins=[cc_in.opt()], outs=[cc_out.opt()]).then_inc(b.c_pl.h, 1)
            b.c_pl.n += 1
    ag_done = b.c_pl.n
    emit_proj_fm(b, wqkv, 1024, Bp,
                 lambda e, j, ps: e.activation(C[j][:], ps[:], AF.Copy,
                                               scale=1.0 / 32.0), wbuf, col_off=0)

    kj = [b.sb([128, 2048], BF16, OFF_D, "l2a_kj0"),
          b.sb([128, 2048], BF16, OFF_D + 4096, "l2a_kj1")]
    u2 = [b.sb([128, 2048], BF16, OFF_D + 8192, "l2a_u0"),
          b.sb([128, 2048], BF16, OFF_D + 12288, "l2a_u1")]
    tmpd = b.sb([128, 2048], BF16, OFF_D + 16384, "l2a_tmp")
    rden16 = b.sb([16, 2048], BF16, OFF_D + 20480, "l2a_rden")
    den16 = b.sb([16, 2048], F32, OFF_D + 24576, "l2a_den")
    rdp = [b.sb([2, 2048], BF16, SCRW + 8192, "l2a_rdp0"),
           b.sb([2, 2048], BF16, OFF_D + 8192, "l2a_rdp1")]

    for j in range(8 if L2A_UPTO >= 2 else 0):
        hp0 = [b.hp[0].n, b.hp[1].n]
        mm0, ev0, dv0 = b.c_mm.n, b.c_ev.n, b.c_dv.n
        with nc.Block() as blk:
            @blk.sync
            def _(sy, j=j):
                sy.wait_ge(b.c_pl.h, ag_done)
                for c in range(KC):
                    if c >= 2:
                        sy.wait_ge(b.c_dv.h, dv0 + c - 1)
                    hp = b.hp[c % 2]
                    sy.dma_start(kj[c % 2][:],
                                 cc_out[j * 2048 + c * 128: j * 2048 + (c + 1) * 128, :]
                                 ).then_inc(hp.h, 16)
                    hp.n += 16
            @blk.vector
            def _(v, j=j):
                for c in range(KC):
                    v.wait_ge(b.hp[c % 2].h, hp0[c % 2] + 16 * (c // 2 + 1))
                    if c >= 2:
                        v.wait_ge(b.c_mm.h, mm0 + c - 1)
                    i = v.tensor_tensor(u2[c % 2][:], C[c][:], kj[c % 2][:],
                                        ALU.mult)
                    b.c_dv.inc(i)
            @blk.tensor
            def _(t, j=j):
                for c in range(KC):
                    t.wait_ge(b.c_dv.h, dv0 + c + 1)
                    last = None
                    for nb in range(NB):
                        last = t.matmul(
                            b.PB[0:16, nb * 512:(nb + 1) * 512],
                            bo16[:, c * 16:(c + 1) * 16],
                            u2[c % 2][:, nb * 512:(nb + 1) * 512],
                            start=(c == 0), stop=(c == KC - 1))
                    b.c_mm.inc(last)
            @blk.scalar
            def _(e, j=j):
                e.wait_ge(b.c_mm.h, mm0 + KC)
                e.activation(Bp[j][0:16, :], b.PB[0:16, 0:2048], AF.Exp,
                             bias=-20.0)

    if L2A_UPTO < 2:
        return A, Bp, OFF_B
    with nc.Block() as blk:
        @blk.vector
        def _(v):
            v.tensor_tensor(den16[:], Bp[0][0:16, :], Bp[1][0:16, :], ALU.add)
            for j in range(2, 8):
                v.drain()
                v.tensor_tensor(den16[:], den16[:], Bp[j][0:16, :], ALU.add)
            v.drain()
            with nc.allow_low_precision("l2a rden"):
                v.reciprocal(rden16[:], den16[:])

    for c in range(KC):
        if L2A_UPTO < 3:
            break
        hp0 = [b.hp[0].n, b.hp[1].n]
        st0l = [b.st[0].n, b.st[1].n]
        mm0, ev0, dv0, pl0 = b.c_mm.n, b.c_ev.n, b.c_dv.n, b.c_pl.n
        with nc.Block() as blk:
            @blk.sync
            def _(sy, c=c):
                for j in range(8):
                    if j >= 2:
                        sy.wait_ge(b.c_dv.h, dv0 + j - 1)
                    hp = b.hp[j % 2]
                    sy.dma_start(kj[j % 2][:],
                                 cc_out[j * 2048 + 1024 + c * 128:
                                        j * 2048 + 1024 + (c + 1) * 128, :]
                                 ).then_inc(hp.h, 16)
                    hp.n += 16
                    st = b.st[j % 2]
                    sy.dma_start(rdp[j % 2][:],
                                 Bp[j][2 * c:2 * c + 2, :]).then_inc(st.h, 16)
                    st.n += 16
            @blk.tensor
            def _(t, c=c):
                for j in range(8):
                    t.wait_ge(b.st[j % 2].h, st0l[j % 2] + 16 * (j // 2 + 1))
                    if j >= 2:
                        t.wait_ge(b.c_dv.h, dv0 + j - 1)
                    tgt = b.PA if j % 2 == 0 else b.PB
                    last = None
                    for nb in range(NB):
                        last = t.matmul(tgt[:, nb * 512:(nb + 1) * 512],
                                        e2[:], rdp[j % 2][:, nb * 512:(nb + 1) * 512],
                                        start=True, stop=True)
                    b.c_mm.inc(last)
            @blk.vector
            def _(v, c=c):
                for j in range(8):
                    v.wait_ge(b.c_mm.h, mm0 + j + 1)
                    v.wait_ge(b.hp[j % 2].h, hp0[j % 2] + 16 * (j // 2 + 1))
                    tgt = b.PA if j % 2 == 0 else b.PB
                    if j == 0:
                        i = v.tensor_tensor(C[c][:], kj[0][:], tgt[:], ALU.mult)
                    else:
                        v.tensor_tensor(tmpd[:], kj[j % 2][:], tgt[:], ALU.mult)
                        v.drain()
                        i = v.tensor_tensor(C[c][:], C[c][:], tmpd[:], ALU.add)
                    v.drain()
                    b.c_dv.inc(i)
        with nc.Block() as blk:
            @blk.sync
            def _(sy, c=c):
                b.ld(sy, "h", rdp[0][:], rden16[2 * c:2 * c + 2, :])
                b.wq(sy, "h")
        with nc.Block() as blk:
            @blk.tensor
            def _(t):
                for nb in range(NB):
                    t.matmul(b.PA[:, nb * 512:(nb + 1) * 512], e2[:],
                             rdp[0][:, nb * 512:(nb + 1) * 512],
                             start=True, stop=True)
        with nc.Block() as blk:
            @blk.vector
            def _(v, c=c):
                v.tensor_tensor(C[c][:], C[c][:], b.PA[:], ALU.mult)

    if L2A_UPTO < 4:
        return A, Bp, OFF_B
    emit_proj_fm(b, wout, 1024, C,
                 lambda e, j, ps: e.tensor_tensor(A[j][:], ps[:], A[j][:], ALU.add),
                 wbuf, evac_eng="vector")
    return A, Bp, OFF_B


NEED_Y = {
    "l0a": [("plain", "pA")],
    "l0f": [("plain", "pA")],
    "l1a": [("affine", "pA", "l1a_g", "l1a_b")],
    "l1f": [("center",)],
    "l2a": [("plain", "pA"), ("affine", "pB", "l2a_g", "l2a_b")],
    "l2f": [("affine", "pA", "l2f_g1", "l2f_b1")],
    "l3a": [("plain", "pA")],
    "l3f": [("plain", "pA")],
}

GB_ROWS = {"l0a_g": "l0a_g_row", "l0a_b": "l0a_b_row",
           "l1f_bias": "l1f_bias_row",
           "l3a_bq": "l3a_bq_row", "l3a_bk": "l3a_bk_row",
           "l3a_bv": "l3a_bv_row", "l3a_bo": "l3a_bo_row",
           "l0f_mur": "l0f_mur_row", "l0f_muk": "l0f_muk_row",
           "l3f_mur": "l3f_mur_row", "l3f_muk": "l3f_muk_row",
           "l1a_g": "l1a_g_row", "l1a_b": "l1a_b_row",
           "l2a_g": "l2a_g_row", "l2a_b": "l2a_b_row",
           "l2f_g1": "l2f_g1_row", "l2f_b1": "l2f_b1_row",
           "l2f_g2": "l2f_g2_row", "l2f_b2": "l2f_b2_row"}

EMITTERS = {}
EMITTERS["l0a"] = emit_l0a
EMITTERS["l0f"] = emit_l0f
EMITTERS["l3f"] = emit_l3f
EMITTERS["l1f"] = emit_l1f
EMITTERS["l1a"] = emit_l1a
EMITTERS["l2f"] = emit_l2f
EMITTERS["l3a"] = emit_l3a
EMITTERS["l2a"] = emit_l2a


def yspecs(env, st):
    out = []
    for sp in NEED_Y[st]:
        if sp[0] == "plain":
            out.append(("plain", env[sp[1]]))
        elif sp[0] == "center":
            out.append(("center",))
        else:
            out.append(("affine", env[sp[1]], env["gb"][sp[2]], env["gb"][sp[3]]))
    return out


def build_nc(arrays, steps):
    b = Bld(arrays)
    nc = b.nc
    x_in = nc.dram_tensor("x_in", [1024, 2048], F32, kind="ExternalInput").ap()
    b.used.append("x_in")
    x_out = nc.dram_tensor("x_out", [1024, 2048], F32, kind="ExternalOutput").ap()

    env = {}
    env["X"] = [b.sb([128, 2048], F32, OFF_X + c * 8192, f"X{c}") for c in range(KC)]
    for nm, off in (("pA", OFF_A), ("pB", OFF_B), ("pC", OFF_C), ("pD", OFF_D)):
        env[nm] = [b.sb([128, 2048], BF16, off + c * 4096, f"{nm}{c}") for c in range(KC)]
    env["wbuf"] = [b.sb([128, 2048], BF16, SCRW, "wb0"),
                   b.sb([128, 2048], BF16, SCRW + 4096, "wb1")]
    env["ones_col"] = b.sb([128, 1], BF16, OFF_CONST, "ones_col")
    env["biases"] = b.sb([128, 7], F32, OFF_CONST + 1856, "biases")
    for bi, bval in enumerate([1e-5, 1e-8, float(np.pi / 2), 1.0, 0.5, -30.0, -20.0]):
        nc.const_aps.register(F32, bval, env["biases"][:, bi:bi + 1])
    env["ones_row"] = b.sb([1, 128], BF16, OFF_CONST + 32, "ones_row")
    env["e2"] = b.sb([2, 128], BF16, OFF_CONST + 1056, "e2")
    env["ones64"] = b.sb([65, 128], BF16, OFF_CONST + 2816, "ones64")
    env["ident"] = b.sb([128, 128], BF16, OFF_CONST + 288, "ident")
    if "l1f" in steps:
        env["yc"] = [b.sb([128, 2048], F32, OFF_A + c * 8192, f"yc{c}")
                     for c in range(KC)]
        env["l1f_inv_hbm"] = nc.dram_tensor("l1f_invh", [1, 2048], BF16).ap()
    env["tri"] = b.sb([128, 128], BF16, OFF_CONST + 544, "tri")
    env["iota8"] = b.sb([128, 8], F32, OFF_CONST + 1312, "iota8")
    env["identf"] = b.sb([128, 128], F32, OFF_CONST + 1344, "identf")
    env["bo16all"] = b.sb([128, 128], BF16, OFF_CONST + 800, "bo16all")

    env["gb"] = {}
    gb_needed = []
    for st in steps:
        for sp in NEED_Y.get(st, []):
            if sp[0] == "affine":
                gb_needed += [sp[2], sp[3]]
    if "l0a" in steps:
        gb_needed += ["l0a_g", "l0a_b"]
    if "l0f" in steps:
        gb_needed += ["l0f_mur", "l0f_muk"]
    if "l3f" in steps:
        gb_needed += ["l3f_mur", "l3f_muk"]
    if "l1f" in steps:
        gb_needed += ["l1f_bias"]
    if "l3a" in steps:
        gb_needed += ["l3a_bq", "l3a_bk", "l3a_bv", "l3a_bo"]
    if "l2f" in steps:
        gb_needed += ["l2f_g2", "l2f_b2"]
    gb_off = OFF_CONST + 1888
    gb_tiles = {}
    for gname in dict.fromkeys(gb_needed):
        wcols = b_arr_len(arrays := None, GB_ROWS[gname]) if False else None
        import numpy as _np
        wid = int(b.arrays[GB_ROWS[gname]].shape[1]) // 128
        tile = b.sb([128, wid], F32, gb_off, f"gb_{gname}")
        gb_tiles[gname] = tile
        env["gb"][gname] = [tile[:, c:c + 1] for c in range(wid)]
        gb_off += 4 * wid
    assert gb_off <= OFF_CONST + 3072, gb_off

    with nc.Block() as blk:
        @blk.sync
        def _(s):
            for c in range(KC):
                b.ld(s, "h", env["X"][c][:], x_in[c * 128:(c + 1) * 128, :])
            b.ld(s, "h", env["ones_col"][:], b.inhbm("c_ones_col")[:])
            b.ld(s, "h", env["biases"][:], b.inhbm("c_biases")[:])
            b.ld(s, "h", env["ones_row"][:], b.inhbm("c_ones_row")[:])
            b.ld(s, "h", env["ident"][:], b.inhbm("c_ident")[:])
            b.ld(s, "h", env["tri"][:], b.inhbm("c_tri")[:])
            b.ld(s, "h", env["iota8"][:], b.inhbm("c_iota8")[:])
            b.ld(s, "h", env["identf"][:], b.inhbm("c_identf")[:])
            b.ld(s, "h", env["bo16all"][:], b.inhbm("c_bo16all")[:])
            b.ld(s, "h", env["e2"][:], b.inhbm("c_e2")[:])
            b.ld(s, "h", env["ones64"][:], b.inhbm("c_ones64")[:])
            for gname, tile in gb_tiles.items():
                src = b.inhbm(GB_ROWS[gname]).rearrange("one (c p) -> (one p) c", p=128)
                b.ld(s, "h", tile[:], src, allow_slow_non_contiguous=True)
            b.wq(s, "h")

    first_specs = yspecs(env, steps[0]) if steps else []
    emit_boundary(b, env, None, env["pD"], OFF_D, first_specs, "binit")

    for i, st in enumerate(steps):
        z_pool, sq_pool, sq_off = EMITTERS[st](b, env)
        nxt = yspecs(env, steps[i + 1]) if i + 1 < len(steps) else []
        emit_boundary(b, env, z_pool, sq_pool, sq_off, nxt, f"b_{st}")

    with nc.Block() as blk:
        @blk.sync
        def _(s):
            for c in range(KC):
                b.ld(s, "h", x_out[c * 128:(c + 1) * 128, :], env["X"][c][:])
            b.wq(s, "h")
    return b, env


def run_steps(arrays, x_fm_percore, steps, trace=False):
    b, env = build_nc(arrays, steps)
    in_maps = []
    for c in range(len(x_fm_percore)):
        m = {"x_in": x_fm_percore[c]}
        for nm in b.used:
            if nm != "x_in":
                m[nm] = arrays[nm]
        in_maps.append(m)
    res = run_bass_kernel_spmd(b.nc, in_maps, core_ids=list(range(len(in_maps))),
                               trace=trace)
    return [r["x_out"] for r in res.results], res


STEPS_ALL = ["l0a", "l0f", "l1a", "l1f", "l2a", "l2f", "l3a", "l3f"]


def kernel(**inputs):
    arrays = _prep(inputs)
    x = np.asarray(inputs["L0_attn_inputs"], dtype=np.float32)
    x_fm = [np.ascontiguousarray(x[c].T) for c in range(8)]
    outs, _ = run_steps(arrays, x_fm, STEPS_ALL)
    return np.stack([o.T for o in outs], axis=0)




# version 03
# speedup vs baseline: 1.1222x; 1.0327x over previous
import numpy as np
import ml_dtypes
import concourse.bass as bass
import concourse.mybir as mybir
from concourse.bass_utils import run_bass_kernel_spmd

F32 = mybir.dt.float32
BF16 = mybir.dt.bfloat16
AF = mybir.ActivationFunctionType
ALU = mybir.AluOpType
AX = mybir.AxisListType

D = 1024
T = 2048
KC = 8
NB = 4
EPS = 1e-5
BF = ml_dtypes.bfloat16

BASE = 16512
OFF_X = BASE
OFF_A = BASE + 65536
OFF_B = BASE + 98304
OFF_C = BASE + 131072
OFF_D = BASE + 163840
OFF_CONST = BASE + 196608 + 512
SCRW = BASE + 196608 + 512 + 3072
SBUF_END = 229344



def _reorder_qkv_ed(W):
    Wt = np.ascontiguousarray(np.asarray(W, dtype=np.float32).T).reshape(1024, 16, 3, 64)
    q = Wt[:, :, 0, :].reshape(1024, 1024)
    k = Wt[:, :, 1, :].reshape(1024, 1024)
    v = Wt[:, :, 2, :].reshape(1024, 1024)
    return np.ascontiguousarray(np.concatenate([q, k, v], axis=1))


def _unitize(W):
    W = np.asarray(W, dtype=np.float32)
    d, n = W.shape
    assert d == 1024 and n % 256 == 0
    Wr = W.reshape(8, 128, n // 256, 256)
    out = Wr.transpose(2, 1, 0, 3).reshape(n // 256, 128, 2048)
    return np.ascontiguousarray(out)


def _prep(inp):
    f = lambda a: np.ascontiguousarray(np.asarray(a, dtype=np.float32))
    row = lambda a: np.ascontiguousarray(np.asarray(a, dtype=np.float32).reshape(1, -1))
    w = {}
    w["l0a_wq"] = _unitize(f(inp["L0_attn_W_Q"]))
    w["l0a_wk"] = _unitize(f(inp["L0_attn_W_K"]))
    w["l0a_wv"] = _unitize(f(inp["L0_attn_W_V"]))
    w["l0a_wfc"] = _unitize(f(inp["L0_attn_W_fc"]))
    w["l0a_g_row"] = row(inp["L0_attn_gamma"])
    w["l0a_b_row"] = row(inp["L0_attn_beta"])
    w["l0f_wr"] = _unitize(f(np.asarray(inp["L0_ffn_W_r"]).T))
    w["l0f_wk"] = _unitize(f(np.asarray(inp["L0_ffn_W_k"]).T))
    w["l0f_wv"] = _unitize(f(np.asarray(inp["L0_ffn_W_v"]).T))
    w["l0f_mur_row"] = row(inp["L0_ffn_mu_r"])
    w["l0f_muk_row"] = row(inp["L0_ffn_mu_k"])
    w["l1a_g_row"] = row(inp["L1_attn_ln_gamma"])
    w["l1a_b_row"] = row(inp["L1_attn_ln_beta"])
    w["l1a_wqkv"] = _unitize(_reorder_qkv_ed(inp["L1_attn_W_qkv"]))
    w["l1a_wo"] = _unitize(f(np.asarray(inp["L1_attn_W_o"]).T))
    w["l1f_w"] = f(inp["L1_ffn_weight"])
    w["l1f_bias_row"] = row(inp["L1_ffn_bias"])
    w["l2a_wqkv"] = _unitize(_reorder_qkv_ed(inp["L2_attn_w_qkv"]))
    w["l2a_wout"] = _unitize(f(np.asarray(inp["L2_attn_w_out"]).T))
    w["l2a_g_row"] = row(inp["L2_attn_ln_w"])
    w["l2a_b_row"] = row(inp["L2_attn_ln_b"])
    w["l2f_g1_row"] = row(inp["L2_ffn_ln1_w"])
    w["l2f_b1_row"] = row(inp["L2_ffn_ln1_b"])
    w["l2f_wqkv"] = _unitize(f(inp["L2_ffn_w_qkv"]))
    w["l2f_wproj"] = _unitize(f(inp["L2_ffn_w_proj"]))
    w["l2f_g2_row"] = row(inp["L2_ffn_ln2_w"])
    w["l2f_b2_row"] = row(inp["L2_ffn_ln2_b"])
    w["l2f_gatew"] = f(inp["L2_ffn_gate_w"])
    w["l2f_gateb_row"] = row(inp["L2_ffn_gate_b"])
    for e in range(8):
        w[f"l2f_w1_{e}"] = _unitize(f(np.asarray(inp["L2_ffn_w1"])[e]))
        w2_ = np.asarray(inp["L2_ffn_w2"], dtype=np.float32)[e]
        w2r = w2_.reshape(32, 128, 2, 512).transpose(2, 1, 0, 3)
        w[f"l2f_w2_{e}"] = np.ascontiguousarray(w2r.reshape(2, 128, 32 * 512))
    w["l3a_phase"] = f(inp["L3_attn_phase"])
    w["l3a_wq"] = _unitize(f(inp["L3_attn_Wq"]))
    w["l3a_wk"] = _unitize(f(inp["L3_attn_Wk"]))
    w["l3a_wv"] = _unitize(f(inp["L3_attn_Wv"]))
    w["l3a_bq_row"] = row(inp["L3_attn_bq"])
    w["l3a_bk_row"] = row(inp["L3_attn_bk"])
    w["l3a_bv_row"] = row(inp["L3_attn_bv"])
    w["l3a_wo"] = f(inp["L3_attn_Wo"])
    w["l3a_bo_row"] = row(inp["L3_attn_bo"])
    w["l3f_wr"] = _unitize(f(np.asarray(inp["L3_ffn_W_r"]).T))
    w["l3f_wk"] = _unitize(f(np.asarray(inp["L3_ffn_W_k"]).T))
    w["l3f_wv"] = _unitize(f(np.asarray(inp["L3_ffn_W_v"]).T))
    w["l3f_mur_row"] = row(inp["L3_ffn_mu_r"])
    w["l3f_muk_row"] = row(inp["L3_ffn_mu_k"])
    BIAS_VALS = [1e-5, 1e-8, float(np.pi / 2), 1.0, 0.5, -30.0, -20.0]
    w["c_biases"] = np.tile(np.asarray(BIAS_VALS, dtype=np.float32)[None, :], (128, 1))
    w["c_ones_row"] = np.ones((1, 128), dtype=BF)
    w["c_ones_col"] = np.ones((128, 1), dtype=BF)
    w["c_ident"] = np.eye(128, dtype=BF)
    w["c_identf"] = np.eye(128, dtype=np.float32)
    w["c_tri"] = (np.arange(128)[None, :] >= np.arange(128)[:, None]).astype(BF)
    w["c_iota8"] = np.tile(np.arange(8, dtype=np.float32)[None, :], (128, 1))
    bo16 = np.zeros((128, 128), dtype=BF)
    for c_ in range(8):
        for h_ in range(16):
            if h_ // 2 == c_:
                rows = slice((h_ % 2) * 64, (h_ % 2) * 64 + 64)
                bo16[rows, c_ * 16 + h_] = 1
    w["c_bo16all"] = bo16
    e2_ = np.zeros((2, 128), dtype=BF)
    e2_[0, :64] = 1
    e2_[1, 64:] = 1
    w["c_e2"] = e2_
    o64 = np.zeros((65, 128), dtype=BF)
    o64[64, :] = 1
    w["c_ones64"] = o64
    return w



class Ctr:
    def __init__(self, h):
        self.h = h
        self.n = 0

    def inc(self, inst, k=1):
        inst.then_inc(self.h, k)
        self.n += k
        return self.n


class Bld:
    def __init__(self, arrays):
        nc = bass.Bass("TRN2", target_bir_lowering=False)
        self.nc = nc
        self.arrays = arrays
        self.hbm = {}
        self.used = []
        self.q = {
            "h": Ctr(nc.alloc_semaphore("sem_h")),
            "s": Ctr(nc.alloc_semaphore("sem_s")),
            "g": Ctr(nc.alloc_semaphore("sem_g")),
        }
        self.ws = [Ctr(nc.alloc_semaphore("sem_w0")),
                   Ctr(nc.alloc_semaphore("sem_w1"))]
        self.st = [Ctr(nc.alloc_semaphore("sem_t0")),
                   Ctr(nc.alloc_semaphore("sem_t1"))]
        self.hp = [Ctr(nc.alloc_semaphore("sem_p0")),
                   Ctr(nc.alloc_semaphore("sem_p1"))]
        self.c_mm = Ctr(nc.alloc_semaphore("sem_mm"))
        self.c_ev = Ctr(nc.alloc_semaphore("sem_ev"))
        self.c_dv = Ctr(nc.alloc_semaphore("sem_dv"))
        self.c_pl = Ctr(nc.alloc_semaphore("sem_pl"))
        self.PA = nc.alloc_psum_tensor("PSA", [128, 2048], F32)
        self.PB = nc.alloc_psum_tensor("PSB", [128, 2048], F32)
        self._sbn = 0

    def inhbm(self, name):
        if name not in self.hbm:
            a = self.arrays[name]
            dt = BF16 if a.dtype == BF else F32
            self.hbm[name] = self.nc.dram_tensor(name, list(a.shape), dt,
                                                 kind="ExternalInput").ap()
            self.used.append(name)
        return self.hbm[name]

    def sb(self, shape, dtype, off, name=None):
        self._sbn += 1
        nm = name or f"sb{self._sbn}"
        esz = 2 if dtype == BF16 else 4
        nbytes = int(np.prod(shape[1:])) * esz
        assert off + nbytes <= SBUF_END, f"{nm}: {off}+{nbytes} over SBUF end"
        return self.nc.alloc_sbuf_tensor_at(nm, list(shape), dtype, offset=off)

    def ld(self, eng, qn, dst, src, **kw):
        eng.dma_start(dst, src, **kw).then_inc(self.q[qn].h, 16)
        self.q[qn].n += 16
        return self.q[qn].n

    def wq(self, eng, qn, tgt=None):
        eng.wait_ge(self.q[qn].h, self.q[qn].n if tgt is None else tgt)



def emit_proj_fm(b, w_hbm, n_out, rhs_tiles, evac, wbuf, evac_eng="scalar",
                 col_off=0):
    nc = b.nc
    n_units = n_out // 256
    mm0, ev0 = b.c_mm.n, b.c_ev.n
    wt = []

    with nc.Block() as blk:
        @blk.gpsimd
        def _(g):
            for u in range(n_units):
                if u >= 2:
                    g.wait_ge(b.c_mm.h, mm0 + 2 * (u - 1))
                src = w_hbm[(col_off // 256) + u]
                ws = b.ws[u % 2]
                g.dma_start(wbuf[u % 2][:], src).then_inc(ws.h, 16)
                ws.n += 16
                wt.append(ws.n)

        @blk.tensor
        def _(t):
            for u in range(n_units):
                t.wait_ge(b.ws[u % 2].h, wt[u])
                for mo in range(2):
                    j = u * 2 + mo
                    tgt = b.PA if j % 2 == 0 else b.PB
                    if j >= 2:
                        t.wait_ge(b.c_ev.h, ev0 + j - 1)
                    last = None
                    for tb in range(NB):
                        for kc in range(KC):
                            last = t.matmul(
                                tgt[:, tb * 512:(tb + 1) * 512],
                                wbuf[u % 2][:, kc * 256 + mo * 128: kc * 256 + (mo + 1) * 128],
                                rhs_tiles[kc][:, tb * 512:(tb + 1) * 512],
                                start=(kc == 0), stop=(kc == KC - 1))
                    b.c_mm.inc(last)

        def evs(e):
            for j in range(2 * n_units):
                tgt = b.PA if j % 2 == 0 else b.PB
                e.wait_ge(b.c_mm.h, mm0 + j + 1)
                last = evac(e, j, tgt)
                b.c_ev.inc(last)

        if evac_eng == "scalar":
            @blk.scalar
            def _(e):
                evs(e)
        else:
            @blk.vector
            def _(e):
                evs(e)


def emit_proj_tm(b, w_hbm, n_out, rhs_tiles, evac, wbuf, evac_eng="scalar",
                 col_off=0):
    nc = b.nc
    n_units = n_out // 256
    mm0, ev0 = b.c_mm.n, b.c_ev.n
    wt = []

    def pslice(i):
        off = (i % 4) * 512
        return b.PA[:, off:off + 256]

    with nc.Block() as blk:
        @blk.gpsimd
        def _(g):
            for u in range(n_units):
                if u >= 2:
                    g.wait_ge(b.c_mm.h, mm0 + 16 * (u - 1))
                src = w_hbm[(col_off // 256) + u]
                ws = b.ws[u % 2]
                g.dma_start(wbuf[u % 2][:], src).then_inc(ws.h, 16)
                ws.n += 16
                wt.append(ws.n)

        @blk.tensor
        def _(t):
            i = 0
            for u in range(n_units):
                t.wait_ge(b.ws[u % 2].h, wt[u])
                for tt in range(16):
                    if i >= 4:
                        t.wait_ge(b.c_ev.h, ev0 + i - 3)
                    last = None
                    for kc in range(KC):
                        last = t.matmul(
                            pslice(i),
                            rhs_tiles[kc][:, tt * 128:(tt + 1) * 128],
                            wbuf[u % 2][:, kc * 256:(kc + 1) * 256],
                            start=(kc == 0), stop=(kc == KC - 1))
                    b.c_mm.inc(last)
                    i += 1

        def evs(e):
            i = 0
            for u in range(n_units):
                for tt in range(16):
                    e.wait_ge(b.c_mm.h, mm0 + i + 1)
                    last = evac(e, tt, u, pslice(i))
                    b.c_ev.inc(last)
                    i += 1

        if evac_eng == "scalar":
            @blk.scalar
            def _(e):
                evs(e)
        else:
            @blk.vector
            def _(e):
                evs(e)



def emit_stats(b, tiles, lhsT_fn, m, dstA=False):
    nc = b.nc
    tgt = b.PA if dstA else b.PB
    with nc.Block() as blk:
        @blk.tensor
        def _(t):
            for nb in range(NB):
                for kc in range(KC):
                    t.matmul(tgt[0:m, nb * 512:(nb + 1) * 512],
                             lhsT_fn(kc), tiles[kc][:, nb * 512:(nb + 1) * 512],
                             start=(kc == 0), stop=(kc == KC - 1))


def emit_rowmath_plain(b, s0_ap, s1_ap, scratch_row, a_row, b_row, eps=EPS):
    nc = b.nc
    with nc.Block() as blk:
        @blk.scalar
        def _(e):
            e.activation(scratch_row[:], s0_ap, AF.Square, scale=1.0 / D)
    with nc.Block() as blk:
        @blk.vector
        def _(v):
            v.scalar_tensor_tensor(scratch_row[:], s1_ap, 1.0 / D, scratch_row[:],
                                   ALU.mult, ALU.subtract)
    with nc.Block() as blk:
        @blk.scalar
        def _(e):
            e.activation(scratch_row[:], scratch_row[:], AF.Sqrt, bias=eps)
    with nc.Block() as blk:
        @blk.vector
        def _(v):
            with b.nc.allow_low_precision("bf16 LN broadcast rows"):
                v.reciprocal(a_row[:], scratch_row[:])
            v.drain()
            v.scalar_tensor_tensor(b_row[:], s0_ap, -1.0 / D, a_row[:],
                                   ALU.mult, ALU.mult)


def emit_bcast2(b, a_row, b_row, env):
    nc = b.nc
    orow = env["ones_row"]
    with nc.Block() as blk:
        @blk.tensor
        def _(t):
            for nb in range(NB):
                t.matmul(b.PA[:, nb * 512:(nb + 1) * 512], orow[:],
                         a_row[:, nb * 512:(nb + 1) * 512], start=True, stop=True)
                t.matmul(b.PB[:, nb * 512:(nb + 1) * 512], orow[:],
                         b_row[:, nb * 512:(nb + 1) * 512], start=True, stop=True)


def emit_affine_apply(b, src_tiles, dst_pool, tmp_a, tmp_b, gb=None, add_into_x=None):
    nc = b.nc
    dv0, pl0 = b.c_dv.n, b.c_pl.n
    pool_mode = (add_into_x is not None) or (gb is not None)
    with nc.Block() as blk:
        @blk.vector
        def _(v):
            for c in range(KC):
                tmp = tmp_a if c % 2 == 0 else tmp_b
                if pool_mode and c >= 2:
                    v.wait_ge(b.c_pl.h, pl0 + (c - 1))
                v.tensor_tensor(tmp[:], src_tiles[c][:], b.PA[:], ALU.mult)
                v.drain()
                if pool_mode:
                    i = v.tensor_tensor(tmp[:], tmp[:], b.PB[:], ALU.add)
                else:
                    i = v.tensor_tensor(dst_pool[c][:], tmp[:], b.PB[:], ALU.add)
                b.c_dv.inc(i)
        if add_into_x is not None:
            @blk.gpsimd
            def _(g):
                for c in range(KC):
                    tmp = tmp_a if c % 2 == 0 else tmp_b
                    g.wait_ge(b.c_dv.h, dv0 + c + 1)
                    i = g.tensor_tensor(add_into_x[c][:], add_into_x[c][:],
                                        tmp[:], ALU.add)
                    b.c_pl.inc(i)
        elif gb is not None:
            gcols, bcols = gb
            @blk.gpsimd
            def _(g):
                for c in range(KC):
                    tmp = tmp_a if c % 2 == 0 else tmp_b
                    g.wait_ge(b.c_dv.h, dv0 + c + 1)
                    i = g.tensor_scalar(dst_pool[c][:], tmp[:],
                                        gcols[c], bcols[c], ALU.mult, ALU.add)
                    b.c_pl.inc(i)


def scratch_rows(b, tag, n, base=SCRW, dtype=BF16):
    ts = []
    sz = 2048 * (2 if dtype == BF16 else 4)
    for i in range(n):
        ts.append(b.sb([1, 2048], dtype, base + i * sz, f"row_{tag}_{i}_{b._sbn}"))
    return ts


def emit_ln_stats_rows(b, env, tiles, sq_pool, sq_off, tag):
    nc = b.nc
    ones_col = env["ones_col"]
    a_row, b_row = scratch_rows(b, tag, 2)
    with nc.Block() as blk:
        @blk.scalar
        def _(e):
            for c in range(KC):
                e.activation(sq_pool[c][:], tiles[c][:], AF.Square)
    emit_stats(b, tiles, lambda kc: ones_col[:], 1)
    emit_stats(b, sq_pool, lambda kc: ones_col[:], 1, dstA=True)
    sc_row = b.sb([1, 2048], F32, sq_off, f"scr_{tag}_{b._sbn}")
    emit_rowmath_plain(b, b.PB[0:1, 0:2048], b.PA[0:1, 0:2048], sc_row,
                       a_row, b_row)
    emit_bcast2(b, a_row, b_row, env)


def emit_boundary(b, env, z_pool, sq_pool, sq_off, y_specs, tag):
    nc = b.nc
    X = env["X"]
    tmp_a = b.sb([128, 2048], F32, sq_off + 8192, f"tba_{tag}")
    tmp_b = b.sb([128, 2048], F32, sq_off + 16384, f"tbb_{tag}")

    if z_pool is not None:
        emit_ln_stats_rows(b, env, z_pool, sq_pool, sq_off, tag + "z")
        emit_affine_apply(b, z_pool, None, tmp_a, tmp_b, add_into_x=X)

    if y_specs:
        has_center = any(sp[0] == "center" for sp in y_specs)
        bf_specs = [sp for sp in y_specs if sp[0] != "center"]
        with nc.Block() as blk:
            @blk.scalar
            def _(e):
                for c in range(KC):
                    e.activation(sq_pool[c][:], X[c][:], AF.Square)
        if bf_specs:
            xcast = bf_specs[0][1]
        elif z_pool is not None and z_pool is not sq_pool:
            xcast = z_pool
        else:
            xcast = env["pC"] if sq_pool is not env["pC"] else env["pD"]
        assert xcast is not sq_pool
        with nc.Block() as blk:
            @blk.vector
            def _(v):
                for c in range(KC):
                    v.tensor_copy(xcast[c][:], X[c][:])
        ones_col = env["ones_col"]
        a_row, b_row = scratch_rows(b, tag + "x", 2)
        emit_stats(b, xcast, lambda kc: ones_col[:], 1)
        emit_stats(b, sq_pool, lambda kc: ones_col[:], 1, dstA=True)
        sc_row = b.sb([1, 2048], F32, sq_off, f"scx_{tag}_{b._sbn}")
        emit_rowmath_plain(b, b.PB[0:1, 0:2048], b.PA[0:1, 0:2048], sc_row,
                           a_row, b_row)
        if has_center:
            with nc.Block() as blk:
                @blk.vector
                def _(v):
                    with nc.allow_low_precision("negmu row"):
                        v.tensor_scalar(b_row[:], b.PB[0:1, 0:2048], -1.0 / D,
                                        None, ALU.mult)
                @blk.sync
                def _(sy):
                    b.ld(sy, "h", env["l1f_inv_hbm"], a_row[:])
                    b.wq(sy, "h")
            with nc.Block() as blk:
                @blk.tensor
                def _(t):
                    for nb in range(NB):
                        t.matmul(b.PA[:, nb * 512:(nb + 1) * 512],
                                 env["ones_row"][:],
                                 b_row[:, nb * 512:(nb + 1) * 512],
                                 start=True, stop=True)
            with nc.Block() as blk:
                @blk.vector
                def _(v):
                    for c in range(KC):
                        v.tensor_tensor(env["yc"][c][:], X[c][:], b.PA[:], ALU.add)
        if bf_specs:
            emit_bcast2(b, a_row, b_row, env)
            for spec in bf_specs:
                if spec[0] == "plain":
                    emit_affine_apply(b, X, spec[1], tmp_a, tmp_b)
                else:
                    emit_affine_apply(b, X, spec[1], tmp_a, tmp_b,
                                      gb=(spec[2], spec[3]))



def tm_tile(pool, tt):
    return pool[tt // 2][:, (tt % 2) * 1024:(tt % 2) * 1024 + 1024]


def emit_l0a(b, env):
    nc = b.nc
    X, A, Bp, C, Dp = env["X"], env["pA"], env["pB"], env["pC"], env["pD"]
    ones_col, ones_row = env["ones_col"], env["ones_row"]
    wbuf = env["wbuf"]
    wq, wk, wv, wfc = (b.inhbm(n) for n in ("l0a_wq", "l0a_wk", "l0a_wv", "l0a_wfc"))

    emit_proj_fm(b, wq, 1024, A,
                 lambda e, j, ps: e.activation(Bp[j][:], ps[:], AF.Sigmoid), wbuf)
    emit_proj_tm(b, wk, 1024, A,
                 lambda e, tt, u, ps: e.activation(
                     tm_tile(C, tt)[:, u * 256:(u + 1) * 256], ps[:], AF.Sigmoid), wbuf)
    emit_proj_tm(b, wv, 1024, A,
                 lambda e, tt, u, ps: e.activation(
                     tm_tile(Dp, tt)[:, u * 256:(u + 1) * 256], ps[:], AF.Copy), wbuf)

    kve = b.sb([128, 8 * 65], BF16, SCRW + 8192, "l0a_kve")
    for pair in range(8):
        with nc.Block() as blk:
            @blk.tensor
            def _(t, pair=pair):
                for tt in range(16):
                    t.matmul(b.PA[0:128, 0:128],
                             tm_tile(C, tt)[:, pair * 128:(pair + 1) * 128],
                             tm_tile(Dp, tt)[:, pair * 128:(pair + 1) * 128],
                             start=(tt == 0), stop=(tt == 15))
                for tt in range(16):
                    t.matmul(b.PB[0:128, 0:1],
                             tm_tile(C, tt)[:, pair * 128:(pair + 1) * 128],
                             ones_col[:],
                             start=(tt == 0), stop=(tt == 15))
        with nc.Block() as blk:
            @blk.vector
            def _(v, pair=pair):
                v.tensor_copy(kve[0:64, pair * 65:pair * 65 + 64], b.PA[0:64, 0:64])
                v.drain()
                v.tensor_copy(kve[64:128, pair * 65:pair * 65 + 64],
                              b.PA[64:128, 64:128])
                v.drain()
                v.tensor_copy(kve[:, pair * 65 + 64:pair * 65 + 65], b.PB[0:128, 0:1])

    den_row = b.sb([1, 2048], F32, OFF_D, "l0a_den")
    rden_row = b.sb([1, 2048], BF16, OFF_D + 12288, "l0a_rden")
    rdenb = b.sb([64, 2048], BF16, OFF_D + 8192, "l0a_rdenb")
    for h in range(16):
        pair, half = h // 2, (h % 2) * 64
        with nc.Block() as blk:
            @blk.tensor
            def _(t, pair=pair, half=half):
                for tb in range(NB):
                    t.matmul(b.PA[0:65, tb * 512:(tb + 1) * 512],
                             kve[half:half + 64, pair * 65:(pair + 1) * 65],
                             Bp[pair][half:half + 64, tb * 512:(tb + 1) * 512],
                             start=True, stop=True)
        with nc.Block() as blk:
            @blk.scalar
            def _(e):
                e.activation(den_row[:], b.PA[64:65, 0:2048], AF.Copy, bias=1e-8)
        with nc.Block() as blk:
            @blk.vector
            def _(v):
                with nc.allow_low_precision("bf16 rden row"):
                    v.reciprocal(rden_row[:], den_row[:])
        with nc.Block() as blk:
            @blk.tensor
            def _(t):
                for nb in range(NB):
                    t.matmul(b.PB[0:64, nb * 512:(nb + 1) * 512], ones_row[:, 0:64],
                             rden_row[:, nb * 512:(nb + 1) * 512],
                             start=True, stop=True)
        with nc.Block() as blk:
            @blk.scalar
            def _(e):
                e.activation(rdenb[:], b.PB[0:64, 0:2048], AF.Copy)
        with nc.Block() as blk:
            @blk.vector
            def _(v, pair=pair, half=half):
                v.tensor_tensor(C[pair][half:half + 64, :], b.PA[0:64, 0:2048],
                                rdenb[:], ALU.mult)

    emit_proj_fm(b, wfc, 1024, C,
                 lambda e, j, ps: e.tensor_tensor(Dp[j][:], ps[:], A[j][:], ALU.add),
                 wbuf, evac_eng="vector")

    emit_ln_stats_rows(b, env, Dp, Bp, OFF_B, "l0a_in")
    tmp_a = b.sb([128, 2048], F32, OFF_B + 8192, "l0a_tma")
    tmp_b = b.sb([128, 2048], F32, OFF_B + 16384, "l0a_tmb")
    emit_affine_apply(b, Dp, A, tmp_a, tmp_b,
                      gb=(env["gb"]["l0a_g"], env["gb"]["l0a_b"]))
    return A, Bp, OFF_B





def emit_ffn(b, env, pfx):
    nc = b.nc
    A, Bp, C, Dp = env["pA"], env["pB"], env["pC"], env["pD"]
    wbuf = env["wbuf"]
    wr = b.inhbm(f"{pfx}_wr")
    wk = b.inhbm(f"{pfx}_wk")
    wv = b.inhbm(f"{pfx}_wv")
    mur = env["gb"][f"{pfx}_mur"]
    muk = env["gb"][f"{pfx}_muk"]

    with nc.Block() as blk:
        @blk.vector
        def _(v):
            for c in range(KC):
                v.memset(Bp[c][:, 0:1], 0.0)
                v.drain()
                v.tensor_copy(Bp[c][:, 1:2048], A[c][:, 0:2047])
    with nc.Block() as blk:
        @blk.vector
        def _(v):
            for c in range(KC):
                v.tensor_tensor(C[c][:], A[c][:], Bp[c][:], ALU.subtract)
                v.drain()
                v.scalar_tensor_tensor(Dp[c][:], C[c][:], muk[c], Bp[c][:],
                                       ALU.mult, ALU.add)
                v.drain()
                v.scalar_tensor_tensor(C[c][:], C[c][:], mur[c], Bp[c][:],
                                       ALU.mult, ALU.add)
    emit_proj_fm(b, wr, 1024, C,
                 lambda e, j, ps: e.activation(Bp[j][:], ps[:], AF.Sigmoid), wbuf)

    def ev_relu_sq(e, j, ps):
        e.activation(C[j][:], ps[:], AF.Relu)
        e.drain()
        return e.activation(C[j][:], C[j][:], AF.Square)
    emit_proj_fm(b, wk, 1024, Dp, ev_relu_sq, wbuf)

    emit_proj_fm(b, wv, 1024, C,
                 lambda e, j, ps: e.tensor_tensor(Dp[j][:], ps[:], Bp[j][:], ALU.mult),
                 wbuf, evac_eng="vector")
    return Dp, Bp, OFF_B


def emit_l0f(b, env):
    return emit_ffn(b, env, "l0f")


def emit_l3f(b, env):
    return emit_ffn(b, env, "l3f")




def _emit_fp6_quant_chain(b, blk_ops, W, G, src_view, rscl_bc, scl_bc,
                          t0, t1, t2, t3, dst):
    v3 = lambda ap: ap.rearrange("p (g k) -> p g k", k=32)

    def ops(v):
        with b.nc.allow_low_precision("fp6 quant chain"):
            v.tensor_tensor(v3(t0), v3(src_view), rscl_bc, ALU.mult)
            v.drain()
            v.tensor_tensor(t1, t0, t0, ALU.mult)
            v.drain()
            v.tensor_single_scalar(t2, t1, 16.0, ALU.is_ge)
            v.drain()
            v.tensor_single_scalar(t1, t1, 4.0, ALU.is_ge)
            v.drain()
            v.tensor_scalar(t3, t2, -2.0, 8.0, ALU.mult, ALU.add)
            v.drain()
            v.scalar_tensor_tensor(t3, t1, -4.0, t3, ALU.mult, ALU.add)
            v.drain()
            v.tensor_scalar(t2, t2, 0.25, 0.125, ALU.mult, ALU.add)
            v.drain()
            v.scalar_tensor_tensor(t2, t1, 0.125, t2, ALU.mult, ALU.add)
            v.drain()
            v.tensor_tensor(t0, t0, t3, ALU.mult)
            v.drain()
            v.tensor_scalar(t0, t0, -1.0, 0.5, ALU.mult, ALU.add)
            v.drain()
            v.tensor_single_scalar(t3, t0, 12582912.0, ALU.add)
            v.drain()
            v.tensor_single_scalar(t3, t3, 12582912.0, ALU.subtract)
            v.drain()
            v.tensor_tensor(t1, t3, t0, ALU.is_gt)
            v.drain()
            v.tensor_tensor(t3, t3, t1, ALU.subtract)
            v.drain()
            v.scalar_tensor_tensor(t0, t3, -1.0, t2, ALU.mult, ALU.mult)
            v.drain()
            v.tensor_tensor(v3(t0), v3(t0), scl_bc, ALU.mult)
            if dst is not None:
                v.drain()
                v.tensor_copy(dst, t0)
    blk_ops.append(ops)


def emit_proj_fm_sb(b, lhsT_tiles, n_out, rhs_tiles, evac, evac_eng="scalar"):
    nc = b.nc
    mm0, ev0 = b.c_mm.n, b.c_ev.n
    with nc.Block() as blk:
        @blk.tensor
        def _(t):
            for j in range(n_out // 128):
                tgt = b.PA if j % 2 == 0 else b.PB
                if j >= 2:
                    t.wait_ge(b.c_ev.h, ev0 + j - 1)
                last = None
                for tb in range(NB):
                    for kc in range(KC):
                        last = t.matmul(
                            tgt[:, tb * 512:(tb + 1) * 512],
                            lhsT_tiles(kc, j),
                            rhs_tiles[kc][:, tb * 512:(tb + 1) * 512],
                            start=(kc == 0), stop=(kc == KC - 1))
                b.c_mm.inc(last)

        def evs(e):
            for j in range(n_out // 128):
                tgt = b.PA if j % 2 == 0 else b.PB
                e.wait_ge(b.c_mm.h, mm0 + j + 1)
                last = evac(e, j, tgt)
                b.c_ev.inc(last)
        if evac_eng == "scalar":
            @blk.scalar
            def _(e):
                evs(e)
        else:
            @blk.vector
            def _(e):
                evs(e)


def emit_l1f(b, env):
    nc = b.nc
    A, Bp, C, Dp = env["pA"], env["pB"], env["pC"], env["pD"]
    ident = env["ident"]
    yc = env["yc"]
    w_hbm = b.inhbm("l1f_w")
    bias = env["gb"]["l1f_bias"]

    tmpt = b.sb([128, 2048], F32, SCRW, "l1f_tmpt")
    inv_row = b.sb([1, 2048], BF16, SCRW, "l1f_invloc")
    amx = b.sb([128, 64], F32, SCRW + 8192, "l1f_amx")
    scl = b.sb([128, 64], F32, SCRW + 8448, "l1f_scl")
    rscl = b.sb([128, 64], F32, SCRW + 8704, "l1f_rscl")
    t0 = b.sb([128, 2048], F32, OFF_D, "l1f_t0")
    t1 = b.sb([128, 2048], F32, OFF_D + 8192, "l1f_t1")
    t2 = b.sb([128, 2048], F32, OFF_D + 16384, "l1f_t2")
    t3 = b.sb([128, 2048], F32, OFF_D + 24576, "l1f_t3")

    for c in range(KC):
        with nc.Block() as blk:
            @blk.vector
            def _(v, c=c):
                v.transpose(tmpt[:], yc[c][:])
        with nc.Block() as blk:
            @blk.vector
            def _(v):
                with nc.allow_low_precision("fp6 absmax"):
                    v.tensor_reduce(amx[:], tmpt[:].rearrange("p (g k) -> p g k", k=32),
                                    AX.X, ALU.max, apply_absolute_value=True)
                v.drain()
                v.tensor_scalar(scl[:], amx[:], 1.0 / 7.5, 1e-30, ALU.mult, ALU.max)
                v.drain()
                v.reciprocal(rscl[:], scl[:])
        blk_ops = []
        sbc = scl[:].unsqueeze(-1).broadcast_to((128, 64, 32))
        rbc = rscl[:].unsqueeze(-1).broadcast_to((128, 64, 32))
        _emit_fp6_quant_chain(b, blk_ops, 2048, 64, tmpt[:], rbc, sbc,
                              t0[:], t1[:], t2[:], t3[:], None)
        with nc.Block() as blk:
            @blk.vector
            def _(v, op=blk_ops[0]):
                op(v)
        with nc.Block() as blk:
            @blk.vector
            def _(v, c=c):
                v.transpose(tmpt[:], t0[:])
                v.drain()
                v.tensor_copy(C[c][:], tmpt[:])

    wf = [b.sb([128, 1024], F32, OFF_A + c * 4096, f"l1f_wf{c}") for c in range(KC)]
    with nc.Block() as blk:
        @blk.sync
        def _(sy):
            for c in range(KC):
                b.ld(sy, "h", wf[c][:], w_hbm[c * 128:(c + 1) * 128, :])
            b.wq(sy, "h")
    for c in range(KC):
        with nc.Block() as blk:
            @blk.vector
            def _(v, c=c):
                with nc.allow_low_precision("fp6 absmax"):
                    v.tensor_reduce(amx[:, 0:32],
                                    wf[c][:].rearrange("p (g k) -> p g k", k=32),
                                    AX.X, ALU.max, apply_absolute_value=True)
                v.drain()
                v.tensor_scalar(scl[:, 0:32], amx[:, 0:32], 1.0 / 7.5, 1e-30,
                                ALU.mult, ALU.max)
                v.drain()
                v.reciprocal(rscl[:, 0:32], scl[:, 0:32])
        blk_ops = []
        sbc = scl[:, 0:32].unsqueeze(-1).broadcast_to((128, 32, 32))
        rbc = rscl[:, 0:32].unsqueeze(-1).broadcast_to((128, 32, 32))
        _emit_fp6_quant_chain(b, blk_ops, 1024, 32, wf[c][:], rbc, sbc,
                              t0[:, 0:1024], t1[:, 0:1024], t2[:, 0:1024],
                              t3[:, 0:1024], None)
        with nc.Block() as blk:
            @blk.vector
            def _(v, op=blk_ops[0]):
                op(v)
        with nc.Block() as blk:
            @blk.vector
            def _(v, c=c):
                v.tensor_copy(Bp[c][:, 0:1024], t0[:, 0:1024])

    pabf = b.PA[:].bitcast(BF16)
    mm0, ev0 = b.c_mm.n, b.c_ev.n
    with nc.Block() as blk:
        @blk.tensor
        def _(t):
            i = 0
            for oc in range(KC):
                for dc in range(KC):
                    if i >= 4:
                        t.wait_ge(b.c_ev.h, ev0 + i - 3)
                    last = t.transpose(pabf[:, (i % 4) * 1024:(i % 4) * 1024 + 128],
                                       Bp[oc][:, dc * 128:(dc + 1) * 128], ident[:])
                    b.c_mm.inc(last)
                    i += 1
        @blk.vector
        def _(v):
            i = 0
            for oc in range(KC):
                for dc in range(KC):
                    v.wait_ge(b.c_mm.h, mm0 + i + 1)
                    last = v.tensor_copy(Bp[dc][:, 1024 + oc * 128:1024 + (oc + 1) * 128],
                                         pabf[:, (i % 4) * 1024:(i % 4) * 1024 + 128])
                    b.c_ev.inc(last)
                    i += 1

    invb = b.sb([128, 2048], BF16, OFF_D, "l1f_invb")
    with nc.Block() as blk:
        @blk.sync
        def _(sy):
            b.ld(sy, "h", inv_row[:], env["l1f_inv_hbm"])
            b.wq(sy, "h")
    with nc.Block() as blk:
        @blk.tensor
        def _(t):
            for nb in range(NB):
                t.matmul(b.PB[:, nb * 512:(nb + 1) * 512], env["ones_row"][:],
                         inv_row[:, nb * 512:(nb + 1) * 512], start=True, stop=True)
    with nc.Block() as blk:
        @blk.scalar
        def _(e):
            e.activation(invb[:], b.PB[:], AF.Copy)

    def ev(e, j, ps):
        e.tensor_tensor(A[j][:], ps[:], invb[:], ALU.mult)
        e.drain()
        return e.tensor_scalar(A[j][:], A[j][:], bias[j], None, ALU.add)
    emit_proj_fm_sb(b, lambda kc, j: Bp[kc][:, 1024 + j * 128:1024 + (j + 1) * 128],
                    1024, C, ev, evac_eng="vector")
    return A, C, OFF_C





def _attn_schedule(causal):
    evs = []
    pairs_of_bq = []
    gp = 0
    for bq in range(NB):
        ntk = 4 * bq + 4 if causal else 16
        np_ = ntk // 2
        sec = []
        for p in range(np_):
            g = gp + p
            sec.append(("sp", g, bq, p))
            if p >= 1:
                sec.append(("av", g - 1, bq, p - 1))
        sec.append(("av", gp + np_ - 1, bq, np_ - 1))
        evs.append(sec)
        pairs_of_bq.append((gp, np_))
        gp += np_
    return evs, pairs_of_bq, gp


def emit_attention(b, env, y_pool, q_pool, k_pool, ctx_pool, wqkv_hbm, scale,
                   causal, tag):
    nc = b.nc
    wbuf = env["wbuf"]
    vt = b.sb([128, 16640], BF16, OFF_D, f"vt_{tag}")
    tri = env["tri"]
    orow = env["ones_row"]
    ones64 = env["ones64"]

    emit_proj_fm(b, wqkv_hbm, 1024, y_pool,
                 lambda e, j, ps: e.activation(q_pool[j][:], ps[:], AF.Copy,
                                               scale=scale), wbuf, col_off=0)
    emit_proj_fm(b, wqkv_hbm, 1024, y_pool,
                 lambda e, j, ps: e.activation(k_pool[j][:], ps[:], AF.Copy),
                 wbuf, col_off=1024)
    with nc.Block() as blk:
        @blk.vector
        def _(v):
            ones_view = vt[:].rearrange("p (t h j) -> p t h j", h=16, j=65)
            v.memset(ones_view[:, :, :, 64:65], 1.0)

    def ev_v(e, tt, u, ps):
        dst = vt[:, tt * 1040 + 4 * u * 65: tt * 1040 + 4 * u * 65 + 260]
        dst = dst.rearrange("p (h j) -> p h j", j=65)[:, :, 0:64]
        return e.activation(dst, ps[:].rearrange("p (h j) -> p h j", j=64), AF.Copy)
    emit_proj_tm(b, wqkv_hbm, 1024, y_pool, ev_v, wbuf, col_off=2048)

    a2 = [b.sb([128, 1024], BF16, SCRW, f"a2a_{tag}"),
          b.sb([128, 1024], BF16, SCRW + 2048, f"a2b_{tag}")]
    ctxs = [b.sb([65, 2048], BF16, SCRW + 4096, f"cx0_{tag}"),
            b.sb([65, 2048], BF16, SCRW + 8192, f"cx1_{tag}")]

    evs, pairs_of_bq, npairs = _attn_schedule(causal)
    diag_pairs = []
    if causal:
        for bq in range(NB):
            gp, np_ = pairs_of_bq[bq]
            diag_pairs += [gp + np_ - 2, gp + np_ - 1]

    prev_tt2 = None
    prev_ctt = [None, None]
    for h in range(16):
        pair, half = h // 2, (h % 2) * 64
        mm0, ev0, dv0 = b.c_mm.n, b.c_ev.n, b.c_dv.n
        sp_ms, av_ms = {}, {}
        mmn = mm0
        for sec in evs:
            for e_ in sec:
                mmn += 1
                (sp_ms if e_[0] == "sp" else av_ms)[e_[1]] = mmn
        bc_ms = [mmn + 1, mmn + 2]
        ex_ms = {g: ev0 + g + 1 for g in range(npairs)}
        evac_ms = ev0 + npairs + 1
        mk_ms = {g: dv0 + i + 1 for i, g in enumerate(diag_pairs)}
        recip_ms = dv0 + len(diag_pairs) + 1
        tt_ms = [recip_ms + 1, recip_ms + 2]
        last_av = max(av_ms.values())

        with nc.Block() as blk:
            @blk.tensor
            def _(t, h=h, pair=pair, half=half, prev_tt2=prev_tt2):
                first_av = True
                for sec in evs:
                    for e_ in sec:
                        kind, g, bq, p = e_
                        if kind == "sp":
                            if g >= 2:
                                t.wait_ge(b.c_ev.h, ex_ms[g - 2])
                            i = None
                            for tki in (2 * p, 2 * p + 1):
                                i = t.matmul(
                                    b.PB[:, (tki % 4) * 512:(tki % 4) * 512 + 512],
                                    k_pool[pair][half:half + 64,
                                                 tki * 128:(tki + 1) * 128],
                                    q_pool[pair][half:half + 64,
                                                 bq * 512:(bq + 1) * 512],
                                    start=True, stop=True)
                            b.c_mm.inc(i)
                            assert b.c_mm.n == sp_ms[g]
                        else:
                            t.wait_ge(b.c_ev.h, ex_ms[g])
                            if g in mk_ms:
                                t.wait_ge(b.c_dv.h, mk_ms[g])
                            if first_av and prev_tt2 is not None:
                                t.wait_ge(b.c_dv.h, prev_tt2)
                                first_av = False
                            elif first_av:
                                first_av = False
                            gp0, np_ = pairs_of_bq[bq]
                            last_of_bq = (p == np_ - 1)
                            i = None
                            for n_, tki in enumerate((2 * p, 2 * p + 1)):
                                lo = 128 * (tki - 4 * bq) if (causal and tki >= 4 * bq) else 0
                                at = a2[g % 2][:, (tki % 2) * 512 + lo:
                                               (tki % 2) * 512 + 512]
                                i = t.matmul(
                                    b.PA[0:65, bq * 512 + lo: bq * 512 + 512],
                                    vt[:, tki * 1040 + h * 65:
                                       tki * 1040 + h * 65 + 65],
                                    at,
                                    start=(tki == 0),
                                    stop=(last_of_bq and n_ == 1))
                            b.c_mm.inc(i)
                            assert b.c_mm.n == av_ms[g]
                for hf in range(2):
                    t.wait_ge(b.c_dv.h, recip_ms if hf == 0 else tt_ms[0])
                    i = None
                    for nb in range(2):
                        cc = hf * 1024 + nb * 512
                        i = t.matmul(b.PA[0:64, nb * 512:(nb + 1) * 512],
                                     ones64[64:65, 0:64],
                                     ctxs[h % 2][64:65, cc:cc + 512],
                                     start=True, stop=True)
                    b.c_mm.inc(i)
                    assert b.c_mm.n == bc_ms[hf]

            @blk.scalar
            def _(e, h=h):
                for g in range(npairs):
                    e.wait_ge(b.c_mm.h, sp_ms[g])
                    if g >= 2:
                        e.wait_ge(b.c_mm.h, av_ms[g - 2])
                    slot = (2 * g) % 4
                    i = e.activation(a2[g % 2][:],
                                     b.PB[:, slot * 512: slot * 512 + 1024],
                                     AF.Exp, bias=-20.0)
                    b.c_ev.inc(i)
                    assert b.c_ev.n == ex_ms[g]
                e.wait_ge(b.c_mm.h, last_av)
                if prev_ctt[h % 2] is not None:
                    e.wait_ge(b.c_dv.h, prev_ctt[h % 2])
                i = e.activation(ctxs[h % 2][:], b.PA[0:65, 0:2048], AF.Copy)
                b.c_ev.inc(i)
                assert b.c_ev.n == evac_ms

            @blk.vector
            def _(v, h=h, pair=pair, half=half):
                for g in diag_pairs:
                    v.wait_ge(b.c_ev.h, ex_ms[g])
                    bq = next(bq_ for bq_ in range(NB)
                              if pairs_of_bq[bq_][0] <= g < pairs_of_bq[bq_][0]
                              + pairs_of_bq[bq_][1])
                    p = g - pairs_of_bq[bq][0]
                    i = None
                    for tki in (2 * p, 2 * p + 1):
                        off = 128 * (tki - 4 * bq)
                        if 0 <= off < 512:
                            ap_ = a2[g % 2][:, (tki % 2) * 512 + off:
                                            (tki % 2) * 512 + off + 128]
                            i = v.tensor_tensor(ap_, ap_, tri[:], ALU.mult)
                            v.drain()
                    b.c_dv.inc(i)
                    assert b.c_dv.n == mk_ms[g]
                v.wait_ge(b.c_ev.h, evac_ms)
                with nc.allow_low_precision("attn rden"):
                    i = v.reciprocal(ctxs[h % 2][64:65, :], ctxs[h % 2][64:65, :])
                b.c_dv.inc(i)
                assert b.c_dv.n == recip_ms
                for hf in range(2):
                    v.wait_ge(b.c_mm.h, bc_ms[hf])
                    i = v.tensor_tensor(
                        ctx_pool[pair][half:half + 64,
                                       hf * 1024:(hf + 1) * 1024],
                        ctxs[h % 2][0:64, hf * 1024:(hf + 1) * 1024],
                        b.PA[0:64, 0:1024], ALU.mult)
                    v.drain()
                    b.c_dv.inc(i)
                    assert b.c_dv.n == tt_ms[hf]
        prev_tt2 = tt_ms[1]
        prev_ctt[h % 2] = tt_ms[1]
    return vt


def emit_l1a(b, env):
    nc = b.nc
    A, Bp, C, Dp = env["pA"], env["pB"], env["pC"], env["pD"]
    wqkv = b.inhbm("l1a_wqkv")
    wo = b.inhbm("l1a_wo")
    emit_attention(b, env, A, Bp, C, A, wqkv, 0.125, True, "l1a")
    emit_proj_fm(b, wo, 1024, A,
                 lambda e, j, ps: e.activation(Bp[j][:], ps[:], AF.Copy),
                 env["wbuf"])
    return Bp, C, OFF_C




def emit_proj_fm_store(b, w_hbm, n_out, rhs_tiles, act, stage, dst_hbm, wbuf,
                       col_off=0, bias_cols=None):
    nc = b.nc
    n_units = n_out // 256
    mm0, ev0 = b.c_mm.n, b.c_ev.n
    wt = []
    st_entry = [b.st[0].n, b.st[1].n]
    stt = [st_entry[j % 2] + 16 * (j // 2 + 1) for j in range(2 * n_units)]
    with nc.Block() as blk:
        @blk.gpsimd
        def _(g):
            for u in range(n_units):
                if u >= 2:
                    g.wait_ge(b.c_mm.h, mm0 + 2 * (u - 1))
                src = w_hbm[(col_off // 256) + u]
                ws = b.ws[u % 2]
                g.dma_start(wbuf[u % 2][:], src).then_inc(ws.h, 16)
                ws.n += 16
                wt.append(ws.n)
        @blk.tensor
        def _(t):
            for u in range(n_units):
                t.wait_ge(b.ws[u % 2].h, wt[u])
                for mo in range(2):
                    j = u * 2 + mo
                    tgt = b.PA if j % 2 == 0 else b.PB
                    if j >= 2:
                        t.wait_ge(b.c_ev.h, ev0 + j - 1)
                    last = None
                    for tb in range(NB):
                        for kc in range(KC):
                            last = t.matmul(
                                tgt[:, tb * 512:(tb + 1) * 512],
                                wbuf[u % 2][:, kc * 256 + mo * 128: kc * 256 + (mo + 1) * 128],
                                rhs_tiles[kc][:, tb * 512:(tb + 1) * 512],
                                start=(kc == 0), stop=(kc == KC - 1))
                    b.c_mm.inc(last)
        @blk.scalar
        def _(e):
            for j in range(2 * n_units):
                tgt = b.PA if j % 2 == 0 else b.PB
                e.wait_ge(b.c_mm.h, mm0 + j + 1)
                e.wait_ge(b.st[j % 2].h,
                          stt[j - 2] if j >= 2 else st_entry[j % 2])
                if bias_cols is not None:
                    last = e.activation(stage[j % 2][:], tgt[:], act,
                                        bias=bias_cols[j])
                else:
                    last = e.activation(stage[j % 2][:], tgt[:], act)
                b.c_ev.inc(last)
        @blk.sync
        def _(sy):
            for j in range(2 * n_units):
                sy.wait_ge(b.c_ev.h, ev0 + j + 1)
                st = b.st[j % 2]
                sy.dma_start(dst_hbm[j * 128:(j + 1) * 128, :],
                             stage[j % 2][:]).then_inc(st.h, 16)
                st.n += 16
                assert st.n == stt[j]
            sy.wait_ge(b.st[0].h, b.st[0].n)
            sy.wait_ge(b.st[1].h, b.st[1].n)


def emit_l2f(b, env):
    nc = b.nc
    A, Bp, C, Dp = env["pA"], env["pB"], env["pC"], env["pD"]
    wbuf = env["wbuf"]
    X = env["X"]
    ones_col, ones_row, ident = env["ones_col"], env["ones_row"], env["ident"]
    wqkv = b.inhbm("l2f_wqkv")
    wproj = b.inhbm("l2f_wproj")

    emit_attention(b, env, A, Bp, C, A, wqkv, 0.125, False, "l2f")
    emit_proj_fm(b, wproj, 1024, A,
                 lambda e, j, ps: e.activation(Bp[j][:], ps[:], AF.Copy), wbuf)

    with nc.Block() as blk:
        @blk.scalar
        def _(e):
            for c in range(KC):
                e.activation(C[c][:], X[c][:], AF.Square)
    with nc.Block() as blk:
        @blk.vector
        def _(v):
            for c in range(KC):
                v.tensor_copy(A[c][:], X[c][:])
    a_row, b_row = scratch_rows(b, "l2fx", 2)
    emit_stats(b, A, lambda kc: ones_col[:], 1)
    emit_stats(b, C, lambda kc: ones_col[:], 1, dstA=True)
    sc_row = b.sb([1, 2048], F32, OFF_C, "l2f_scrow")
    emit_rowmath_plain(b, b.PB[0:1, 0:2048], b.PA[0:1, 0:2048], sc_row, a_row, b_row)
    emit_bcast2(b, a_row, b_row, env)
    tmp_a = b.sb([128, 2048], F32, OFF_C, "l2f_tma")
    tmp_b = b.sb([128, 2048], F32, OFF_C + 8192, "l2f_tmb")
    with nc.Block() as blk:
        @blk.vector
        def _(v):
            for c in range(KC):
                tmp = tmp_a if c % 2 == 0 else tmp_b
                v.tensor_tensor(tmp[:], X[c][:], b.PA[:], ALU.mult)
                v.drain()
                v.tensor_tensor(tmp[:], tmp[:], b.PB[:], ALU.add)
                v.drain()
                v.tensor_tensor(Bp[c][:], Bp[c][:], tmp[:], ALU.add)
                v.drain()

    emit_ln_stats_rows(b, env, Bp, C, OFF_C, "l2f_ln2")
    tmp_a2 = b.sb([128, 2048], F32, OFF_A, "l2f_tma2")
    tmp_b2 = b.sb([128, 2048], F32, OFF_A + 8192, "l2f_tmb2")
    emit_affine_apply(b, Bp, C, tmp_a2, tmp_b2,
                      gb=(env["gb"]["l2f_g2"], env["gb"]["l2f_b2"]))

    gw = b.sb([128, 64], BF16, OFF_D, "l2f_gw")
    gb_col = b.sb([8, 1], F32, OFF_D + 128, "l2f_gbcol")
    lg = b.sb([8, 2048], F32, OFF_D + 1024, "l2f_lg")
    ltm = b.sb([128, 128], F32, OFF_D + 9216, "l2f_ltm")
    mxt = b.sb([128, 16], F32, OFF_D + 9728, "l2f_mx")
    eqt = b.sb([128, 128], F32, OFF_D + 9856, "l2f_eq")
    idxc = b.sb([128, 16], F32, OFF_D + 10368, "l2f_idxc")
    idxrow = b.sb([1, 2048], BF16, SCRW + 8192, "l2f_idxrow")
    mbs = b.sb([128, 2048], mybir.dt.uint8, OFF_A + 4096, "l2f_mbs")
    iota8 = env["iota8"]
    with nc.Block() as blk:
        @blk.gpsimd
        def _(g):
            b.ld(g, "g", gw[:].rearrange("p (c e) -> p c e", e=8),
                 b.inhbm("l2f_gatew").rearrange("(c p) e -> p c e", p=128),
                 allow_slow_non_contiguous=True)
            b.ld(g, "g", gb_col[:], b.inhbm("l2f_gateb_row").rearrange(
                "one e -> e one"), allow_slow_non_contiguous=True)
            b.wq(g, "g")
    with nc.Block() as blk:
        @blk.tensor
        def _(t):
            for nb in range(NB):
                for kc in range(KC):
                    t.matmul(b.PB[0:8, nb * 512:(nb + 1) * 512],
                             gw[:, kc * 8:(kc + 1) * 8],
                             C[kc][:, nb * 512:(nb + 1) * 512],
                             start=(kc == 0), stop=(kc == KC - 1))
    with nc.Block() as blk:
        @blk.scalar
        def _(e):
            e.activation(lg[:], b.PB[0:8, 0:2048], AF.Identity, bias=gb_col[:])
    mm0, ev0 = b.c_mm.n, b.c_ev.n
    with nc.Block() as blk:
        @blk.tensor
        def _(t):
            for tt in range(16):
                if tt >= 4:
                    t.wait_ge(b.c_ev.h, ev0 + tt - 3)
                i = t.transpose(b.PA[:, (tt % 4) * 512:(tt % 4) * 512 + 8],
                                lg[:, tt * 128:(tt + 1) * 128],
                                env["identf"][0:8, 0:8])
                b.c_mm.inc(i)
        @blk.vector
        def _(v):
            for tt in range(16):
                v.wait_ge(b.c_mm.h, mm0 + tt + 1)
                i = v.tensor_copy(ltm[:, tt * 8:(tt + 1) * 8],
                                  b.PA[:, (tt % 4) * 512:(tt % 4) * 512 + 8])
                b.c_ev.inc(i)
    with nc.Block() as blk:
        @blk.vector
        def _(v):
            v.tensor_reduce(mxt[:], ltm[:].rearrange("p (t e) -> p t e", e=8),
                            AX.X, ALU.max)
            v.drain()
            v.tensor_tensor(eqt[:].rearrange("p (t e) -> p t e", e=8),
                            ltm[:].rearrange("p (t e) -> p t e", e=8),
                            mxt[:].unsqueeze(-1).broadcast_to((128, 16, 8)),
                            ALU.is_ge)
            v.drain()
            v.scalar_tensor_tensor(eqt[:].rearrange("p (t e) -> p t e", e=8),
                                   eqt[:].rearrange("p (t e) -> p t e", e=8),
                                   -99.0,
                                   iota8[:].unsqueeze(1).broadcast_to((128, 16, 8)),
                                   ALU.mult, ALU.add)
            v.drain()
            v.tensor_single_scalar(eqt[:], eqt[:], 99.0, ALU.add)
            v.drain()
            v.tensor_reduce(idxc[:], eqt[:].rearrange("p (t e) -> p t e", e=8),
                            AX.X, ALU.min)
    idx_hbm = nc.dram_tensor("l2f_idxs", [128, 16], F32).ap()
    with nc.Block() as blk:
        @blk.sync
        def _(sy):
            b.ld(sy, "h", idx_hbm, idxc[:])
            b.wq(sy, "h")
    with nc.Block() as blk:
        @blk.gpsimd
        def _(g):
            b.ld(g, "g", idxrow[:].rearrange("one (tt p) -> one tt p", p=128),
                 idx_hbm.rearrange("p tt -> tt p").unsqueeze(0),
                 allow_slow_non_contiguous=True)
            b.wq(g, "g")

    stage = [b.sb([128, 2048], BF16, OFF_A, "l2f_st0"),
             b.sb([128, 2048], BF16, OFF_A + 12288, "l2f_st1")]
    h_hbm = []
    for e in range(8):
        h_hbm.append(nc.dram_tensor(f"l2f_h{e}", [4096, 2048], BF16).ap())
    for e in range(8):
        emit_proj_fm_store(b, b.inhbm(f"l2f_w1_{e}"), 4096, C,
                           AF.Gelu_apprx_tanh, stage, h_hbm[e], wbuf)

    w2s = b.sb([128, 16384], BF16, OFF_D, "l2f_w2s")
    htile = [b.sb([128, 1024], BF16, OFF_A, "l2f_h0t"),
             b.sb([128, 1024], BF16, OFF_A + 2048, "l2f_h1t")]
    for e in range(8):
        with nc.Block() as blk:
            @blk.tensor
            def _(t):
                for nb in range(NB):
                    t.matmul(b.PB[:, nb * 512:(nb + 1) * 512], ones_row[:],
                             idxrow[:, nb * 512:(nb + 1) * 512],
                             start=True, stop=True)
        with nc.Block() as blk:
            @blk.vector
            def _(v, e=e):
                v.tensor_single_scalar(mbs[:], b.PB[:], float(e), ALU.is_equal)
        for mh in range(2):
            with nc.Block() as blk:
                @blk.gpsimd
                def _(g, e=e, mh=mh):
                    b.ld(g, "g", w2s[:], b.inhbm(f"l2f_w2_{e}")[mh])
                    b.wq(g, "g")
            for tb in range(2):
                mm0b, ev0b = b.c_mm.n, b.c_ev.n
                wtl = []
                with nc.Block() as blk:
                    @blk.sync
                    def _(sy, e=e, tb=tb):
                        for kc in range(32):
                            if kc >= 2:
                                sy.wait_ge(b.c_mm.h, mm0b + kc - 1)
                            hp = b.hp[kc % 2]
                            sy.dma_start(
                                htile[kc % 2][:],
                                h_hbm[e][kc * 128:(kc + 1) * 128,
                                         tb * 1024:(tb + 1) * 1024]
                            ).then_inc(hp.h, 16)
                            hp.n += 16
                            wtl.append(hp.n)
                    @blk.tensor
                    def _(t, mh=mh):
                        for kc in range(32):
                            t.wait_ge(b.hp[kc % 2].h, wtl[kc])
                            last = None
                            for m in range(4):
                                tgt = b.PA if m < 2 else b.PB
                                off = (m % 2) * 1024
                                for nn in range(2):
                                    last = t.matmul(
                                        tgt[:, off + nn * 512: off + (nn + 1) * 512],
                                        w2s[:, kc * 512 + m * 128: kc * 512 + (m + 1) * 128],
                                        htile[kc % 2][:, nn * 512:(nn + 1) * 512],
                                        start=(kc == 0), stop=(kc == 31))
                            b.c_mm.inc(last)
                with nc.Block() as blk:
                    @blk.vector
                    def _(v, e=e, mh=mh, tb=tb):
                        for m in range(4):
                            tgt = b.PA if m < 2 else b.PB
                            off = (m % 2) * 1024
                            oc = mh * 4 + m
                            v.copy_predicated(
                                C[oc][:, tb * 1024:(tb + 1) * 1024],
                                mbs[:, tb * 1024:(tb + 1) * 1024],
                                tgt[:, off:off + 1024])
                            v.drain()

    with nc.Block() as blk:
        @blk.vector
        def _(v):
            for c in range(KC):
                v.tensor_tensor(C[c][:], C[c][:], Bp[c][:], ALU.add)
    return C, Bp, OFF_B






def emit_l3a(b, env):
    nc = b.nc
    A, Bp, C, Dp = env["pA"], env["pB"], env["pC"], env["pD"]
    wbuf = env["wbuf"]
    orow = env["ones_row"]

    q_hbm = nc.dram_tensor("l3a_qs", [4096, 2048], BF16).ap()
    k_hbm = nc.dram_tensor("l3a_ks", [4096, 2048], BF16).ap()
    v_hbm = nc.dram_tensor("l3a_vs", [4096, 2048], BF16).ap()
    o_hbm = nc.dram_tensor("l3a_os", [4096, 2048], BF16).ap()
    wo_bf = nc.dram_tensor("l3a_wobf", [4096, 1024], BF16).ap()

    ph = b.sb([128, 32], F32, SCRW + 8192, "l3a_ph")
    dif = b.sb([128, 128], F32, SCRW + 8320, "l3a_dif")
    wcol = b.sb([128, 128], BF16, SCRW + 8832, "l3a_wcol")
    with nc.Block() as blk:
        @blk.sync
        def _(sy):
            b.ld(sy, "h", ph[:].rearrange("p (h c) -> p h c", c=8),
                 b.inhbm("l3a_phase").rearrange("h (c p) -> p h c", p=128),
                 allow_slow_non_contiguous=True)
            b.wq(sy, "h")
    with nc.Block() as blk:
        @blk.vector
        def _(v):
            for hh in range(4):
                for gg in range(4):
                    p = hh * 4 + gg
                    v.tensor_tensor(dif[:, p * 8:(p + 1) * 8],
                                    ph[:, hh * 8:(hh + 1) * 8],
                                    ph[:, gg * 8:(gg + 1) * 8], ALU.subtract)
    with nc.Block() as blk:
        @blk.scalar
        def _(e):
            e.activation(dif[:], dif[:], AF.Sin, bias=float(np.pi / 2))
    with nc.Block() as blk:
        @blk.vector
        def _(v):
            v.tensor_scalar_mul(wcol[:], dif[:], 1.0 / 32.0)

    stage = [b.sb([128, 2048], BF16, OFF_B, "l3a_st0"),
             b.sb([128, 2048], BF16, OFF_B + 4096, "l3a_st1")]
    for w_name, bias_name, dst in (("l3a_wq", "l3a_bq", q_hbm),
                                   ("l3a_wk", "l3a_bk", k_hbm),
                                   ("l3a_wv", "l3a_bv", v_hbm)):
        emit_proj_fm_store(b, b.inhbm(w_name), 4096, A, AF.Identity, stage,
                           dst, wbuf, bias_cols=env["gb"][bias_name])
    with nc.Block() as blk:
        @blk.gpsimd
        def _(g):
            b.ld(g, "g", wo_bf, b.inhbm("l3a_wo")[:])
            b.wq(g, "g")

    u8 = [b.sb([128, 2048], BF16, OFF_D, "l3a_u8a"),
          b.sb([128, 2048], BF16, OFF_D + 4096, "l3a_u8b")]
    expr = [b.sb([1, 256], BF16, OFF_D + 8192 + i * 512, f"l3a_ex{i}")
            for i in range(16)]
    denr = [b.sb([1, 256], F32, OFF_D + 16384 + i * 1024, f"l3a_den{i}")
            for i in range(4)]
    rdenr = [b.sb([1, 256], BF16, OFF_D + 20480 + i * 512, f"l3a_rd{i}")
             for i in range(4)]
    abs_t = [b.sb([128, 256], BF16, OFF_D + 22528, "l3a_ab0"),
             b.sb([128, 256], BF16, OFF_D + 23040, "l3a_ab1")]
    tmpc = b.sb([128, 256], BF16, OFF_D + 23552, "l3a_tmpc")

    def sub(base_pool, base, fc, tw=256):
        per = 2048 // tw
        return base_pool[base + fc // per][:, (fc % per) * tw:(fc % per + 1) * tw]

    for tc in range(8):
        t0c = tc * 256
        with nc.Block() as blk:
            @blk.sync
            def _(sy, t0c=t0c):
                for i in range(4):
                    b.ld(sy, "h",
                         Bp[i][:].rearrange("p (s t) -> p s t", t=256),
                         q_hbm[i * 1024:(i + 1) * 1024, t0c:t0c + 256]
                         .rearrange("(s p) t -> p s t", p=128))
                    b.ld(sy, "h",
                         Bp[4 + i][:].rearrange("p (s t) -> p s t", t=256),
                         k_hbm[i * 1024:(i + 1) * 1024, t0c:t0c + 256]
                         .rearrange("(s p) t -> p s t", p=128))
                    b.ld(sy, "h",
                         C[i][:].rearrange("p (s t) -> p s t", t=256),
                         v_hbm[i * 1024:(i + 1) * 1024, t0c:t0c + 256]
                         .rearrange("(s p) t -> p s t", p=128))
                b.wq(sy, "h")
        mm0, ev0, dv0 = b.c_mm.n, b.c_ev.n, b.c_dv.n
        with nc.Block() as blk:
            @blk.vector
            def _(v):
                for p in range(16):
                    if p >= 2:
                        v.wait_ge(b.c_mm.h, mm0 + p - 1)
                    hh, gg = p // 4, p % 4
                    last = None
                    for c in range(KC):
                        last = v.tensor_tensor(
                            u8[p % 2][:, c * 256:(c + 1) * 256],
                            sub(Bp, 0, hh * 8 + c),
                            sub(Bp, 4, gg * 8 + c), ALU.mult)
                    b.c_dv.inc(last)
            @blk.tensor
            def _(t):
                for p in range(16):
                    t.wait_ge(b.c_dv.h, dv0 + p + 1)
                    if p >= 2:
                        t.wait_ge(b.c_ev.h, ev0 + p - 1)
                    last = None
                    for c in range(KC):
                        last = t.matmul(
                            b.PB[0:1, (p % 2) * 512:(p % 2) * 512 + 256],
                            wcol[:, p * 8 + c: p * 8 + c + 1],
                            u8[p % 2][:, c * 256:(c + 1) * 256],
                            start=(c == 0), stop=(c == KC - 1))
                    b.c_mm.inc(last)
            @blk.scalar
            def _(e):
                for p in range(16):
                    e.wait_ge(b.c_mm.h, mm0 + p + 1)
                    last = e.activation(expr[p][:],
                                        b.PB[0:1, (p % 2) * 512:(p % 2) * 512 + 256],
                                        AF.Exp, bias=-20.0)
                    b.c_ev.inc(last)
        with nc.Block() as blk:
            @blk.vector
            def _(v):
                for hh in range(4):
                    v.tensor_tensor(denr[hh][:], expr[4 * hh][:],
                                    expr[4 * hh + 1][:], ALU.add)
                    v.drain()
                    v.tensor_tensor(denr[hh][:], denr[hh][:],
                                    expr[4 * hh + 2][:], ALU.add)
                    v.drain()
                    v.tensor_tensor(denr[hh][:], denr[hh][:],
                                    expr[4 * hh + 3][:], ALU.add)
                    v.drain()
                    with nc.allow_low_precision("l3a rden"):
                        v.reciprocal(rdenr[hh][:], denr[hh][:])
                    v.drain()
                    for gg in range(4):
                        v.tensor_tensor(expr[4 * hh + gg][:],
                                        expr[4 * hh + gg][:],
                                        rdenr[hh][:], ALU.mult)
                        v.drain()
        mm0c, ev0c, dv0c = b.c_mm.n, b.c_ev.n, b.c_dv.n
        with nc.Block() as blk:
            @blk.tensor
            def _(t):
                for p in range(16):
                    if p >= 2:
                        t.wait_ge(b.c_ev.h, ev0c + p - 1)
                    i = t.matmul(b.PA[0:128, (p % 2) * 512:(p % 2) * 512 + 256],
                                 orow[:], expr[p][:], start=True, stop=True)
                    b.c_mm.inc(i)
            @blk.scalar
            def _(e):
                for p in range(16):
                    e.wait_ge(b.c_mm.h, mm0c + p + 1)
                    if p >= 2:
                        e.wait_ge(b.c_dv.h, dv0c + p - 1)
                    i = e.activation(abs_t[p % 2][:],
                                     b.PA[0:128, (p % 2) * 512:(p % 2) * 512 + 256],
                                     AF.Copy)
                    b.c_ev.inc(i)
            @blk.vector
            def _(v):
                for p in range(16):
                    hh, gg = p // 4, p % 4
                    v.wait_ge(b.c_ev.h, ev0c + p + 1)
                    last = None
                    for c in range(KC):
                        if gg == 0:
                            last = v.tensor_tensor(sub(C, 4, hh * 8 + c),
                                                   sub(C, 0, gg * 8 + c),
                                                   abs_t[p % 2][:], ALU.mult)
                        else:
                            v.tensor_tensor(tmpc[:], sub(C, 0, gg * 8 + c),
                                            abs_t[p % 2][:], ALU.mult)
                            v.drain()
                            last = v.tensor_tensor(sub(C, 4, hh * 8 + c),
                                                   sub(C, 4, hh * 8 + c),
                                                   tmpc[:], ALU.add)
                            v.drain()
                    b.c_dv.inc(last)
        with nc.Block() as blk:
            @blk.sync
            def _(sy, t0c=t0c):
                for i in range(4):
                    b.ld(sy, "h",
                         o_hbm[i * 1024:(i + 1) * 1024, t0c:t0c + 256]
                         .rearrange("(s p) t -> p s t", p=128),
                         C[4 + i][:].rearrange("p (s t) -> p s t", t=256))
                b.wq(sy, "h")

    bo_cols = env["gb"]["l3a_bo"]
    ot = [b.sb([128, 2048], BF16, OFF_B + i * 4096, f"l3a_ot{i}") for i in range(2)]
    for hh in range(4):
        mm0w, ev0w = b.c_mm.n, b.c_ev.n
        hp0 = [b.hp[0].n, b.hp[1].n]
        wt0 = [b.ws[0].n, b.ws[1].n]
        with nc.Block() as blk:
            @blk.sync
            def _(sy, hh=hh):
                for kc in range(KC):
                    if kc >= 2:
                        sy.wait_ge(b.c_mm.h, mm0w + 4 * (kc - 1))
                    hp = b.hp[kc % 2]
                    sy.dma_start(ot[kc % 2][:],
                                 o_hbm[hh * 1024 + kc * 128: hh * 1024 + (kc + 1) * 128, :]
                                 ).then_inc(hp.h, 16)
                    hp.n += 16
            @blk.gpsimd
            def _(g, hh=hh):
                i = 0
                for kc in range(KC):
                    for r in range(4):
                        if i >= 2:
                            g.wait_ge(b.c_mm.h, mm0w + i - 1)
                        ws = b.ws[i % 2]
                        g.dma_start(
                            wbuf[i % 2][:, 0:1024],
                            wo_bf[r * 1024 + kc * 128: r * 1024 + (kc + 1) * 128, :]
                        ).then_inc(ws.h, 16)
                        ws.n += 16
                        i += 1
            @blk.tensor
            def _(t, hh=hh):
                i = 0
                for kc in range(KC):
                    t.wait_ge(b.hp[kc % 2].h, hp0[kc % 2] + 16 * (kc // 2 + 1))
                    for r in range(4):
                        t.wait_ge(b.ws[i % 2].h, wt0[i % 2] + 16 * (i // 2 + 1))
                        rhs = ot[kc % 2][:].rearrange("p (a r) -> p r a", r=4)[:, r, :]
                        last = None
                        for m in range(8):
                            tgt = b.PA if m < 4 else b.PB
                            last = t.matmul(
                                tgt[:, (m % 4) * 512:(m % 4) * 512 + 512],
                                wbuf[i % 2][:, m * 128:(m + 1) * 128],
                                rhs,
                                start=(kc == 0 and r == 0),
                                stop=(kc == KC - 1 and r == 3))
                        b.c_mm.inc(last)
                        i += 1
        with nc.Block() as blk:
            @blk.scalar
            def _(e, hh=hh):
                for m in range(8):
                    tgt = b.PA if m < 4 else b.PB
                    e.activation(A[m][:, hh * 512:(hh + 1) * 512],
                                 tgt[:, (m % 4) * 512:(m % 4) * 512 + 512],
                                 AF.Identity, bias=bo_cols[m])
    return A, Bp, OFF_B




L2A_UPTO = 4


def emit_l2a(b, env):
    nc = b.nc
    A, Bp, C, Dp = env["pA"], env["pB"], env["pC"], env["pD"]
    wbuf = env["wbuf"]
    e2, bo16 = env["e2"], env["bo16all"]
    wqkv = b.inhbm("l2a_wqkv")
    wout = b.inhbm("l2a_wout")

    cc_in = nc.dram_tensor("l2a_ccin", [2048, 2048], BF16).ap()
    cc_out = nc.dram_tensor("l2a_ccout", [16384, 2048], BF16,
                            addr_space="Shared").ap()

    stage = [b.sb([128, 2048], BF16, OFF_D, "l2a_st0"),
             b.sb([128, 2048], BF16, OFF_D + 4096, "l2a_st1")]
    emit_proj_fm_store(b, wqkv, 1024, Bp, AF.Copy, stage, cc_in[0:1024, :],
                       wbuf, col_off=1024)
    emit_proj_fm_store(b, wqkv, 1024, Bp, AF.Copy, stage, cc_in[1024:2048, :],
                       wbuf, col_off=2048)
    with nc.Block() as blk:
        @blk.gpsimd
        def _(g):
            g.wait_ge(b.st[0].h, b.st[0].n)
            g.wait_ge(b.st[1].h, b.st[1].n)
            g.collective_compute(
                "AllGather", ALU.bypass, replica_groups=[list(range(8))],
                ins=[cc_in.opt()], outs=[cc_out.opt()]).then_inc(b.c_pl.h, 1)
            b.c_pl.n += 1
    ag_done = b.c_pl.n
    emit_proj_fm(b, wqkv, 1024, Bp,
                 lambda e, j, ps: e.activation(C[j][:], ps[:], AF.Copy,
                                               scale=1.0 / 32.0), wbuf, col_off=0)

    kj = [b.sb([128, 2048], BF16, OFF_D, "l2a_kj0"),
          b.sb([128, 2048], BF16, OFF_D + 4096, "l2a_kj1")]
    u2 = [b.sb([128, 2048], BF16, OFF_D + 8192, "l2a_u0"),
          b.sb([128, 2048], BF16, OFF_D + 12288, "l2a_u1")]
    tmpd = b.sb([128, 2048], BF16, OFF_D + 16384, "l2a_tmp")
    rden16 = b.sb([16, 2048], BF16, OFF_D + 20480, "l2a_rden")
    den16 = b.sb([16, 2048], F32, OFF_D + 24576, "l2a_den")
    rdp = [b.sb([2, 2048], BF16, SCRW + 8192, "l2a_rdp0"),
           b.sb([2, 2048], BF16, OFF_D + 8192, "l2a_rdp1")]

    for j in range(8 if L2A_UPTO >= 2 else 0):
        hp0 = [b.hp[0].n, b.hp[1].n]
        mm0, ev0, dv0 = b.c_mm.n, b.c_ev.n, b.c_dv.n
        with nc.Block() as blk:
            @blk.sync
            def _(sy, j=j):
                sy.wait_ge(b.c_pl.h, ag_done)
                for c in range(KC):
                    if c >= 2:
                        sy.wait_ge(b.c_dv.h, dv0 + c - 1)
                    hp = b.hp[c % 2]
                    sy.dma_start(kj[c % 2][:],
                                 cc_out[j * 2048 + c * 128: j * 2048 + (c + 1) * 128, :]
                                 ).then_inc(hp.h, 16)
                    hp.n += 16
            @blk.vector
            def _(v, j=j):
                for c in range(KC):
                    v.wait_ge(b.hp[c % 2].h, hp0[c % 2] + 16 * (c // 2 + 1))
                    if c >= 2:
                        v.wait_ge(b.c_mm.h, mm0 + c - 1)
                    i = v.tensor_tensor(u2[c % 2][:], C[c][:], kj[c % 2][:],
                                        ALU.mult)
                    b.c_dv.inc(i)
            @blk.tensor
            def _(t, j=j):
                for c in range(KC):
                    t.wait_ge(b.c_dv.h, dv0 + c + 1)
                    last = None
                    for nb in range(NB):
                        last = t.matmul(
                            b.PB[0:16, nb * 512:(nb + 1) * 512],
                            bo16[:, c * 16:(c + 1) * 16],
                            u2[c % 2][:, nb * 512:(nb + 1) * 512],
                            start=(c == 0), stop=(c == KC - 1))
                    b.c_mm.inc(last)
            @blk.scalar
            def _(e, j=j):
                e.wait_ge(b.c_mm.h, mm0 + KC)
                e.activation(Bp[j][0:16, :], b.PB[0:16, 0:2048], AF.Exp,
                             bias=-20.0)

    if L2A_UPTO < 2:
        return A, Bp, OFF_B
    with nc.Block() as blk:
        @blk.vector
        def _(v):
            v.tensor_tensor(den16[:], Bp[0][0:16, :], Bp[1][0:16, :], ALU.add)
            for j in range(2, 8):
                v.drain()
                v.tensor_tensor(den16[:], den16[:], Bp[j][0:16, :], ALU.add)
            v.drain()
            with nc.allow_low_precision("l2a rden"):
                v.reciprocal(rden16[:], den16[:])

    for c in range(KC):
        if L2A_UPTO < 3:
            break
        hp0 = [b.hp[0].n, b.hp[1].n]
        st0l = [b.st[0].n, b.st[1].n]
        mm0, ev0, dv0, pl0 = b.c_mm.n, b.c_ev.n, b.c_dv.n, b.c_pl.n
        with nc.Block() as blk:
            @blk.sync
            def _(sy, c=c):
                for j in range(8):
                    if j >= 2:
                        sy.wait_ge(b.c_dv.h, dv0 + j - 1)
                    hp = b.hp[j % 2]
                    sy.dma_start(kj[j % 2][:],
                                 cc_out[j * 2048 + 1024 + c * 128:
                                        j * 2048 + 1024 + (c + 1) * 128, :]
                                 ).then_inc(hp.h, 16)
                    hp.n += 16
                    st = b.st[j % 2]
                    sy.dma_start(rdp[j % 2][:],
                                 Bp[j][2 * c:2 * c + 2, :]).then_inc(st.h, 16)
                    st.n += 16
            @blk.tensor
            def _(t, c=c):
                for j in range(8):
                    t.wait_ge(b.st[j % 2].h, st0l[j % 2] + 16 * (j // 2 + 1))
                    if j >= 2:
                        t.wait_ge(b.c_dv.h, dv0 + j - 1)
                    tgt = b.PA if j % 2 == 0 else b.PB
                    last = None
                    for nb in range(NB):
                        last = t.matmul(tgt[:, nb * 512:(nb + 1) * 512],
                                        e2[:], rdp[j % 2][:, nb * 512:(nb + 1) * 512],
                                        start=True, stop=True)
                    b.c_mm.inc(last)
            @blk.vector
            def _(v, c=c):
                for j in range(8):
                    v.wait_ge(b.c_mm.h, mm0 + j + 1)
                    v.wait_ge(b.hp[j % 2].h, hp0[j % 2] + 16 * (j // 2 + 1))
                    tgt = b.PA if j % 2 == 0 else b.PB
                    if j == 0:
                        i = v.tensor_tensor(C[c][:], kj[0][:], tgt[:], ALU.mult)
                    else:
                        v.tensor_tensor(tmpd[:], kj[j % 2][:], tgt[:], ALU.mult)
                        v.drain()
                        i = v.tensor_tensor(C[c][:], C[c][:], tmpd[:], ALU.add)
                    v.drain()
                    b.c_dv.inc(i)
        with nc.Block() as blk:
            @blk.sync
            def _(sy, c=c):
                b.ld(sy, "h", rdp[0][:], rden16[2 * c:2 * c + 2, :])
                b.wq(sy, "h")
        with nc.Block() as blk:
            @blk.tensor
            def _(t):
                for nb in range(NB):
                    t.matmul(b.PA[:, nb * 512:(nb + 1) * 512], e2[:],
                             rdp[0][:, nb * 512:(nb + 1) * 512],
                             start=True, stop=True)
        with nc.Block() as blk:
            @blk.vector
            def _(v, c=c):
                v.tensor_tensor(C[c][:], C[c][:], b.PA[:], ALU.mult)

    if L2A_UPTO < 4:
        return A, Bp, OFF_B
    emit_proj_fm(b, wout, 1024, C,
                 lambda e, j, ps: e.tensor_tensor(A[j][:], ps[:], A[j][:], ALU.add),
                 wbuf, evac_eng="vector")
    return A, Bp, OFF_B


NEED_Y = {
    "l0a": [("plain", "pA")],
    "l0f": [("plain", "pA")],
    "l1a": [("affine", "pA", "l1a_g", "l1a_b")],
    "l1f": [("center",)],
    "l2a": [("plain", "pA"), ("affine", "pB", "l2a_g", "l2a_b")],
    "l2f": [("affine", "pA", "l2f_g1", "l2f_b1")],
    "l3a": [("plain", "pA")],
    "l3f": [("plain", "pA")],
}

GB_ROWS = {"l0a_g": "l0a_g_row", "l0a_b": "l0a_b_row",
           "l1f_bias": "l1f_bias_row",
           "l3a_bq": "l3a_bq_row", "l3a_bk": "l3a_bk_row",
           "l3a_bv": "l3a_bv_row", "l3a_bo": "l3a_bo_row",
           "l0f_mur": "l0f_mur_row", "l0f_muk": "l0f_muk_row",
           "l3f_mur": "l3f_mur_row", "l3f_muk": "l3f_muk_row",
           "l1a_g": "l1a_g_row", "l1a_b": "l1a_b_row",
           "l2a_g": "l2a_g_row", "l2a_b": "l2a_b_row",
           "l2f_g1": "l2f_g1_row", "l2f_b1": "l2f_b1_row",
           "l2f_g2": "l2f_g2_row", "l2f_b2": "l2f_b2_row"}

EMITTERS = {}
EMITTERS["l0a"] = emit_l0a
EMITTERS["l0f"] = emit_l0f
EMITTERS["l3f"] = emit_l3f
EMITTERS["l1f"] = emit_l1f
EMITTERS["l1a"] = emit_l1a
EMITTERS["l2f"] = emit_l2f
EMITTERS["l3a"] = emit_l3a
EMITTERS["l2a"] = emit_l2a


def yspecs(env, st):
    out = []
    for sp in NEED_Y[st]:
        if sp[0] == "plain":
            out.append(("plain", env[sp[1]]))
        elif sp[0] == "center":
            out.append(("center",))
        else:
            out.append(("affine", env[sp[1]], env["gb"][sp[2]], env["gb"][sp[3]]))
    return out


def build_nc(arrays, steps):
    b = Bld(arrays)
    nc = b.nc
    x_in = nc.dram_tensor("x_in", [1024, 2048], F32, kind="ExternalInput").ap()
    b.used.append("x_in")
    x_out = nc.dram_tensor("x_out", [1024, 2048], F32, kind="ExternalOutput").ap()

    env = {}
    env["X"] = [b.sb([128, 2048], F32, OFF_X + c * 8192, f"X{c}") for c in range(KC)]
    for nm, off in (("pA", OFF_A), ("pB", OFF_B), ("pC", OFF_C), ("pD", OFF_D)):
        env[nm] = [b.sb([128, 2048], BF16, off + c * 4096, f"{nm}{c}") for c in range(KC)]
    env["wbuf"] = [b.sb([128, 2048], BF16, SCRW, "wb0"),
                   b.sb([128, 2048], BF16, SCRW + 4096, "wb1")]
    env["ones_col"] = b.sb([128, 1], BF16, OFF_CONST, "ones_col")
    env["biases"] = b.sb([128, 7], F32, OFF_CONST + 1856, "biases")
    for bi, bval in enumerate([1e-5, 1e-8, float(np.pi / 2), 1.0, 0.5, -30.0, -20.0]):
        nc.const_aps.register(F32, bval, env["biases"][:, bi:bi + 1])
    env["ones_row"] = b.sb([1, 128], BF16, OFF_CONST + 32, "ones_row")
    env["e2"] = b.sb([2, 128], BF16, OFF_CONST + 1056, "e2")
    env["ones64"] = b.sb([65, 128], BF16, OFF_CONST + 2816, "ones64")
    env["ident"] = b.sb([128, 128], BF16, OFF_CONST + 288, "ident")
    if "l1f" in steps:
        env["yc"] = [b.sb([128, 2048], F32, OFF_A + c * 8192, f"yc{c}")
                     for c in range(KC)]
        env["l1f_inv_hbm"] = nc.dram_tensor("l1f_invh", [1, 2048], BF16).ap()
    env["tri"] = b.sb([128, 128], BF16, OFF_CONST + 544, "tri")
    env["iota8"] = b.sb([128, 8], F32, OFF_CONST + 1312, "iota8")
    env["identf"] = b.sb([128, 128], F32, OFF_CONST + 1344, "identf")
    env["bo16all"] = b.sb([128, 128], BF16, OFF_CONST + 800, "bo16all")

    env["gb"] = {}
    gb_needed = []
    for st in steps:
        for sp in NEED_Y.get(st, []):
            if sp[0] == "affine":
                gb_needed += [sp[2], sp[3]]
    if "l0a" in steps:
        gb_needed += ["l0a_g", "l0a_b"]
    if "l0f" in steps:
        gb_needed += ["l0f_mur", "l0f_muk"]
    if "l3f" in steps:
        gb_needed += ["l3f_mur", "l3f_muk"]
    if "l1f" in steps:
        gb_needed += ["l1f_bias"]
    if "l3a" in steps:
        gb_needed += ["l3a_bq", "l3a_bk", "l3a_bv", "l3a_bo"]
    if "l2f" in steps:
        gb_needed += ["l2f_g2", "l2f_b2"]
    gb_off = OFF_CONST + 1888
    gb_tiles = {}
    for gname in dict.fromkeys(gb_needed):
        wcols = b_arr_len(arrays := None, GB_ROWS[gname]) if False else None
        import numpy as _np
        wid = int(b.arrays[GB_ROWS[gname]].shape[1]) // 128
        tile = b.sb([128, wid], F32, gb_off, f"gb_{gname}")
        gb_tiles[gname] = tile
        env["gb"][gname] = [tile[:, c:c + 1] for c in range(wid)]
        gb_off += 4 * wid
    assert gb_off <= OFF_CONST + 3072, gb_off

    with nc.Block() as blk:
        @blk.sync
        def _(s):
            for c in range(KC):
                b.ld(s, "h", env["X"][c][:], x_in[c * 128:(c + 1) * 128, :])
            b.ld(s, "h", env["ones_col"][:], b.inhbm("c_ones_col")[:])
            b.ld(s, "h", env["biases"][:], b.inhbm("c_biases")[:])
            b.ld(s, "h", env["ones_row"][:], b.inhbm("c_ones_row")[:])
            b.ld(s, "h", env["ident"][:], b.inhbm("c_ident")[:])
            b.ld(s, "h", env["tri"][:], b.inhbm("c_tri")[:])
            b.ld(s, "h", env["iota8"][:], b.inhbm("c_iota8")[:])
            b.ld(s, "h", env["identf"][:], b.inhbm("c_identf")[:])
            b.ld(s, "h", env["bo16all"][:], b.inhbm("c_bo16all")[:])
            b.ld(s, "h", env["e2"][:], b.inhbm("c_e2")[:])
            b.ld(s, "h", env["ones64"][:], b.inhbm("c_ones64")[:])
            for gname, tile in gb_tiles.items():
                src = b.inhbm(GB_ROWS[gname]).rearrange("one (c p) -> (one p) c", p=128)
                b.ld(s, "h", tile[:], src, allow_slow_non_contiguous=True)
            b.wq(s, "h")

    first_specs = yspecs(env, steps[0]) if steps else []
    emit_boundary(b, env, None, env["pD"], OFF_D, first_specs, "binit")

    for i, st in enumerate(steps):
        z_pool, sq_pool, sq_off = EMITTERS[st](b, env)
        nxt = yspecs(env, steps[i + 1]) if i + 1 < len(steps) else []
        emit_boundary(b, env, z_pool, sq_pool, sq_off, nxt, f"b_{st}")

    with nc.Block() as blk:
        @blk.sync
        def _(s):
            for c in range(KC):
                b.ld(s, "h", x_out[c * 128:(c + 1) * 128, :], env["X"][c][:])
            b.wq(s, "h")
    return b, env


def run_steps(arrays, x_fm_percore, steps, trace=False):
    b, env = build_nc(arrays, steps)
    in_maps = []
    for c in range(len(x_fm_percore)):
        m = {"x_in": x_fm_percore[c]}
        for nm in b.used:
            if nm != "x_in":
                m[nm] = arrays[nm]
        in_maps.append(m)
    res = run_bass_kernel_spmd(b.nc, in_maps, core_ids=list(range(len(in_maps))),
                               trace=trace)
    return [r["x_out"] for r in res.results], res


STEPS_ALL = ["l0a", "l0f", "l1a", "l1f", "l2a", "l2f", "l3a", "l3f"]


def kernel(**inputs):
    arrays = _prep(inputs)
    x = np.asarray(inputs["L0_attn_inputs"], dtype=np.float32)
    x_fm = [np.ascontiguousarray(x[c].T) for c in range(8)]
    outs, _ = run_steps(arrays, x_fm, STEPS_ALL)
    return np.stack([o.T for o in outs], axis=0)




# version 05
# speedup vs baseline: 1.1620x; 1.0355x over previous
import numpy as np
import ml_dtypes
import concourse.bass as bass
import concourse.mybir as mybir
from concourse.bass_utils import run_bass_kernel_spmd

F32 = mybir.dt.float32
BF16 = mybir.dt.bfloat16
AF = mybir.ActivationFunctionType
ALU = mybir.AluOpType
AX = mybir.AxisListType

D = 1024
T = 2048
KC = 8
NB = 4
EPS = 1e-5
BF = ml_dtypes.bfloat16

BASE = 16512
OFF_X = BASE
OFF_A = BASE + 65536
OFF_B = BASE + 98304
OFF_C = BASE + 131072
OFF_D = BASE + 163840
OFF_CONST = BASE + 196608 + 512
SCRW = BASE + 196608 + 512 + 3072
SBUF_END = 229344



def _reorder_qkv_ed(W):
    Wt = np.ascontiguousarray(np.asarray(W, dtype=np.float32).T).reshape(1024, 16, 3, 64)
    q = Wt[:, :, 0, :].reshape(1024, 1024)
    k = Wt[:, :, 1, :].reshape(1024, 1024)
    v = Wt[:, :, 2, :].reshape(1024, 1024)
    return np.ascontiguousarray(np.concatenate([q, k, v], axis=1))


def _unitize(W):
    W = np.asarray(W, dtype=np.float32)
    d, n = W.shape
    assert d == 1024 and n % 256 == 0
    Wr = W.reshape(8, 128, n // 256, 256)
    out = Wr.transpose(2, 1, 0, 3).reshape(n // 256, 128, 2048)
    return np.ascontiguousarray(out)


def _prep(inp):
    f = lambda a: np.ascontiguousarray(np.asarray(a, dtype=np.float32))
    row = lambda a: np.ascontiguousarray(np.asarray(a, dtype=np.float32).reshape(1, -1))
    w = {}
    w["l0a_wq"] = _unitize(f(inp["L0_attn_W_Q"]))
    w["l0a_wk"] = _unitize(f(inp["L0_attn_W_K"]))
    w["l0a_wv"] = _unitize(f(inp["L0_attn_W_V"]))
    w["l0a_wfc"] = _unitize(f(inp["L0_attn_W_fc"]))
    w["l0a_g_row"] = row(inp["L0_attn_gamma"])
    w["l0a_b_row"] = row(inp["L0_attn_beta"])
    w["l0f_wr"] = _unitize(f(np.asarray(inp["L0_ffn_W_r"]).T))
    w["l0f_wk"] = _unitize(f(np.asarray(inp["L0_ffn_W_k"]).T))
    w["l0f_wv"] = _unitize(f(np.asarray(inp["L0_ffn_W_v"]).T))
    w["l0f_mur_row"] = row(inp["L0_ffn_mu_r"])
    w["l0f_muk_row"] = row(inp["L0_ffn_mu_k"])
    w["l1a_g_row"] = row(inp["L1_attn_ln_gamma"])
    w["l1a_b_row"] = row(inp["L1_attn_ln_beta"])
    w["l1a_wqkv"] = _unitize(_reorder_qkv_ed(inp["L1_attn_W_qkv"]))
    w["l1a_wo"] = _unitize(f(np.asarray(inp["L1_attn_W_o"]).T))
    w["l1f_w"] = f(inp["L1_ffn_weight"])
    w["l1f_bias_row"] = row(inp["L1_ffn_bias"])
    w["l2a_wqkv"] = _unitize(_reorder_qkv_ed(inp["L2_attn_w_qkv"]))
    w["l2a_wout"] = _unitize(f(np.asarray(inp["L2_attn_w_out"]).T))
    w["l2a_g_row"] = row(inp["L2_attn_ln_w"])
    w["l2a_b_row"] = row(inp["L2_attn_ln_b"])
    w["l2f_g1_row"] = row(inp["L2_ffn_ln1_w"])
    w["l2f_b1_row"] = row(inp["L2_ffn_ln1_b"])
    w["l2f_wqkv"] = _unitize(f(inp["L2_ffn_w_qkv"]))
    w["l2f_wproj"] = _unitize(f(inp["L2_ffn_w_proj"]))
    w["l2f_g2_row"] = row(inp["L2_ffn_ln2_w"])
    w["l2f_b2_row"] = row(inp["L2_ffn_ln2_b"])
    w["l2f_gatew"] = f(inp["L2_ffn_gate_w"])
    w["l2f_gateb_row"] = row(inp["L2_ffn_gate_b"])
    for e in range(8):
        w[f"l2f_w1_{e}"] = _unitize(f(np.asarray(inp["L2_ffn_w1"])[e]))
        w2_ = np.asarray(inp["L2_ffn_w2"], dtype=np.float32)[e]
        w2r = w2_.reshape(32, 128, 2, 512).transpose(2, 1, 0, 3)
        w[f"l2f_w2_{e}"] = np.ascontiguousarray(w2r.reshape(2, 128, 32 * 512))
    w["l3a_phase"] = f(inp["L3_attn_phase"])
    w["l3a_wq"] = _unitize(f(inp["L3_attn_Wq"]))
    w["l3a_wk"] = _unitize(f(inp["L3_attn_Wk"]))
    w["l3a_wv"] = _unitize(f(inp["L3_attn_Wv"]))
    w["l3a_bq_row"] = row(inp["L3_attn_bq"])
    w["l3a_bk_row"] = row(inp["L3_attn_bk"])
    w["l3a_bv_row"] = row(inp["L3_attn_bv"])
    w["l3a_wo"] = f(inp["L3_attn_Wo"])
    w["l3a_bo_row"] = row(inp["L3_attn_bo"])
    w["l3f_wr"] = _unitize(f(np.asarray(inp["L3_ffn_W_r"]).T))
    w["l3f_wk"] = _unitize(f(np.asarray(inp["L3_ffn_W_k"]).T))
    w["l3f_wv"] = _unitize(f(np.asarray(inp["L3_ffn_W_v"]).T))
    w["l3f_mur_row"] = row(inp["L3_ffn_mu_r"])
    w["l3f_muk_row"] = row(inp["L3_ffn_mu_k"])
    BIAS_VALS = [1e-5, 1e-8, float(np.pi / 2), 1.0, 0.5, -30.0, -20.0]
    w["c_biases"] = np.tile(np.asarray(BIAS_VALS, dtype=np.float32)[None, :], (128, 1))
    w["c_ones_row"] = np.ones((1, 128), dtype=BF)
    w["c_ones_col"] = np.ones((128, 1), dtype=BF)
    w["c_ident"] = np.eye(128, dtype=BF)
    w["c_identf"] = np.eye(128, dtype=np.float32)
    w["c_tri"] = (np.arange(128)[None, :] >= np.arange(128)[:, None]).astype(BF)
    w["c_iota8"] = np.tile(np.arange(8, dtype=np.float32)[None, :], (128, 1))
    bo16 = np.zeros((128, 128), dtype=BF)
    for c_ in range(8):
        for h_ in range(16):
            if h_ // 2 == c_:
                rows = slice((h_ % 2) * 64, (h_ % 2) * 64 + 64)
                bo16[rows, c_ * 16 + h_] = 1
    w["c_bo16all"] = bo16
    e2_ = np.zeros((2, 128), dtype=BF)
    e2_[0, :64] = 1
    e2_[1, 64:] = 1
    w["c_e2"] = e2_
    o64 = np.zeros((65, 128), dtype=BF)
    o64[64, :] = 1
    w["c_ones64"] = o64
    return w



class Ctr:
    def __init__(self, h):
        self.h = h
        self.n = 0

    def inc(self, inst, k=1):
        inst.then_inc(self.h, k)
        self.n += k
        return self.n


class Bld:
    def __init__(self, arrays):
        nc = bass.Bass("TRN2", target_bir_lowering=False)
        self.nc = nc
        self.arrays = arrays
        self.hbm = {}
        self.used = []
        self.q = {
            "h": Ctr(nc.alloc_semaphore("sem_h")),
            "s": Ctr(nc.alloc_semaphore("sem_s")),
            "g": Ctr(nc.alloc_semaphore("sem_g")),
        }
        self.ws = [Ctr(nc.alloc_semaphore("sem_w0")),
                   Ctr(nc.alloc_semaphore("sem_w1"))]
        self.st = [Ctr(nc.alloc_semaphore("sem_t0")),
                   Ctr(nc.alloc_semaphore("sem_t1"))]
        self.hp = [Ctr(nc.alloc_semaphore("sem_p0")),
                   Ctr(nc.alloc_semaphore("sem_p1"))]
        self.c_mm = Ctr(nc.alloc_semaphore("sem_mm"))
        self.c_ev = Ctr(nc.alloc_semaphore("sem_ev"))
        self.c_dv = Ctr(nc.alloc_semaphore("sem_dv"))
        self.c_pl = Ctr(nc.alloc_semaphore("sem_pl"))
        self.PA = nc.alloc_psum_tensor("PSA", [128, 2048], F32)
        self.PB = nc.alloc_psum_tensor("PSB", [128, 2048], F32)
        self._sbn = 0

    def inhbm(self, name):
        if name not in self.hbm:
            a = self.arrays[name]
            dt = BF16 if a.dtype == BF else F32
            self.hbm[name] = self.nc.dram_tensor(name, list(a.shape), dt,
                                                 kind="ExternalInput").ap()
            self.used.append(name)
        return self.hbm[name]

    def sb(self, shape, dtype, off, name=None):
        self._sbn += 1
        nm = name or f"sb{self._sbn}"
        esz = 2 if dtype == BF16 else 4
        nbytes = int(np.prod(shape[1:])) * esz
        assert off + nbytes <= SBUF_END, f"{nm}: {off}+{nbytes} over SBUF end"
        return self.nc.alloc_sbuf_tensor_at(nm, list(shape), dtype, offset=off)

    def ld(self, eng, qn, dst, src, **kw):
        eng.dma_start(dst, src, **kw).then_inc(self.q[qn].h, 16)
        self.q[qn].n += 16
        return self.q[qn].n

    def wq(self, eng, qn, tgt=None):
        eng.wait_ge(self.q[qn].h, self.q[qn].n if tgt is None else tgt)



def emit_proj_fm(b, w_hbm, n_out, rhs_tiles, evac, wbuf, evac_eng="scalar",
                 col_off=0):
    nc = b.nc
    n_units = n_out // 256
    mm0, ev0 = b.c_mm.n, b.c_ev.n
    wt = []

    with nc.Block() as blk:
        @blk.gpsimd
        def _(g):
            for u in range(n_units):
                if u >= 2:
                    g.wait_ge(b.c_mm.h, mm0 + 2 * (u - 1))
                src = w_hbm[(col_off // 256) + u]
                ws = b.ws[u % 2]
                g.dma_start(wbuf[u % 2][:], src).then_inc(ws.h, 16)
                ws.n += 16
                wt.append(ws.n)

        @blk.tensor
        def _(t):
            for u in range(n_units):
                t.wait_ge(b.ws[u % 2].h, wt[u])
                for mo in range(2):
                    j = u * 2 + mo
                    tgt = b.PA if j % 2 == 0 else b.PB
                    if j >= 2:
                        t.wait_ge(b.c_ev.h, ev0 + j - 1)
                    last = None
                    for tb in range(NB):
                        for kc in range(KC):
                            last = t.matmul(
                                tgt[:, tb * 512:(tb + 1) * 512],
                                wbuf[u % 2][:, kc * 256 + mo * 128: kc * 256 + (mo + 1) * 128],
                                rhs_tiles[kc][:, tb * 512:(tb + 1) * 512],
                                start=(kc == 0), stop=(kc == KC - 1))
                    b.c_mm.inc(last)

        def evs(e):
            for j in range(2 * n_units):
                tgt = b.PA if j % 2 == 0 else b.PB
                e.wait_ge(b.c_mm.h, mm0 + j + 1)
                last = evac(e, j, tgt)
                b.c_ev.inc(last)

        if evac_eng == "scalar":
            @blk.scalar
            def _(e):
                evs(e)
        else:
            @blk.vector
            def _(e):
                evs(e)


def emit_proj_tm(b, w_hbm, n_out, rhs_tiles, evac, wbuf, evac_eng="scalar",
                 col_off=0):
    nc = b.nc
    n_units = n_out // 256
    mm0, ev0 = b.c_mm.n, b.c_ev.n
    wt = []

    def pslice(i):
        off = (i % 4) * 512
        return b.PA[:, off:off + 256]

    with nc.Block() as blk:
        @blk.gpsimd
        def _(g):
            for u in range(n_units):
                if u >= 2:
                    g.wait_ge(b.c_mm.h, mm0 + 16 * (u - 1))
                src = w_hbm[(col_off // 256) + u]
                ws = b.ws[u % 2]
                g.dma_start(wbuf[u % 2][:], src).then_inc(ws.h, 16)
                ws.n += 16
                wt.append(ws.n)

        @blk.tensor
        def _(t):
            i = 0
            for u in range(n_units):
                t.wait_ge(b.ws[u % 2].h, wt[u])
                for tt in range(16):
                    if i >= 4:
                        t.wait_ge(b.c_ev.h, ev0 + i - 3)
                    last = None
                    for kc in range(KC):
                        last = t.matmul(
                            pslice(i),
                            rhs_tiles[kc][:, tt * 128:(tt + 1) * 128],
                            wbuf[u % 2][:, kc * 256:(kc + 1) * 256],
                            start=(kc == 0), stop=(kc == KC - 1))
                    b.c_mm.inc(last)
                    i += 1

        def evs(e):
            i = 0
            for u in range(n_units):
                for tt in range(16):
                    e.wait_ge(b.c_mm.h, mm0 + i + 1)
                    last = evac(e, tt, u, pslice(i))
                    b.c_ev.inc(last)
                    i += 1

        if evac_eng == "scalar":
            @blk.scalar
            def _(e):
                evs(e)
        else:
            @blk.vector
            def _(e):
                evs(e)



def emit_stats(b, tiles, lhsT_fn, m, dstA=False):
    nc = b.nc
    tgt = b.PA if dstA else b.PB
    with nc.Block() as blk:
        @blk.tensor
        def _(t):
            for nb in range(NB):
                for kc in range(KC):
                    t.matmul(tgt[0:m, nb * 512:(nb + 1) * 512],
                             lhsT_fn(kc), tiles[kc][:, nb * 512:(nb + 1) * 512],
                             start=(kc == 0), stop=(kc == KC - 1))


def emit_rowmath_plain(b, s0_ap, s1_ap, scratch_row, a_row, b_row, eps=EPS):
    nc = b.nc
    with nc.Block() as blk:
        @blk.scalar
        def _(e):
            e.activation(scratch_row[:], s0_ap, AF.Square, scale=1.0 / D)
    with nc.Block() as blk:
        @blk.vector
        def _(v):
            v.scalar_tensor_tensor(scratch_row[:], s1_ap, 1.0 / D, scratch_row[:],
                                   ALU.mult, ALU.subtract)
    with nc.Block() as blk:
        @blk.scalar
        def _(e):
            e.activation(scratch_row[:], scratch_row[:], AF.Sqrt, bias=eps)
    with nc.Block() as blk:
        @blk.vector
        def _(v):
            with b.nc.allow_low_precision("bf16 LN broadcast rows"):
                v.reciprocal(a_row[:], scratch_row[:])
            v.drain()
            v.scalar_tensor_tensor(b_row[:], s0_ap, -1.0 / D, a_row[:],
                                   ALU.mult, ALU.mult)


def emit_bcast2(b, a_row, b_row, env):
    nc = b.nc
    orow = env["ones_row"]
    with nc.Block() as blk:
        @blk.tensor
        def _(t):
            for nb in range(NB):
                t.matmul(b.PA[:, nb * 512:(nb + 1) * 512], orow[:],
                         a_row[:, nb * 512:(nb + 1) * 512], start=True, stop=True)
                t.matmul(b.PB[:, nb * 512:(nb + 1) * 512], orow[:],
                         b_row[:, nb * 512:(nb + 1) * 512], start=True, stop=True)


def emit_affine_apply(b, src_tiles, dst_pool, tmp_a, tmp_b, gb=None, add_into_x=None):
    nc = b.nc
    dv0, pl0 = b.c_dv.n, b.c_pl.n
    pool_mode = (add_into_x is not None) or (gb is not None)
    with nc.Block() as blk:
        @blk.vector
        def _(v):
            for ck in range(0, KC, 2):
                for c in (ck, ck + 1):
                    tmp = tmp_a if c % 2 == 0 else tmp_b
                    if pool_mode and c >= 2:
                        v.wait_ge(b.c_pl.h, pl0 + (c - 1))
                    v.tensor_tensor(tmp[:], src_tiles[c][:], b.PA[:], ALU.mult)
                v.drain()
                for c in (ck, ck + 1):
                    tmp = tmp_a if c % 2 == 0 else tmp_b
                    if pool_mode:
                        i = v.tensor_tensor(tmp[:], tmp[:], b.PB[:], ALU.add)
                    else:
                        i = v.tensor_tensor(dst_pool[c][:], tmp[:], b.PB[:],
                                            ALU.add)
                    b.c_dv.inc(i)
        if add_into_x is not None:
            @blk.gpsimd
            def _(g):
                for c in range(KC):
                    tmp = tmp_a if c % 2 == 0 else tmp_b
                    g.wait_ge(b.c_dv.h, dv0 + c + 1)
                    i = g.tensor_tensor(add_into_x[c][:], add_into_x[c][:],
                                        tmp[:], ALU.add)
                    b.c_pl.inc(i)
        elif gb is not None:
            gcols, bcols = gb
            @blk.gpsimd
            def _(g):
                for c in range(KC):
                    tmp = tmp_a if c % 2 == 0 else tmp_b
                    g.wait_ge(b.c_dv.h, dv0 + c + 1)
                    i = g.tensor_scalar(dst_pool[c][:], tmp[:],
                                        gcols[c], bcols[c], ALU.mult, ALU.add)
                    b.c_pl.inc(i)


def scratch_rows(b, tag, n, base=SCRW, dtype=BF16):
    ts = []
    sz = 2048 * (2 if dtype == BF16 else 4)
    for i in range(n):
        ts.append(b.sb([1, 2048], dtype, base + i * sz, f"row_{tag}_{i}_{b._sbn}"))
    return ts


def emit_ln_stats_rows(b, env, tiles, sq_pool, sq_off, tag):
    nc = b.nc
    ones_col = env["ones_col"]
    a_row, b_row = scratch_rows(b, tag, 2)
    with nc.Block() as blk:
        @blk.scalar
        def _(e):
            for c in range(KC):
                e.activation(sq_pool[c][:], tiles[c][:], AF.Square)
    emit_stats(b, tiles, lambda kc: ones_col[:], 1)
    emit_stats(b, sq_pool, lambda kc: ones_col[:], 1, dstA=True)
    sc_row = b.sb([1, 2048], F32, sq_off, f"scr_{tag}_{b._sbn}")
    emit_rowmath_plain(b, b.PB[0:1, 0:2048], b.PA[0:1, 0:2048], sc_row,
                       a_row, b_row)
    emit_bcast2(b, a_row, b_row, env)


def emit_boundary(b, env, z_pool, sq_pool, sq_off, y_specs, tag):
    nc = b.nc
    X = env["X"]
    tmp_a = b.sb([128, 2048], F32, sq_off + 8192, f"tba_{tag}")
    tmp_b = b.sb([128, 2048], F32, sq_off + 16384, f"tbb_{tag}")

    if z_pool is not None:
        emit_ln_stats_rows(b, env, z_pool, sq_pool, sq_off, tag + "z")
        emit_affine_apply(b, z_pool, None, tmp_a, tmp_b, add_into_x=X)

    if y_specs:
        has_center = any(sp[0] == "center" for sp in y_specs)
        bf_specs = [sp for sp in y_specs if sp[0] != "center"]
        with nc.Block() as blk:
            @blk.scalar
            def _(e):
                for c in range(KC):
                    e.activation(sq_pool[c][:], X[c][:], AF.Square)
        if bf_specs:
            xcast = bf_specs[0][1]
        elif z_pool is not None and z_pool is not sq_pool:
            xcast = z_pool
        else:
            xcast = env["pC"] if sq_pool is not env["pC"] else env["pD"]
        assert xcast is not sq_pool
        with nc.Block() as blk:
            @blk.vector
            def _(v):
                for c in range(KC):
                    v.tensor_copy(xcast[c][:], X[c][:])
        ones_col = env["ones_col"]
        a_row, b_row = scratch_rows(b, tag + "x", 2)
        emit_stats(b, xcast, lambda kc: ones_col[:], 1)
        emit_stats(b, sq_pool, lambda kc: ones_col[:], 1, dstA=True)
        sc_row = b.sb([1, 2048], F32, sq_off, f"scx_{tag}_{b._sbn}")
        emit_rowmath_plain(b, b.PB[0:1, 0:2048], b.PA[0:1, 0:2048], sc_row,
                           a_row, b_row)
        if has_center:
            with nc.Block() as blk:
                @blk.vector
                def _(v):
                    with nc.allow_low_precision("negmu row"):
                        v.tensor_scalar(b_row[:], b.PB[0:1, 0:2048], -1.0 / D,
                                        None, ALU.mult)
                @blk.sync
                def _(sy):
                    b.ld(sy, "h", env["l1f_inv_hbm"], a_row[:])
                    b.wq(sy, "h")
            with nc.Block() as blk:
                @blk.tensor
                def _(t):
                    for nb in range(NB):
                        t.matmul(b.PA[:, nb * 512:(nb + 1) * 512],
                                 env["ones_row"][:],
                                 b_row[:, nb * 512:(nb + 1) * 512],
                                 start=True, stop=True)
            with nc.Block() as blk:
                @blk.vector
                def _(v):
                    for c in range(KC):
                        v.tensor_tensor(env["yc"][c][:], X[c][:], b.PA[:], ALU.add)
        if bf_specs:
            emit_bcast2(b, a_row, b_row, env)
            for spec in bf_specs:
                if spec[0] == "plain":
                    emit_affine_apply(b, X, spec[1], tmp_a, tmp_b)
                else:
                    emit_affine_apply(b, X, spec[1], tmp_a, tmp_b,
                                      gb=(spec[2], spec[3]))



def tm_tile(pool, tt):
    return pool[tt // 2][:, (tt % 2) * 1024:(tt % 2) * 1024 + 1024]


def emit_l0a(b, env):
    nc = b.nc
    X, A, Bp, C, Dp = env["X"], env["pA"], env["pB"], env["pC"], env["pD"]
    ones_col, ones_row = env["ones_col"], env["ones_row"]
    wbuf = env["wbuf"]
    wq, wk, wv, wfc = (b.inhbm(n) for n in ("l0a_wq", "l0a_wk", "l0a_wv", "l0a_wfc"))

    emit_proj_fm(b, wq, 1024, A,
                 lambda e, j, ps: e.activation(Bp[j][:], ps[:], AF.Sigmoid), wbuf)
    emit_proj_tm(b, wk, 1024, A,
                 lambda e, tt, u, ps: e.activation(
                     tm_tile(C, tt)[:, u * 256:(u + 1) * 256], ps[:], AF.Sigmoid), wbuf)
    emit_proj_tm(b, wv, 1024, A,
                 lambda e, tt, u, ps: e.activation(
                     tm_tile(Dp, tt)[:, u * 256:(u + 1) * 256], ps[:], AF.Copy), wbuf)

    kve = b.sb([128, 8 * 65], BF16, SCRW + 8192, "l0a_kve")
    for pair in range(8):
        with nc.Block() as blk:
            @blk.tensor
            def _(t, pair=pair):
                for tt in range(16):
                    t.matmul(b.PA[0:128, 0:128],
                             tm_tile(C, tt)[:, pair * 128:(pair + 1) * 128],
                             tm_tile(Dp, tt)[:, pair * 128:(pair + 1) * 128],
                             start=(tt == 0), stop=(tt == 15))
                for tt in range(16):
                    t.matmul(b.PB[0:128, 0:1],
                             tm_tile(C, tt)[:, pair * 128:(pair + 1) * 128],
                             ones_col[:],
                             start=(tt == 0), stop=(tt == 15))
        with nc.Block() as blk:
            @blk.vector
            def _(v, pair=pair):
                v.tensor_copy(kve[0:64, pair * 65:pair * 65 + 64], b.PA[0:64, 0:64])
                v.drain()
                v.tensor_copy(kve[64:128, pair * 65:pair * 65 + 64],
                              b.PA[64:128, 64:128])
                v.drain()
                v.tensor_copy(kve[:, pair * 65 + 64:pair * 65 + 65], b.PB[0:128, 0:1])

    den_row = b.sb([1, 2048], F32, OFF_D, "l0a_den")
    rden_row = b.sb([1, 2048], BF16, OFF_D + 12288, "l0a_rden")
    rdenb = b.sb([64, 2048], BF16, OFF_D + 8192, "l0a_rdenb")
    for h in range(16):
        pair, half = h // 2, (h % 2) * 64
        with nc.Block() as blk:
            @blk.tensor
            def _(t, pair=pair, half=half):
                for tb in range(NB):
                    t.matmul(b.PA[0:65, tb * 512:(tb + 1) * 512],
                             kve[half:half + 64, pair * 65:(pair + 1) * 65],
                             Bp[pair][half:half + 64, tb * 512:(tb + 1) * 512],
                             start=True, stop=True)
        with nc.Block() as blk:
            @blk.scalar
            def _(e):
                e.activation(den_row[:], b.PA[64:65, 0:2048], AF.Copy, bias=1e-8)
        with nc.Block() as blk:
            @blk.vector
            def _(v):
                with nc.allow_low_precision("bf16 rden row"):
                    v.reciprocal(rden_row[:], den_row[:])
        with nc.Block() as blk:
            @blk.tensor
            def _(t):
                for nb in range(NB):
                    t.matmul(b.PB[0:64, nb * 512:(nb + 1) * 512], ones_row[:, 0:64],
                             rden_row[:, nb * 512:(nb + 1) * 512],
                             start=True, stop=True)
        with nc.Block() as blk:
            @blk.scalar
            def _(e):
                e.activation(rdenb[:], b.PB[0:64, 0:2048], AF.Copy)
        with nc.Block() as blk:
            @blk.vector
            def _(v, pair=pair, half=half):
                v.tensor_tensor(C[pair][half:half + 64, :], b.PA[0:64, 0:2048],
                                rdenb[:], ALU.mult)

    emit_proj_fm(b, wfc, 1024, C,
                 lambda e, j, ps: e.tensor_tensor(Dp[j][:], ps[:], A[j][:], ALU.add),
                 wbuf, evac_eng="vector")

    emit_ln_stats_rows(b, env, Dp, Bp, OFF_B, "l0a_in")
    tmp_a = b.sb([128, 2048], F32, OFF_B + 8192, "l0a_tma")
    tmp_b = b.sb([128, 2048], F32, OFF_B + 16384, "l0a_tmb")
    emit_affine_apply(b, Dp, A, tmp_a, tmp_b,
                      gb=(env["gb"]["l0a_g"], env["gb"]["l0a_b"]))
    return A, Bp, OFF_B





def emit_ffn(b, env, pfx):
    nc = b.nc
    A, Bp, C, Dp = env["pA"], env["pB"], env["pC"], env["pD"]
    wbuf = env["wbuf"]
    wr = b.inhbm(f"{pfx}_wr")
    wk = b.inhbm(f"{pfx}_wk")
    wv = b.inhbm(f"{pfx}_wv")
    mur = env["gb"][f"{pfx}_mur"]
    muk = env["gb"][f"{pfx}_muk"]

    with nc.Block() as blk:
        @blk.vector
        def _(v):
            for c in range(KC):
                v.memset(Bp[c][:, 0:1], 0.0)
            v.drain()
            for c in range(KC):
                v.tensor_copy(Bp[c][:, 1:2048], A[c][:, 0:2047])
    with nc.Block() as blk:
        @blk.vector
        def _(v):
            for ck in range(0, KC, 2):
                for c in (ck, ck + 1):
                    v.tensor_tensor(C[c][:], A[c][:], Bp[c][:], ALU.subtract)
                v.drain()
                for c in (ck, ck + 1):
                    v.scalar_tensor_tensor(Dp[c][:], C[c][:], muk[c], Bp[c][:],
                                           ALU.mult, ALU.add)
                for c in (ck, ck + 1):
                    v.scalar_tensor_tensor(C[c][:], C[c][:], mur[c], Bp[c][:],
                                           ALU.mult, ALU.add)
    emit_proj_fm(b, wr, 1024, C,
                 lambda e, j, ps: e.activation(Bp[j][:], ps[:], AF.Sigmoid), wbuf)

    def ev_relu_sq(e, j, ps):
        e.activation(C[j][:], ps[:], AF.Relu)
        e.drain()
        return e.activation(C[j][:], C[j][:], AF.Square)
    emit_proj_fm(b, wk, 1024, Dp, ev_relu_sq, wbuf)

    emit_proj_fm(b, wv, 1024, C,
                 lambda e, j, ps: e.tensor_tensor(Dp[j][:], ps[:], Bp[j][:], ALU.mult),
                 wbuf, evac_eng="vector")
    return Dp, Bp, OFF_B


def emit_l0f(b, env):
    return emit_ffn(b, env, "l0f")


def emit_l3f(b, env):
    return emit_ffn(b, env, "l3f")




def _emit_fp6_quant_chain(b, blk_ops, W, G, src_view, rscl_bc, scl_bc,
                          t0, t1, t2, t3, dst):
    v3 = lambda ap: ap.rearrange("p (g k) -> p g k", k=32)

    def ops(v):
        with b.nc.allow_low_precision("fp6 quant chain"):
            v.tensor_tensor(v3(t0), v3(src_view), rscl_bc, ALU.mult)
            v.drain()
            v.tensor_tensor(t1, t0, t0, ALU.mult)
            v.drain()
            v.tensor_single_scalar(t2, t1, 16.0, ALU.is_ge)
            v.drain()
            v.tensor_single_scalar(t1, t1, 4.0, ALU.is_ge)
            v.drain()
            v.tensor_scalar(t3, t2, -2.0, 8.0, ALU.mult, ALU.add)
            v.drain()
            v.scalar_tensor_tensor(t3, t1, -4.0, t3, ALU.mult, ALU.add)
            v.drain()
            v.tensor_scalar(t2, t2, 0.25, 0.125, ALU.mult, ALU.add)
            v.drain()
            v.scalar_tensor_tensor(t2, t1, 0.125, t2, ALU.mult, ALU.add)
            v.drain()
            v.tensor_tensor(t0, t0, t3, ALU.mult)
            v.drain()
            v.tensor_scalar(t0, t0, -1.0, 0.5, ALU.mult, ALU.add)
            v.drain()
            v.tensor_single_scalar(t3, t0, 12582912.0, ALU.add)
            v.drain()
            v.tensor_single_scalar(t3, t3, 12582912.0, ALU.subtract)
            v.drain()
            v.tensor_tensor(t1, t3, t0, ALU.is_gt)
            v.drain()
            v.tensor_tensor(t3, t3, t1, ALU.subtract)
            v.drain()
            v.scalar_tensor_tensor(t0, t3, -1.0, t2, ALU.mult, ALU.mult)
            v.drain()
            v.tensor_tensor(v3(t0), v3(t0), scl_bc, ALU.mult)
            if dst is not None:
                v.drain()
                v.tensor_copy(dst, t0)
    blk_ops.append(ops)


def emit_proj_fm_sb(b, lhsT_tiles, n_out, rhs_tiles, evac, evac_eng="scalar"):
    nc = b.nc
    mm0, ev0 = b.c_mm.n, b.c_ev.n
    with nc.Block() as blk:
        @blk.tensor
        def _(t):
            for j in range(n_out // 128):
                tgt = b.PA if j % 2 == 0 else b.PB
                if j >= 2:
                    t.wait_ge(b.c_ev.h, ev0 + j - 1)
                last = None
                for tb in range(NB):
                    for kc in range(KC):
                        last = t.matmul(
                            tgt[:, tb * 512:(tb + 1) * 512],
                            lhsT_tiles(kc, j),
                            rhs_tiles[kc][:, tb * 512:(tb + 1) * 512],
                            start=(kc == 0), stop=(kc == KC - 1))
                b.c_mm.inc(last)

        def evs(e):
            for j in range(n_out // 128):
                tgt = b.PA if j % 2 == 0 else b.PB
                e.wait_ge(b.c_mm.h, mm0 + j + 1)
                last = evac(e, j, tgt)
                b.c_ev.inc(last)
        if evac_eng == "scalar":
            @blk.scalar
            def _(e):
                evs(e)
        else:
            @blk.vector
            def _(e):
                evs(e)


def emit_l1f(b, env):
    nc = b.nc
    A, Bp, C, Dp = env["pA"], env["pB"], env["pC"], env["pD"]
    ident = env["ident"]
    yc = env["yc"]
    w_hbm = b.inhbm("l1f_w")
    bias = env["gb"]["l1f_bias"]

    tmpt = b.sb([128, 2048], F32, SCRW, "l1f_tmpt")
    inv_row = b.sb([1, 2048], BF16, SCRW, "l1f_invloc")
    amx = b.sb([128, 64], F32, SCRW + 8192, "l1f_amx")
    scl = b.sb([128, 64], F32, SCRW + 8448, "l1f_scl")
    rscl = b.sb([128, 64], F32, SCRW + 8704, "l1f_rscl")
    t0 = b.sb([128, 2048], F32, OFF_D, "l1f_t0")
    t1 = b.sb([128, 2048], F32, OFF_D + 8192, "l1f_t1")
    t2 = b.sb([128, 2048], F32, OFF_D + 16384, "l1f_t2")
    t3 = b.sb([128, 2048], F32, OFF_D + 24576, "l1f_t3")

    for c in range(KC):
        with nc.Block() as blk:
            @blk.vector
            def _(v, c=c):
                v.transpose(tmpt[:], yc[c][:])
        with nc.Block() as blk:
            @blk.vector
            def _(v):
                with nc.allow_low_precision("fp6 absmax"):
                    v.tensor_reduce(amx[:], tmpt[:].rearrange("p (g k) -> p g k", k=32),
                                    AX.X, ALU.max, apply_absolute_value=True)
                v.drain()
                v.tensor_scalar(scl[:], amx[:], 1.0 / 7.5, 1e-30, ALU.mult, ALU.max)
                v.drain()
                v.reciprocal(rscl[:], scl[:])
        blk_ops = []
        sbc = scl[:].unsqueeze(-1).broadcast_to((128, 64, 32))
        rbc = rscl[:].unsqueeze(-1).broadcast_to((128, 64, 32))
        _emit_fp6_quant_chain(b, blk_ops, 2048, 64, tmpt[:], rbc, sbc,
                              t0[:], t1[:], t2[:], t3[:], None)
        with nc.Block() as blk:
            @blk.vector
            def _(v, op=blk_ops[0]):
                op(v)
        with nc.Block() as blk:
            @blk.vector
            def _(v, c=c):
                v.transpose(tmpt[:], t0[:])
                v.drain()
                v.tensor_copy(C[c][:], tmpt[:])

    wf = [b.sb([128, 1024], F32, OFF_A + c * 4096, f"l1f_wf{c}") for c in range(KC)]
    with nc.Block() as blk:
        @blk.sync
        def _(sy):
            for c in range(KC):
                b.ld(sy, "h", wf[c][:], w_hbm[c * 128:(c + 1) * 128, :])
            b.wq(sy, "h")
    for c in range(KC):
        with nc.Block() as blk:
            @blk.vector
            def _(v, c=c):
                with nc.allow_low_precision("fp6 absmax"):
                    v.tensor_reduce(amx[:, 0:32],
                                    wf[c][:].rearrange("p (g k) -> p g k", k=32),
                                    AX.X, ALU.max, apply_absolute_value=True)
                v.drain()
                v.tensor_scalar(scl[:, 0:32], amx[:, 0:32], 1.0 / 7.5, 1e-30,
                                ALU.mult, ALU.max)
                v.drain()
                v.reciprocal(rscl[:, 0:32], scl[:, 0:32])
        blk_ops = []
        sbc = scl[:, 0:32].unsqueeze(-1).broadcast_to((128, 32, 32))
        rbc = rscl[:, 0:32].unsqueeze(-1).broadcast_to((128, 32, 32))
        _emit_fp6_quant_chain(b, blk_ops, 1024, 32, wf[c][:], rbc, sbc,
                              t0[:, 0:1024], t1[:, 0:1024], t2[:, 0:1024],
                              t3[:, 0:1024], None)
        with nc.Block() as blk:
            @blk.vector
            def _(v, op=blk_ops[0]):
                op(v)
        with nc.Block() as blk:
            @blk.vector
            def _(v, c=c):
                v.tensor_copy(Bp[c][:, 0:1024], t0[:, 0:1024])

    pabf = b.PA[:].bitcast(BF16)
    mm0, ev0 = b.c_mm.n, b.c_ev.n
    with nc.Block() as blk:
        @blk.tensor
        def _(t):
            i = 0
            for oc in range(KC):
                for dc in range(KC):
                    if i >= 4:
                        t.wait_ge(b.c_ev.h, ev0 + i - 3)
                    last = t.transpose(pabf[:, (i % 4) * 1024:(i % 4) * 1024 + 128],
                                       Bp[oc][:, dc * 128:(dc + 1) * 128], ident[:])
                    b.c_mm.inc(last)
                    i += 1
        @blk.vector
        def _(v):
            i = 0
            for oc in range(KC):
                for dc in range(KC):
                    v.wait_ge(b.c_mm.h, mm0 + i + 1)
                    last = v.tensor_copy(Bp[dc][:, 1024 + oc * 128:1024 + (oc + 1) * 128],
                                         pabf[:, (i % 4) * 1024:(i % 4) * 1024 + 128])
                    b.c_ev.inc(last)
                    i += 1

    invb = b.sb([128, 2048], BF16, OFF_D, "l1f_invb")
    with nc.Block() as blk:
        @blk.sync
        def _(sy):
            b.ld(sy, "h", inv_row[:], env["l1f_inv_hbm"])
            b.wq(sy, "h")
    with nc.Block() as blk:
        @blk.tensor
        def _(t):
            for nb in range(NB):
                t.matmul(b.PB[:, nb * 512:(nb + 1) * 512], env["ones_row"][:],
                         inv_row[:, nb * 512:(nb + 1) * 512], start=True, stop=True)
    with nc.Block() as blk:
        @blk.scalar
        def _(e):
            e.activation(invb[:], b.PB[:], AF.Copy)

    def ev(e, j, ps):
        e.tensor_tensor(A[j][:], ps[:], invb[:], ALU.mult)
        e.drain()
        return e.tensor_scalar(A[j][:], A[j][:], bias[j], None, ALU.add)
    emit_proj_fm_sb(b, lambda kc, j: Bp[kc][:, 1024 + j * 128:1024 + (j + 1) * 128],
                    1024, C, ev, evac_eng="vector")
    return A, C, OFF_C





def _attn_schedule(causal):
    evs = []
    pairs_of_bq = []
    gp = 0
    for bq in range(NB):
        ntk = 4 * bq + 4 if causal else 16
        np_ = ntk // 2
        sec = []
        for p in range(np_):
            g = gp + p
            sec.append(("sp", g, bq, p))
            if p >= 1:
                sec.append(("av", g - 1, bq, p - 1))
        sec.append(("av", gp + np_ - 1, bq, np_ - 1))
        evs.append(sec)
        pairs_of_bq.append((gp, np_))
        gp += np_
    return evs, pairs_of_bq, gp


def emit_attention(b, env, y_pool, q_pool, k_pool, ctx_pool, wqkv_hbm, scale,
                   causal, tag):
    nc = b.nc
    wbuf = env["wbuf"]
    vt = b.sb([128, 16640], BF16, OFF_D, f"vt_{tag}")
    tri = env["tri"]
    orow = env["ones_row"]
    ones64 = env["ones64"]

    emit_proj_fm(b, wqkv_hbm, 1024, y_pool,
                 lambda e, j, ps: e.activation(q_pool[j][:], ps[:], AF.Copy,
                                               scale=scale), wbuf, col_off=0)
    emit_proj_fm(b, wqkv_hbm, 1024, y_pool,
                 lambda e, j, ps: e.activation(k_pool[j][:], ps[:], AF.Copy),
                 wbuf, col_off=1024)
    with nc.Block() as blk:
        @blk.vector
        def _(v):
            ones_view = vt[:].rearrange("p (t h j) -> p t h j", h=16, j=65)
            v.memset(ones_view[:, :, :, 64:65], 1.0)

    def ev_v(e, tt, u, ps):
        dst = vt[:, tt * 1040 + 4 * u * 65: tt * 1040 + 4 * u * 65 + 260]
        dst = dst.rearrange("p (h j) -> p h j", j=65)[:, :, 0:64]
        return e.activation(dst, ps[:].rearrange("p (h j) -> p h j", j=64), AF.Copy)
    emit_proj_tm(b, wqkv_hbm, 1024, y_pool, ev_v, wbuf, col_off=2048)

    a2 = [b.sb([128, 1024], BF16, SCRW, f"a2a_{tag}"),
          b.sb([128, 1024], BF16, SCRW + 2048, f"a2b_{tag}")]
    ctxs = [b.sb([65, 2048], BF16, SCRW + 4096, f"cx0_{tag}"),
            b.sb([65, 2048], BF16, SCRW + 8192, f"cx1_{tag}")]

    evs, pairs_of_bq, npairs = _attn_schedule(causal)
    diag_pairs = []
    if causal:
        for bq in range(NB):
            gp, np_ = pairs_of_bq[bq]
            diag_pairs += [gp + np_ - 2, gp + np_ - 1]

    prev_tt2 = None
    prev_ctt = [None, None]
    for h in range(16):
        pair, half = h // 2, (h % 2) * 64
        mm0, ev0, dv0 = b.c_mm.n, b.c_ev.n, b.c_dv.n
        sp_ms, av_ms = {}, {}
        mmn = mm0
        for sec in evs:
            for e_ in sec:
                mmn += 1
                (sp_ms if e_[0] == "sp" else av_ms)[e_[1]] = mmn
        bc_ms = [mmn + 1, mmn + 2]
        ex_ms = {g: ev0 + g + 1 for g in range(npairs)}
        evac_ms = ev0 + npairs + 1
        mk_ms = {g: dv0 + i + 1 for i, g in enumerate(diag_pairs)}
        recip_ms = dv0 + len(diag_pairs) + 1
        tt_ms = [recip_ms + 1, recip_ms + 2]
        last_av = max(av_ms.values())

        with nc.Block() as blk:
            @blk.tensor
            def _(t, h=h, pair=pair, half=half, prev_tt2=prev_tt2):
                first_av = True
                for sec in evs:
                    for e_ in sec:
                        kind, g, bq, p = e_
                        if kind == "sp":
                            if g >= 2:
                                t.wait_ge(b.c_ev.h, ex_ms[g - 2])
                            i = None
                            for tki in (2 * p, 2 * p + 1):
                                i = t.matmul(
                                    b.PB[:, (tki % 4) * 512:(tki % 4) * 512 + 512],
                                    k_pool[pair][half:half + 64,
                                                 tki * 128:(tki + 1) * 128],
                                    q_pool[pair][half:half + 64,
                                                 bq * 512:(bq + 1) * 512],
                                    start=True, stop=True)
                            b.c_mm.inc(i)
                            assert b.c_mm.n == sp_ms[g]
                        else:
                            t.wait_ge(b.c_ev.h, ex_ms[g])
                            if g in mk_ms:
                                t.wait_ge(b.c_dv.h, mk_ms[g])
                            if first_av and prev_tt2 is not None:
                                t.wait_ge(b.c_dv.h, prev_tt2)
                                first_av = False
                            elif first_av:
                                first_av = False
                            gp0, np_ = pairs_of_bq[bq]
                            last_of_bq = (p == np_ - 1)
                            i = None
                            for n_, tki in enumerate((2 * p, 2 * p + 1)):
                                lo = 128 * (tki - 4 * bq) if (causal and tki >= 4 * bq) else 0
                                at = a2[g % 2][:, (tki % 2) * 512 + lo:
                                               (tki % 2) * 512 + 512]
                                i = t.matmul(
                                    b.PA[0:65, bq * 512 + lo: bq * 512 + 512],
                                    vt[:, tki * 1040 + h * 65:
                                       tki * 1040 + h * 65 + 65],
                                    at,
                                    start=(tki == 0),
                                    stop=(last_of_bq and n_ == 1))
                            b.c_mm.inc(i)
                            assert b.c_mm.n == av_ms[g]
                for hf in range(2):
                    t.wait_ge(b.c_dv.h, recip_ms if hf == 0 else tt_ms[0])
                    i = None
                    for nb in range(2):
                        cc = hf * 1024 + nb * 512
                        i = t.matmul(b.PA[0:64, nb * 512:(nb + 1) * 512],
                                     ones64[64:65, 0:64],
                                     ctxs[h % 2][64:65, cc:cc + 512],
                                     start=True, stop=True)
                    b.c_mm.inc(i)
                    assert b.c_mm.n == bc_ms[hf]

            @blk.scalar
            def _(e, h=h):
                for g in range(npairs):
                    e.wait_ge(b.c_mm.h, sp_ms[g])
                    if g >= 2:
                        e.wait_ge(b.c_mm.h, av_ms[g - 2])
                    slot = (2 * g) % 4
                    i = e.activation(a2[g % 2][:],
                                     b.PB[:, slot * 512: slot * 512 + 1024],
                                     AF.Exp, bias=-20.0)
                    b.c_ev.inc(i)
                    assert b.c_ev.n == ex_ms[g]
                e.wait_ge(b.c_mm.h, last_av)
                if prev_ctt[h % 2] is not None:
                    e.wait_ge(b.c_dv.h, prev_ctt[h % 2])
                i = e.activation(ctxs[h % 2][:], b.PA[0:65, 0:2048], AF.Copy)
                b.c_ev.inc(i)
                assert b.c_ev.n == evac_ms

            @blk.vector
            def _(v, h=h, pair=pair, half=half):
                for g in diag_pairs:
                    v.wait_ge(b.c_ev.h, ex_ms[g])
                    bq = next(bq_ for bq_ in range(NB)
                              if pairs_of_bq[bq_][0] <= g < pairs_of_bq[bq_][0]
                              + pairs_of_bq[bq_][1])
                    p = g - pairs_of_bq[bq][0]
                    i = None
                    for tki in (2 * p, 2 * p + 1):
                        off = 128 * (tki - 4 * bq)
                        if 0 <= off < 512:
                            ap_ = a2[g % 2][:, (tki % 2) * 512 + off:
                                            (tki % 2) * 512 + off + 128]
                            i = v.tensor_tensor(ap_, ap_, tri[:], ALU.mult)
                            v.drain()
                    b.c_dv.inc(i)
                    assert b.c_dv.n == mk_ms[g]
                v.wait_ge(b.c_ev.h, evac_ms)
                with nc.allow_low_precision("attn rden"):
                    i = v.reciprocal(ctxs[h % 2][64:65, :], ctxs[h % 2][64:65, :])
                b.c_dv.inc(i)
                assert b.c_dv.n == recip_ms
                for hf in range(2):
                    v.wait_ge(b.c_mm.h, bc_ms[hf])
                    i = v.tensor_tensor(
                        ctx_pool[pair][half:half + 64,
                                       hf * 1024:(hf + 1) * 1024],
                        ctxs[h % 2][0:64, hf * 1024:(hf + 1) * 1024],
                        b.PA[0:64, 0:1024], ALU.mult)
                    v.drain()
                    b.c_dv.inc(i)
                    assert b.c_dv.n == tt_ms[hf]
        prev_tt2 = tt_ms[1]
        prev_ctt[h % 2] = tt_ms[1]
    return vt


def emit_l1a(b, env):
    nc = b.nc
    A, Bp, C, Dp = env["pA"], env["pB"], env["pC"], env["pD"]
    wqkv = b.inhbm("l1a_wqkv")
    wo = b.inhbm("l1a_wo")
    emit_attention(b, env, A, Bp, C, A, wqkv, 0.125, True, "l1a")
    emit_proj_fm(b, wo, 1024, A,
                 lambda e, j, ps: e.activation(Bp[j][:], ps[:], AF.Copy),
                 env["wbuf"])
    return Bp, C, OFF_C




def emit_proj_fm_store(b, w_hbm, n_out, rhs_tiles, act, stage, dst_hbm, wbuf,
                       col_off=0, bias_cols=None):
    nc = b.nc
    n_units = n_out // 256
    mm0, ev0 = b.c_mm.n, b.c_ev.n
    wt = []
    st_entry = [b.st[0].n, b.st[1].n]
    stt = [st_entry[j % 2] + 16 * (j // 2 + 1) for j in range(2 * n_units)]
    with nc.Block() as blk:
        @blk.gpsimd
        def _(g):
            for u in range(n_units):
                if u >= 2:
                    g.wait_ge(b.c_mm.h, mm0 + 2 * (u - 1))
                src = w_hbm[(col_off // 256) + u]
                ws = b.ws[u % 2]
                g.dma_start(wbuf[u % 2][:], src).then_inc(ws.h, 16)
                ws.n += 16
                wt.append(ws.n)
        @blk.tensor
        def _(t):
            for u in range(n_units):
                t.wait_ge(b.ws[u % 2].h, wt[u])
                for mo in range(2):
                    j = u * 2 + mo
                    tgt = b.PA if j % 2 == 0 else b.PB
                    if j >= 2:
                        t.wait_ge(b.c_ev.h, ev0 + j - 1)
                    last = None
                    for tb in range(NB):
                        for kc in range(KC):
                            last = t.matmul(
                                tgt[:, tb * 512:(tb + 1) * 512],
                                wbuf[u % 2][:, kc * 256 + mo * 128: kc * 256 + (mo + 1) * 128],
                                rhs_tiles[kc][:, tb * 512:(tb + 1) * 512],
                                start=(kc == 0), stop=(kc == KC - 1))
                    b.c_mm.inc(last)
        @blk.scalar
        def _(e):
            for j in range(2 * n_units):
                tgt = b.PA if j % 2 == 0 else b.PB
                e.wait_ge(b.c_mm.h, mm0 + j + 1)
                e.wait_ge(b.st[j % 2].h,
                          stt[j - 2] if j >= 2 else st_entry[j % 2])
                if bias_cols is not None:
                    last = e.activation(stage[j % 2][:], tgt[:], act,
                                        bias=bias_cols[j])
                else:
                    last = e.activation(stage[j % 2][:], tgt[:], act)
                b.c_ev.inc(last)
        @blk.sync
        def _(sy):
            for j in range(2 * n_units):
                sy.wait_ge(b.c_ev.h, ev0 + j + 1)
                st = b.st[j % 2]
                sy.dma_start(dst_hbm[j * 128:(j + 1) * 128, :],
                             stage[j % 2][:]).then_inc(st.h, 16)
                st.n += 16
                assert st.n == stt[j]
            sy.wait_ge(b.st[0].h, b.st[0].n)
            sy.wait_ge(b.st[1].h, b.st[1].n)


def emit_l2f(b, env):
    nc = b.nc
    A, Bp, C, Dp = env["pA"], env["pB"], env["pC"], env["pD"]
    wbuf = env["wbuf"]
    X = env["X"]
    ones_col, ones_row, ident = env["ones_col"], env["ones_row"], env["ident"]
    wqkv = b.inhbm("l2f_wqkv")
    wproj = b.inhbm("l2f_wproj")

    emit_attention(b, env, A, Bp, C, A, wqkv, 0.125, False, "l2f")
    emit_proj_fm(b, wproj, 1024, A,
                 lambda e, j, ps: e.activation(Bp[j][:], ps[:], AF.Copy), wbuf)

    with nc.Block() as blk:
        @blk.scalar
        def _(e):
            for c in range(KC):
                e.activation(C[c][:], X[c][:], AF.Square)
    with nc.Block() as blk:
        @blk.vector
        def _(v):
            for c in range(KC):
                v.tensor_copy(A[c][:], X[c][:])
    a_row, b_row = scratch_rows(b, "l2fx", 2)
    emit_stats(b, A, lambda kc: ones_col[:], 1)
    emit_stats(b, C, lambda kc: ones_col[:], 1, dstA=True)
    sc_row = b.sb([1, 2048], F32, OFF_C, "l2f_scrow")
    emit_rowmath_plain(b, b.PB[0:1, 0:2048], b.PA[0:1, 0:2048], sc_row, a_row, b_row)
    emit_bcast2(b, a_row, b_row, env)
    tmp_a = b.sb([128, 2048], F32, OFF_C, "l2f_tma")
    tmp_b = b.sb([128, 2048], F32, OFF_C + 8192, "l2f_tmb")
    with nc.Block() as blk:
        @blk.vector
        def _(v):
            for c in range(KC):
                tmp = tmp_a if c % 2 == 0 else tmp_b
                v.tensor_tensor(tmp[:], X[c][:], b.PA[:], ALU.mult)
                v.drain()
                v.tensor_tensor(tmp[:], tmp[:], b.PB[:], ALU.add)
                v.drain()
                v.tensor_tensor(Bp[c][:], Bp[c][:], tmp[:], ALU.add)
                v.drain()

    emit_ln_stats_rows(b, env, Bp, C, OFF_C, "l2f_ln2")
    tmp_a2 = b.sb([128, 2048], F32, OFF_A, "l2f_tma2")
    tmp_b2 = b.sb([128, 2048], F32, OFF_A + 8192, "l2f_tmb2")
    emit_affine_apply(b, Bp, C, tmp_a2, tmp_b2,
                      gb=(env["gb"]["l2f_g2"], env["gb"]["l2f_b2"]))

    gw = b.sb([128, 64], BF16, OFF_D, "l2f_gw")
    gb_col = b.sb([8, 1], F32, OFF_D + 128, "l2f_gbcol")
    lg = b.sb([8, 2048], F32, OFF_D + 1024, "l2f_lg")
    ltm = b.sb([128, 128], F32, OFF_D + 9216, "l2f_ltm")
    mxt = b.sb([128, 16], F32, OFF_D + 9728, "l2f_mx")
    eqt = b.sb([128, 128], F32, OFF_D + 9856, "l2f_eq")
    idxc = b.sb([128, 16], F32, OFF_D + 10368, "l2f_idxc")
    idxrow = b.sb([1, 2048], BF16, SCRW + 8192, "l2f_idxrow")
    mbs_all = [b.sb([128, 2048], mybir.dt.uint8, OFF_A + 16384 + e * 2048,
                    f"l2f_mbs{e}") for e in range(8)]
    iota8 = env["iota8"]
    with nc.Block() as blk:
        @blk.gpsimd
        def _(g):
            b.ld(g, "g", gw[:].rearrange("p (c e) -> p c e", e=8),
                 b.inhbm("l2f_gatew").rearrange("(c p) e -> p c e", p=128),
                 allow_slow_non_contiguous=True)
            b.ld(g, "g", gb_col[:], b.inhbm("l2f_gateb_row").rearrange(
                "one e -> e one"), allow_slow_non_contiguous=True)
            b.wq(g, "g")
    with nc.Block() as blk:
        @blk.tensor
        def _(t):
            for nb in range(NB):
                for kc in range(KC):
                    t.matmul(b.PB[0:8, nb * 512:(nb + 1) * 512],
                             gw[:, kc * 8:(kc + 1) * 8],
                             C[kc][:, nb * 512:(nb + 1) * 512],
                             start=(kc == 0), stop=(kc == KC - 1))
    with nc.Block() as blk:
        @blk.scalar
        def _(e):
            e.activation(lg[:], b.PB[0:8, 0:2048], AF.Identity, bias=gb_col[:])
    mm0, ev0 = b.c_mm.n, b.c_ev.n
    with nc.Block() as blk:
        @blk.tensor
        def _(t):
            for tt in range(16):
                if tt >= 4:
                    t.wait_ge(b.c_ev.h, ev0 + tt - 3)
                i = t.transpose(b.PA[:, (tt % 4) * 512:(tt % 4) * 512 + 8],
                                lg[:, tt * 128:(tt + 1) * 128],
                                env["identf"][0:8, 0:8])
                b.c_mm.inc(i)
        @blk.vector
        def _(v):
            for tt in range(16):
                v.wait_ge(b.c_mm.h, mm0 + tt + 1)
                i = v.tensor_copy(ltm[:, tt * 8:(tt + 1) * 8],
                                  b.PA[:, (tt % 4) * 512:(tt % 4) * 512 + 8])
                b.c_ev.inc(i)
    with nc.Block() as blk:
        @blk.vector
        def _(v):
            v.tensor_reduce(mxt[:], ltm[:].rearrange("p (t e) -> p t e", e=8),
                            AX.X, ALU.max)
            v.drain()
            v.tensor_tensor(eqt[:].rearrange("p (t e) -> p t e", e=8),
                            ltm[:].rearrange("p (t e) -> p t e", e=8),
                            mxt[:].unsqueeze(-1).broadcast_to((128, 16, 8)),
                            ALU.is_ge)
            v.drain()
            v.scalar_tensor_tensor(eqt[:].rearrange("p (t e) -> p t e", e=8),
                                   eqt[:].rearrange("p (t e) -> p t e", e=8),
                                   -99.0,
                                   iota8[:].unsqueeze(1).broadcast_to((128, 16, 8)),
                                   ALU.mult, ALU.add)
            v.drain()
            v.tensor_single_scalar(eqt[:], eqt[:], 99.0, ALU.add)
            v.drain()
            v.tensor_reduce(idxc[:], eqt[:].rearrange("p (t e) -> p t e", e=8),
                            AX.X, ALU.min)
    idx_hbm = nc.dram_tensor("l2f_idxs", [128, 16], F32).ap()
    with nc.Block() as blk:
        @blk.sync
        def _(sy):
            b.ld(sy, "h", idx_hbm, idxc[:])
            b.wq(sy, "h")
    with nc.Block() as blk:
        @blk.gpsimd
        def _(g):
            b.ld(g, "g", idxrow[:].rearrange("one (tt p) -> one tt p", p=128),
                 idx_hbm.rearrange("p tt -> tt p").unsqueeze(0),
                 allow_slow_non_contiguous=True)
            b.wq(g, "g")

    mm0m, dv0m = b.c_mm.n, b.c_dv.n
    with nc.Block() as blk:
        @blk.tensor
        def _(t):
            for e in range(8):
                tgt = b.PA if e % 2 == 0 else b.PB
                if e >= 2:
                    t.wait_ge(b.c_dv.h, dv0m + e - 1)
                last = None
                for nb in range(NB):
                    last = t.matmul(tgt[:, nb * 512:(nb + 1) * 512], ones_row[:],
                                    idxrow[:, nb * 512:(nb + 1) * 512],
                                    start=True, stop=True)
                b.c_mm.inc(last)

        @blk.vector
        def _(v):
            for e in range(8):
                tgt = b.PA if e % 2 == 0 else b.PB
                v.wait_ge(b.c_mm.h, mm0m + e + 1)
                i = v.tensor_single_scalar(mbs_all[e][:], tgt[:], float(e),
                                           ALU.is_equal)
                b.c_dv.inc(i)

    stage = [b.sb([128, 2048], BF16, OFF_A, "l2f_st0"),
             b.sb([128, 2048], BF16, OFF_A + 12288, "l2f_st1")]
    h_hbm = []
    for e in range(8):
        h_hbm.append(nc.dram_tensor(f"l2f_h{e}", [4096, 2048], BF16).ap())
    for e in range(8):
        emit_proj_fm_store(b, b.inhbm(f"l2f_w1_{e}"), 4096, C,
                           AF.Gelu_apprx_tanh, stage, h_hbm[e], wbuf)

    w2c = [b.sb([128, 512], BF16, OFF_D + i * 1024, f"l2f_w2c{i}")
           for i in range(2)]
    ht2 = [b.sb([128, 1024], BF16, OFF_D + 2048 + i * 2048, f"l2f_ht{i}")
           for i in range(2)]
    groups = [(e, mh, tp) for e in range(8) for mh in range(2)
              for tp in range(2)]
    NG = len(groups)
    mm0p, ev0p = b.c_mm.n, b.c_ev.n
    st_fin = [b.st[0].n, b.st[1].n]
    h_t0, s_t0, g_t0 = b.q["h"].n, b.q["s"].n, b.q["g"].n
    mm_kc = lambda g, kc: mm0p + g * 32 + kc + 1
    ev_gx = lambda g, i: ev0p + g * 8 + i + 1
    with nc.Block() as blk:
        @blk.sync
        def _(sy):
            sy.wait_ge(b.st[0].h, st_fin[0])
            sy.wait_ge(b.st[1].h, st_fin[1])
            for g, (e, mh, tp) in enumerate(groups):
                for kc in range(0, 32, 2):
                    w = mm0p + g * 32 + kc - 1
                    if w > mm0p:
                        sy.wait_ge(b.c_mm.h, w)
                    b.ld(sy, "h", ht2[0][:],
                         h_hbm[e][kc * 128:(kc + 1) * 128,
                                  tp * 1024:(tp + 1) * 1024])

        @blk.scalar
        def _(e_):
            e_.wait_ge(b.st[0].h, st_fin[0])
            e_.wait_ge(b.st[1].h, st_fin[1])
            for g, (e, mh, tp) in enumerate(groups):
                for kc in range(1, 32, 2):
                    w = mm0p + g * 32 + kc - 1
                    if w > mm0p:
                        e_.wait_ge(b.c_mm.h, w)
                    b.ld(e_, "s", ht2[1][:],
                         h_hbm[e][kc * 128:(kc + 1) * 128,
                                  tp * 1024:(tp + 1) * 1024])

        @blk.gpsimd
        def _(g_):
            for g, (e, mh, tp) in enumerate(groups):
                w2hbm = b.inhbm(f"l2f_w2_{e}")
                for kc in range(32):
                    w = mm0p + g * 32 + kc - 1
                    if w > mm0p:
                        g_.wait_ge(b.c_mm.h, w)
                    b.ld(g_, "g", w2c[kc % 2][:],
                         w2hbm[mh][:, kc * 512:(kc + 1) * 512])

        @blk.tensor
        def _(t):
            t.wait_ge(b.c_ev.h, ev0p)
            for g, (e, mh, tp) in enumerate(groups):
                for kc in range(32):
                    qn = "h" if kc % 2 == 0 else "s"
                    t0 = h_t0 if kc % 2 == 0 else s_t0
                    nprev = g * 16 + kc // 2 + 1
                    t.wait_ge(b.q[qn].h, t0 + 16 * nprev)
                    t.wait_ge(b.q["g"].h, g_t0 + 16 * (g * 32 + kc + 1))
                    last = None
                    for m in range(4):
                        for tb01 in range(2):
                            if kc == 0 and g > 0:
                                t.wait_ge(b.c_ev.h,
                                          ev_gx(g - 1, m * 2 + tb01))
                            tgt = b.PA if tb01 == 0 else b.PB
                            last = t.matmul(
                                tgt[:, m * 512:(m + 1) * 512],
                                w2c[kc % 2][:, m * 128:(m + 1) * 128],
                                ht2[kc % 2][:, tb01 * 512:(tb01 + 1) * 512],
                                start=(kc == 0), stop=(kc == 31))
                    b.c_mm.inc(last)
                    assert b.c_mm.n == mm_kc(g, kc)

        @blk.vector
        def _(v):
            for g, (e, mh, tp) in enumerate(groups):
                v.wait_ge(b.c_mm.h, mm_kc(g, 31))
                for m in range(4):
                    for tb01 in range(2):
                        tgt = b.PA if tb01 == 0 else b.PB
                        oc = mh * 4 + m
                        col = tp * 1024 + tb01 * 512
                        i = v.copy_predicated(
                            C[oc][:, col:col + 512],
                            mbs_all[e][:, col:col + 512],
                            tgt[:, m * 512:(m + 1) * 512])
                        b.c_ev.inc(i)
                        assert b.c_ev.n == ev_gx(g, m * 2 + tb01)

    with nc.Block() as blk:
        @blk.vector
        def _(v):
            for c in range(KC):
                v.tensor_tensor(C[c][:], C[c][:], Bp[c][:], ALU.add)
    return C, Bp, OFF_B






def emit_l3a(b, env):
    nc = b.nc
    A, Bp, C, Dp = env["pA"], env["pB"], env["pC"], env["pD"]
    wbuf = env["wbuf"]
    orow = env["ones_row"]

    q_hbm = nc.dram_tensor("l3a_qs", [4096, 2048], BF16).ap()
    k_hbm = nc.dram_tensor("l3a_ks", [4096, 2048], BF16).ap()
    v_hbm = nc.dram_tensor("l3a_vs", [4096, 2048], BF16).ap()
    o_hbm = nc.dram_tensor("l3a_os", [4096, 2048], BF16).ap()
    wo_bf = nc.dram_tensor("l3a_wobf", [4096, 1024], BF16).ap()

    ph = b.sb([128, 32], F32, SCRW + 8192, "l3a_ph")
    dif = b.sb([128, 128], F32, SCRW + 8320, "l3a_dif")
    wcol = b.sb([128, 128], BF16, SCRW + 8832, "l3a_wcol")
    with nc.Block() as blk:
        @blk.sync
        def _(sy):
            b.ld(sy, "h", ph[:].rearrange("p (h c) -> p h c", c=8),
                 b.inhbm("l3a_phase").rearrange("h (c p) -> p h c", p=128),
                 allow_slow_non_contiguous=True)
            b.wq(sy, "h")
    with nc.Block() as blk:
        @blk.vector
        def _(v):
            for hh in range(4):
                for gg in range(4):
                    p = hh * 4 + gg
                    v.tensor_tensor(dif[:, p * 8:(p + 1) * 8],
                                    ph[:, hh * 8:(hh + 1) * 8],
                                    ph[:, gg * 8:(gg + 1) * 8], ALU.subtract)
    with nc.Block() as blk:
        @blk.scalar
        def _(e):
            e.activation(dif[:], dif[:], AF.Sin, bias=float(np.pi / 2))
    with nc.Block() as blk:
        @blk.vector
        def _(v):
            v.tensor_scalar_mul(wcol[:], dif[:], 1.0 / 32.0)

    stage = [b.sb([128, 2048], BF16, OFF_B, "l3a_st0"),
             b.sb([128, 2048], BF16, OFF_B + 4096, "l3a_st1")]
    for w_name, bias_name, dst in (("l3a_wq", "l3a_bq", q_hbm),
                                   ("l3a_wk", "l3a_bk", k_hbm),
                                   ("l3a_wv", "l3a_bv", v_hbm)):
        emit_proj_fm_store(b, b.inhbm(w_name), 4096, A, AF.Identity, stage,
                           dst, wbuf, bias_cols=env["gb"][bias_name])
    with nc.Block() as blk:
        @blk.gpsimd
        def _(g):
            b.ld(g, "g", wo_bf, b.inhbm("l3a_wo")[:])
            b.wq(g, "g")

    u8 = [b.sb([128, 2048], BF16, OFF_D, "l3a_u8a"),
          b.sb([128, 2048], BF16, OFF_D + 4096, "l3a_u8b")]
    expr_all = b.sb([1, 4096], BF16, OFF_D + 8192, "l3a_expr")
    den_row = b.sb([1, 1024], F32, OFF_D + 16384, "l3a_den")
    den2_row = b.sb([1, 1024], F32, OFF_D, "l3a_den2")
    rden_row = b.sb([1, 1024], BF16, OFF_D + 20480, "l3a_rden")
    tmpg = [b.sb([128, 2048], BF16, OFF_D + 22528, "l3a_tg0"),
            b.sb([128, 2048], BF16, OFF_D + 26624, "l3a_tg1")]
    qset = [Bp, A]

    mm_ent, ev_ent, dv_ent = b.c_mm.n, b.c_ev.n, b.c_dv.n
    st_fin3 = [b.st[0].n, b.st[1].n]
    h_base, s_base = b.q["h"].n, b.q["s"].n
    MM_T, EV_T, DV_T = 32, 16, 45
    mm_red = lambda T, p: mm_ent + T * MM_T + p + 1
    bc_rank = lambda p: (p % 4) * 4 + p // 4
    mm_bc = lambda T, p: mm_ent + T * MM_T + 16 + bc_rank(p) + 1
    ev_exp = lambda T, p: ev_ent + T * EV_T + p + 1
    dv_mul = lambda T, p: dv_ent + T * DV_T + p + 1
    dv_soft = lambda T: dv_ent + T * DV_T + 17

    def _cons_ord(p):
        h, g = p // 4, p % 4
        if g == 0:
            return h + 1
        return 4 + (g - 1) * 8 + (1, 2, 5, 6)[h]
    dv_cons = lambda T, p: dv_ent + T * DV_T + 17 + _cons_ord(p)
    dv_end = lambda T: dv_ent + (T + 1) * DV_T
    qk_tgt = lambda T: h_base + 16 * 8 * (T + 1)
    v_tgt = lambda T: s_base + 16 * (8 * T + 4)

    def s3(ap):
        return ap.rearrange("p (s t) -> p s t", t=256)

    def hb(hsrc, i, T):
        return hsrc[i * 1024:(i + 1) * 1024, T * 256:T * 256 + 256].rearrange(
            "(s p) t -> p s t", p=128)

    with nc.Block() as blk:
        @blk.sync
        def _(sy):
            sy.wait_ge(b.st[0].h, st_fin3[0])
            sy.wait_ge(b.st[1].h, st_fin3[1])
            sy.wait_ge(b.c_ev.h, ev_ent)
            sy.wait_ge(b.c_mm.h, mm_ent)
            for T in range(8):
                if T >= 2:
                    sy.wait_ge(b.c_dv.h, dv_mul(T - 2, 15))
                qs = qset[T % 2]
                for i in range(4):
                    b.ld(sy, "h", s3(qs[i][:]), hb(q_hbm, i, T))
                    b.ld(sy, "h", s3(qs[4 + i][:]), hb(k_hbm, i, T))

        @blk.scalar
        def _(e_):
            e_.wait_ge(b.st[0].h, st_fin3[0])
            e_.wait_ge(b.st[1].h, st_fin3[1])
            e_.wait_ge(b.c_dv.h, dv_ent)
            for i in range(4):
                b.ld(e_, "s", s3(C[i][:]), hb(v_hbm, i, 0))
            for T in range(8):
                for p in range(16):
                    e_.wait_ge(b.c_mm.h, mm_red(T, p))
                    if T > 0 and p == 0:
                        e_.wait_ge(b.c_mm.h, mm_bc(T - 1, 15))
                    i = e_.activation(expr_all[0:1, p * 256:(p + 1) * 256],
                                      b.PB[0:1, (p % 4) * 512:(p % 4) * 512 + 256],
                                      AF.Exp, bias=-20.0)
                    b.c_ev.inc(i)
                    assert b.c_ev.n == ev_exp(T, p)
                e_.wait_ge(b.c_dv.h, dv_end(T))
                for i in range(4):
                    b.ld(e_, "s", hb(o_hbm, i, T), s3(C[4 + i][:]))
                if T < 7:
                    for i in range(4):
                        b.ld(e_, "s", s3(C[i][:]), hb(v_hbm, i, T + 1))

        @blk.tensor
        def _(t):
            t.wait_ge(b.c_ev.h, ev_ent)
            for T in range(8):
                for p in range(16):
                    t.wait_ge(b.c_dv.h, dv_mul(T, p))
                    if p >= 4:
                        t.wait_ge(b.c_ev.h, ev_exp(T, p - 4))
                    elif T > 0:
                        t.wait_ge(b.c_ev.h, ev_exp(T - 1, 12 + p))
                    last = None
                    for c in range(KC):
                        last = t.matmul(
                            b.PB[0:1, (p % 4) * 512:(p % 4) * 512 + 256],
                            wcol[:, p * 8 + c: p * 8 + c + 1],
                            u8[p % 2][:, c * 256:(c + 1) * 256],
                            start=(c == 0), stop=(c == KC - 1))
                    b.c_mm.inc(last)
                    assert b.c_mm.n == mm_red(T, p)
                for g_ in range(4):
                    for hh in range(4):
                        p = hh * 4 + g_
                        if g_ == 0 and hh == 0:
                            t.wait_ge(b.c_dv.h, dv_soft(T))
                        if g_ >= 1:
                            t.wait_ge(b.c_dv.h, dv_cons(T, hh * 4 + g_ - 1))
                        elif T > 0:
                            t.wait_ge(b.c_dv.h, dv_cons(T - 1, hh * 4 + 3))
                        i = t.matmul(b.PA[0:128, hh * 512: hh * 512 + 256],
                                     orow[:],
                                     expr_all[0:1, p * 256:(p + 1) * 256],
                                     start=True, stop=True)
                        b.c_mm.inc(i)
                        assert b.c_mm.n == mm_bc(T, p)

        @blk.vector
        def _(v):
            ev3 = lambda g_: expr_all[:].rearrange(
                "one (h gt) -> one h gt", gt=1024)[:, :, g_ * 256:(g_ + 1) * 256]
            d3 = den_row[:].rearrange("one (h t) -> one h t", t=256)
            d23 = den2_row[:].rearrange("one (h t) -> one h t", t=256)
            r3 = rden_row[:].rearrange("one (h t) -> one h t", t=256)
            for T in range(8):
                qs = qset[T % 2]
                v.wait_ge(b.q["h"].h, qk_tgt(T))
                for p in range(16):
                    hh, gg = p // 4, p % 4
                    if p >= 2:
                        v.wait_ge(b.c_mm.h, mm_red(T, p - 2))
                    elif T > 0:
                        v.wait_ge(b.c_mm.h, mm_red(T - 1, 14 + p))
                    i = v.tensor_tensor(u8[p % 2][:], qs[hh][:],
                                        qs[4 + gg][:], ALU.mult)
                    b.c_dv.inc(i)
                    assert b.c_dv.n == dv_mul(T, p)
                v.wait_ge(b.c_ev.h, ev_exp(T, 15))
                v.tensor_tensor(d3, ev3(0), ev3(1), ALU.add)
                v.tensor_tensor(d23, ev3(2), ev3(3), ALU.add)
                v.drain()
                v.tensor_tensor(den_row[:], den_row[:], den2_row[:], ALU.add)
                v.drain()
                with nc.allow_low_precision("l3a rden"):
                    v.reciprocal(rden_row[:], den_row[:])
                v.drain()
                last = None
                for g_ in range(4):
                    last = v.tensor_tensor(ev3(g_), ev3(g_), r3, ALU.mult)
                b.c_dv.inc(last)
                assert b.c_dv.n == dv_soft(T)
                v.wait_ge(b.q["s"].h, v_tgt(T))
                for g_ in range(4):
                    if g_ == 0:
                        for hh in range(4):
                            p = hh * 4
                            v.wait_ge(b.c_mm.h, mm_bc(T, p))
                            bc = b.PA[:, hh * 512: hh * 512 + 256] \
                                .unsqueeze(1).broadcast_to((128, 8, 256))
                            i = v.tensor_tensor(s3(C[4 + hh][:]), s3(C[0][:]),
                                                bc, ALU.mult)
                            b.c_dv.inc(i)
                            assert b.c_dv.n == dv_cons(T, p)
                    else:
                        for half in range(2):
                            for hh2 in range(2):
                                hh = half * 2 + hh2
                                p = hh * 4 + g_
                                v.wait_ge(b.c_mm.h, mm_bc(T, p))
                                bc = b.PA[:, hh * 512: hh * 512 + 256] \
                                    .unsqueeze(1).broadcast_to((128, 8, 256))
                                i = v.tensor_tensor(s3(tmpg[hh2][:]),
                                                    s3(C[g_][:]), bc, ALU.mult)
                                b.c_dv.inc(i)
                                assert b.c_dv.n == dv_cons(T, p)
                            v.drain()
                            for hh2 in range(2):
                                hh = half * 2 + hh2
                                i = v.tensor_tensor(C[4 + hh][:], C[4 + hh][:],
                                                    tmpg[hh2][:], ALU.add)
                                b.c_dv.inc(i)
                assert b.c_dv.n == dv_end(T)
    assert b.c_mm.n == mm_ent + 8 * MM_T
    assert b.c_ev.n == ev_ent + 8 * EV_T
    assert b.c_dv.n == dv_ent + 8 * DV_T
    s_after_l3a = b.q["s"].n

    bo_cols = env["gb"]["l3a_bo"]
    ot = [b.sb([128, 2048], BF16, OFF_B + i * 4096, f"l3a_ot{i}") for i in range(2)]
    for hh in range(4):
        mm0w, ev0w = b.c_mm.n, b.c_ev.n
        hp0 = [b.hp[0].n, b.hp[1].n]
        wt0 = [b.ws[0].n, b.ws[1].n]
        with nc.Block() as blk:
            @blk.sync
            def _(sy, hh=hh):
                if hh == 0:
                    sy.wait_ge(b.q["s"].h, s_after_l3a)
                for kc in range(KC):
                    if kc >= 2:
                        sy.wait_ge(b.c_mm.h, mm0w + 4 * (kc - 1))
                    hp = b.hp[kc % 2]
                    sy.dma_start(ot[kc % 2][:],
                                 o_hbm[hh * 1024 + kc * 128: hh * 1024 + (kc + 1) * 128, :]
                                 ).then_inc(hp.h, 16)
                    hp.n += 16
            @blk.gpsimd
            def _(g, hh=hh):
                i = 0
                for kc in range(KC):
                    for r in range(4):
                        if i >= 2:
                            g.wait_ge(b.c_mm.h, mm0w + i - 1)
                        ws = b.ws[i % 2]
                        g.dma_start(
                            wbuf[i % 2][:, 0:1024],
                            wo_bf[r * 1024 + kc * 128: r * 1024 + (kc + 1) * 128, :]
                        ).then_inc(ws.h, 16)
                        ws.n += 16
                        i += 1
            @blk.tensor
            def _(t, hh=hh):
                i = 0
                for kc in range(KC):
                    t.wait_ge(b.hp[kc % 2].h, hp0[kc % 2] + 16 * (kc // 2 + 1))
                    for r in range(4):
                        t.wait_ge(b.ws[i % 2].h, wt0[i % 2] + 16 * (i // 2 + 1))
                        rhs = ot[kc % 2][:].rearrange("p (a r) -> p r a", r=4)[:, r, :]
                        last = None
                        for m in range(8):
                            tgt = b.PA if m < 4 else b.PB
                            last = t.matmul(
                                tgt[:, (m % 4) * 512:(m % 4) * 512 + 512],
                                wbuf[i % 2][:, m * 128:(m + 1) * 128],
                                rhs,
                                start=(kc == 0 and r == 0),
                                stop=(kc == KC - 1 and r == 3))
                        b.c_mm.inc(last)
                        i += 1
        with nc.Block() as blk:
            @blk.scalar
            def _(e, hh=hh):
                for m in range(8):
                    tgt = b.PA if m < 4 else b.PB
                    e.activation(A[m][:, hh * 512:(hh + 1) * 512],
                                 tgt[:, (m % 4) * 512:(m % 4) * 512 + 512],
                                 AF.Identity, bias=bo_cols[m])
    return A, Bp, OFF_B




L2A_UPTO = 4


def emit_l2a(b, env):
    nc = b.nc
    A, Bp, C, Dp = env["pA"], env["pB"], env["pC"], env["pD"]
    wbuf = env["wbuf"]
    e2, bo16 = env["e2"], env["bo16all"]
    wqkv = b.inhbm("l2a_wqkv")
    wout = b.inhbm("l2a_wout")

    cc_in = nc.dram_tensor("l2a_ccin", [2048, 2048], BF16).ap()
    cc_out = nc.dram_tensor("l2a_ccout", [16384, 2048], BF16,
                            addr_space="Shared").ap()

    stage = [b.sb([128, 2048], BF16, OFF_D, "l2a_st0"),
             b.sb([128, 2048], BF16, OFF_D + 4096, "l2a_st1")]
    emit_proj_fm_store(b, wqkv, 1024, Bp, AF.Copy, stage, cc_in[0:1024, :],
                       wbuf, col_off=1024)
    emit_proj_fm_store(b, wqkv, 1024, Bp, AF.Copy, stage, cc_in[1024:2048, :],
                       wbuf, col_off=2048)
    with nc.Block() as blk:
        @blk.gpsimd
        def _(g):
            g.wait_ge(b.st[0].h, b.st[0].n)
            g.wait_ge(b.st[1].h, b.st[1].n)
            g.collective_compute(
                "AllGather", ALU.bypass, replica_groups=[list(range(8))],
                ins=[cc_in.opt()], outs=[cc_out.opt()]).then_inc(b.c_pl.h, 1)
            b.c_pl.n += 1
    ag_done = b.c_pl.n
    emit_proj_fm(b, wqkv, 1024, Bp,
                 lambda e, j, ps: e.activation(C[j][:], ps[:], AF.Copy,
                                               scale=1.0 / 32.0), wbuf, col_off=0)

    kj = [b.sb([128, 2048], BF16, OFF_D, "l2a_kj0"),
          b.sb([128, 2048], BF16, OFF_D + 4096, "l2a_kj1")]
    u2 = [b.sb([128, 2048], BF16, OFF_D + 8192, "l2a_u0"),
          b.sb([128, 2048], BF16, OFF_D + 12288, "l2a_u1")]
    tmpd = b.sb([128, 2048], BF16, OFF_D + 16384, "l2a_tmp")
    rden16 = b.sb([16, 2048], BF16, OFF_D + 20480, "l2a_rden")
    den16 = b.sb([16, 2048], F32, OFF_D + 24576, "l2a_den")
    rdp = [b.sb([2, 2048], BF16, SCRW + 8192, "l2a_rdp0"),
           b.sb([2, 2048], BF16, OFF_D + 8192, "l2a_rdp1")]

    for j in range(8 if L2A_UPTO >= 2 else 0):
        hp0 = [b.hp[0].n, b.hp[1].n]
        mm0, ev0, dv0 = b.c_mm.n, b.c_ev.n, b.c_dv.n
        with nc.Block() as blk:
            @blk.sync
            def _(sy, j=j):
                sy.wait_ge(b.c_pl.h, ag_done)
                for c in range(KC):
                    if c >= 2:
                        sy.wait_ge(b.c_dv.h, dv0 + c - 1)
                    hp = b.hp[c % 2]
                    sy.dma_start(kj[c % 2][:],
                                 cc_out[j * 2048 + c * 128: j * 2048 + (c + 1) * 128, :]
                                 ).then_inc(hp.h, 16)
                    hp.n += 16
            @blk.vector
            def _(v, j=j):
                for c in range(KC):
                    v.wait_ge(b.hp[c % 2].h, hp0[c % 2] + 16 * (c // 2 + 1))
                    if c >= 2:
                        v.wait_ge(b.c_mm.h, mm0 + c - 1)
                    i = v.tensor_tensor(u2[c % 2][:], C[c][:], kj[c % 2][:],
                                        ALU.mult)
                    b.c_dv.inc(i)
            @blk.tensor
            def _(t, j=j):
                for c in range(KC):
                    t.wait_ge(b.c_dv.h, dv0 + c + 1)
                    last = None
                    for nb in range(NB):
                        last = t.matmul(
                            b.PB[0:16, nb * 512:(nb + 1) * 512],
                            bo16[:, c * 16:(c + 1) * 16],
                            u2[c % 2][:, nb * 512:(nb + 1) * 512],
                            start=(c == 0), stop=(c == KC - 1))
                    b.c_mm.inc(last)
            @blk.scalar
            def _(e, j=j):
                e.wait_ge(b.c_mm.h, mm0 + KC)
                e.activation(Bp[j][0:16, :], b.PB[0:16, 0:2048], AF.Exp,
                             bias=-20.0)

    if L2A_UPTO < 2:
        return A, Bp, OFF_B
    with nc.Block() as blk:
        @blk.vector
        def _(v):
            v.tensor_tensor(den16[:], Bp[0][0:16, :], Bp[1][0:16, :], ALU.add)
            for j in range(2, 8):
                v.drain()
                v.tensor_tensor(den16[:], den16[:], Bp[j][0:16, :], ALU.add)
            v.drain()
            with nc.allow_low_precision("l2a rden"):
                v.reciprocal(rden16[:], den16[:])

    for c in range(KC):
        if L2A_UPTO < 3:
            break
        hp0 = [b.hp[0].n, b.hp[1].n]
        st0l = [b.st[0].n, b.st[1].n]
        mm0, ev0, dv0, pl0 = b.c_mm.n, b.c_ev.n, b.c_dv.n, b.c_pl.n
        with nc.Block() as blk:
            @blk.sync
            def _(sy, c=c):
                for j in range(8):
                    if j >= 2:
                        sy.wait_ge(b.c_dv.h, dv0 + j - 1)
                    hp = b.hp[j % 2]
                    sy.dma_start(kj[j % 2][:],
                                 cc_out[j * 2048 + 1024 + c * 128:
                                        j * 2048 + 1024 + (c + 1) * 128, :]
                                 ).then_inc(hp.h, 16)
                    hp.n += 16
                    st = b.st[j % 2]
                    sy.dma_start(rdp[j % 2][:],
                                 Bp[j][2 * c:2 * c + 2, :]).then_inc(st.h, 16)
                    st.n += 16
            @blk.tensor
            def _(t, c=c):
                for j in range(8):
                    t.wait_ge(b.st[j % 2].h, st0l[j % 2] + 16 * (j // 2 + 1))
                    if j >= 2:
                        t.wait_ge(b.c_dv.h, dv0 + j - 1)
                    tgt = b.PA if j % 2 == 0 else b.PB
                    last = None
                    for nb in range(NB):
                        last = t.matmul(tgt[:, nb * 512:(nb + 1) * 512],
                                        e2[:], rdp[j % 2][:, nb * 512:(nb + 1) * 512],
                                        start=True, stop=True)
                    b.c_mm.inc(last)
            @blk.vector
            def _(v, c=c):
                for j in range(8):
                    v.wait_ge(b.c_mm.h, mm0 + j + 1)
                    v.wait_ge(b.hp[j % 2].h, hp0[j % 2] + 16 * (j // 2 + 1))
                    tgt = b.PA if j % 2 == 0 else b.PB
                    if j == 0:
                        i = v.tensor_tensor(C[c][:], kj[0][:], tgt[:], ALU.mult)
                    else:
                        v.tensor_tensor(tmpd[:], kj[j % 2][:], tgt[:], ALU.mult)
                        v.drain()
                        i = v.tensor_tensor(C[c][:], C[c][:], tmpd[:], ALU.add)
                    v.drain()
                    b.c_dv.inc(i)
        with nc.Block() as blk:
            @blk.sync
            def _(sy, c=c):
                b.ld(sy, "h", rdp[0][:], rden16[2 * c:2 * c + 2, :])
                b.wq(sy, "h")
        with nc.Block() as blk:
            @blk.tensor
            def _(t):
                for nb in range(NB):
                    t.matmul(b.PA[:, nb * 512:(nb + 1) * 512], e2[:],
                             rdp[0][:, nb * 512:(nb + 1) * 512],
                             start=True, stop=True)
        with nc.Block() as blk:
            @blk.vector
            def _(v, c=c):
                v.tensor_tensor(C[c][:], C[c][:], b.PA[:], ALU.mult)

    if L2A_UPTO < 4:
        return A, Bp, OFF_B
    emit_proj_fm(b, wout, 1024, C,
                 lambda e, j, ps: e.tensor_tensor(A[j][:], ps[:], A[j][:], ALU.add),
                 wbuf, evac_eng="vector")
    return A, Bp, OFF_B


NEED_Y = {
    "l0a": [("plain", "pA")],
    "l0f": [("plain", "pA")],
    "l1a": [("affine", "pA", "l1a_g", "l1a_b")],
    "l1f": [("center",)],
    "l2a": [("plain", "pA"), ("affine", "pB", "l2a_g", "l2a_b")],
    "l2f": [("affine", "pA", "l2f_g1", "l2f_b1")],
    "l3a": [("plain", "pA")],
    "l3f": [("plain", "pA")],
}

GB_ROWS = {"l0a_g": "l0a_g_row", "l0a_b": "l0a_b_row",
           "l1f_bias": "l1f_bias_row",
           "l3a_bq": "l3a_bq_row", "l3a_bk": "l3a_bk_row",
           "l3a_bv": "l3a_bv_row", "l3a_bo": "l3a_bo_row",
           "l0f_mur": "l0f_mur_row", "l0f_muk": "l0f_muk_row",
           "l3f_mur": "l3f_mur_row", "l3f_muk": "l3f_muk_row",
           "l1a_g": "l1a_g_row", "l1a_b": "l1a_b_row",
           "l2a_g": "l2a_g_row", "l2a_b": "l2a_b_row",
           "l2f_g1": "l2f_g1_row", "l2f_b1": "l2f_b1_row",
           "l2f_g2": "l2f_g2_row", "l2f_b2": "l2f_b2_row"}

EMITTERS = {}
EMITTERS["l0a"] = emit_l0a
EMITTERS["l0f"] = emit_l0f
EMITTERS["l3f"] = emit_l3f
EMITTERS["l1f"] = emit_l1f
EMITTERS["l1a"] = emit_l1a
EMITTERS["l2f"] = emit_l2f
EMITTERS["l3a"] = emit_l3a
EMITTERS["l2a"] = emit_l2a


def yspecs(env, st):
    out = []
    for sp in NEED_Y[st]:
        if sp[0] == "plain":
            out.append(("plain", env[sp[1]]))
        elif sp[0] == "center":
            out.append(("center",))
        else:
            out.append(("affine", env[sp[1]], env["gb"][sp[2]], env["gb"][sp[3]]))
    return out


def build_nc(arrays, steps):
    b = Bld(arrays)
    nc = b.nc
    x_in = nc.dram_tensor("x_in", [1024, 2048], F32, kind="ExternalInput").ap()
    b.used.append("x_in")
    x_out = nc.dram_tensor("x_out", [1024, 2048], F32, kind="ExternalOutput").ap()

    env = {}
    env["X"] = [b.sb([128, 2048], F32, OFF_X + c * 8192, f"X{c}") for c in range(KC)]
    for nm, off in (("pA", OFF_A), ("pB", OFF_B), ("pC", OFF_C), ("pD", OFF_D)):
        env[nm] = [b.sb([128, 2048], BF16, off + c * 4096, f"{nm}{c}") for c in range(KC)]
    env["wbuf"] = [b.sb([128, 2048], BF16, SCRW, "wb0"),
                   b.sb([128, 2048], BF16, SCRW + 4096, "wb1")]
    env["ones_col"] = b.sb([128, 1], BF16, OFF_CONST, "ones_col")
    env["biases"] = b.sb([128, 7], F32, OFF_CONST + 1856, "biases")
    for bi, bval in enumerate([1e-5, 1e-8, float(np.pi / 2), 1.0, 0.5, -30.0, -20.0]):
        nc.const_aps.register(F32, bval, env["biases"][:, bi:bi + 1])
    env["ones_row"] = b.sb([1, 128], BF16, OFF_CONST + 32, "ones_row")
    env["e2"] = b.sb([2, 128], BF16, OFF_CONST + 1056, "e2")
    env["ones64"] = b.sb([65, 128], BF16, OFF_CONST + 2816, "ones64")
    env["ident"] = b.sb([128, 128], BF16, OFF_CONST + 288, "ident")
    if "l1f" in steps:
        env["yc"] = [b.sb([128, 2048], F32, OFF_A + c * 8192, f"yc{c}")
                     for c in range(KC)]
        env["l1f_inv_hbm"] = nc.dram_tensor("l1f_invh", [1, 2048], BF16).ap()
    env["tri"] = b.sb([128, 128], BF16, OFF_CONST + 544, "tri")
    env["iota8"] = b.sb([128, 8], F32, OFF_CONST + 1312, "iota8")
    env["identf"] = b.sb([128, 128], F32, OFF_CONST + 1344, "identf")
    env["bo16all"] = b.sb([128, 128], BF16, OFF_CONST + 800, "bo16all")

    env["gb"] = {}
    gb_needed = []
    for st in steps:
        for sp in NEED_Y.get(st, []):
            if sp[0] == "affine":
                gb_needed += [sp[2], sp[3]]
    if "l0a" in steps:
        gb_needed += ["l0a_g", "l0a_b"]
    if "l0f" in steps:
        gb_needed += ["l0f_mur", "l0f_muk"]
    if "l3f" in steps:
        gb_needed += ["l3f_mur", "l3f_muk"]
    if "l1f" in steps:
        gb_needed += ["l1f_bias"]
    if "l3a" in steps:
        gb_needed += ["l3a_bq", "l3a_bk", "l3a_bv", "l3a_bo"]
    if "l2f" in steps:
        gb_needed += ["l2f_g2", "l2f_b2"]
    gb_off = OFF_CONST + 1888
    gb_tiles = {}
    for gname in dict.fromkeys(gb_needed):
        wcols = b_arr_len(arrays := None, GB_ROWS[gname]) if False else None
        import numpy as _np
        wid = int(b.arrays[GB_ROWS[gname]].shape[1]) // 128
        tile = b.sb([128, wid], F32, gb_off, f"gb_{gname}")
        gb_tiles[gname] = tile
        env["gb"][gname] = [tile[:, c:c + 1] for c in range(wid)]
        gb_off += 4 * wid
    assert gb_off <= OFF_CONST + 3072, gb_off

    with nc.Block() as blk:
        @blk.sync
        def _(s):
            for c in range(KC):
                b.ld(s, "h", env["X"][c][:], x_in[c * 128:(c + 1) * 128, :])
            b.ld(s, "h", env["ones_col"][:], b.inhbm("c_ones_col")[:])
            b.ld(s, "h", env["biases"][:], b.inhbm("c_biases")[:])
            b.ld(s, "h", env["ones_row"][:], b.inhbm("c_ones_row")[:])
            b.ld(s, "h", env["ident"][:], b.inhbm("c_ident")[:])
            b.ld(s, "h", env["tri"][:], b.inhbm("c_tri")[:])
            b.ld(s, "h", env["iota8"][:], b.inhbm("c_iota8")[:])
            b.ld(s, "h", env["identf"][:], b.inhbm("c_identf")[:])
            b.ld(s, "h", env["bo16all"][:], b.inhbm("c_bo16all")[:])
            b.ld(s, "h", env["e2"][:], b.inhbm("c_e2")[:])
            b.ld(s, "h", env["ones64"][:], b.inhbm("c_ones64")[:])
            for gname, tile in gb_tiles.items():
                src = b.inhbm(GB_ROWS[gname]).rearrange("one (c p) -> (one p) c", p=128)
                b.ld(s, "h", tile[:], src, allow_slow_non_contiguous=True)
            b.wq(s, "h")

    first_specs = yspecs(env, steps[0]) if steps else []
    emit_boundary(b, env, None, env["pD"], OFF_D, first_specs, "binit")

    for i, st in enumerate(steps):
        z_pool, sq_pool, sq_off = EMITTERS[st](b, env)
        nxt = yspecs(env, steps[i + 1]) if i + 1 < len(steps) else []
        emit_boundary(b, env, z_pool, sq_pool, sq_off, nxt, f"b_{st}")

    with nc.Block() as blk:
        @blk.sync
        def _(s):
            for c in range(KC):
                b.ld(s, "h", x_out[c * 128:(c + 1) * 128, :], env["X"][c][:])
            b.wq(s, "h")
    return b, env


def run_steps(arrays, x_fm_percore, steps, trace=False):
    b, env = build_nc(arrays, steps)
    in_maps = []
    for c in range(len(x_fm_percore)):
        m = {"x_in": x_fm_percore[c]}
        for nm in b.used:
            if nm != "x_in":
                m[nm] = arrays[nm]
        in_maps.append(m)
    res = run_bass_kernel_spmd(b.nc, in_maps, core_ids=list(range(len(in_maps))),
                               trace=trace)
    return [r["x_out"] for r in res.results], res


STEPS_ALL = ["l0a", "l0f", "l1a", "l1f", "l2a", "l2f", "l3a", "l3f"]


def kernel(**inputs):
    arrays = _prep(inputs)
    x = np.asarray(inputs["L0_attn_inputs"], dtype=np.float32)
    x_fm = [np.ascontiguousarray(x[c].T) for c in range(8)]
    outs, _ = run_steps(arrays, x_fm, STEPS_ALL)
    return np.stack([o.T for o in outs], axis=0)


